# version 24
# baseline (speedup 1.0000x reference)
import numpy as np
import ml_dtypes
from contextlib import ExitStack

import concourse.bass as bass
import concourse.mybir as mybir
from concourse.bass_utils import run_bass_kernel_spmd

F32 = mybir.dt.float32
BF16 = mybir.dt.bfloat16
AF = mybir.ActivationFunctionType
ALU = mybir.AluOpType
AX = mybir.AxisListType
BF = ml_dtypes.bfloat16

_DBG_PHASE = 4
_DBG_DUMP = False
_DBG_NSTEPS = None

N = 4096
D = 1024
H = 16
HD = 64
NBLK = 8
NT = 8
G = 2


class Plan:

    def __init__(self):
        self.steps = []
        self.labels = {}

    def step(self, eng, fn, waits=(), incs=(), label=None, serialize=False):
        self.steps.append([eng, fn, list(waits), list(incs), label, serialize])

    def resolve(self):
        cnt = {}
        for st in self.steps:
            _eng, _fn, waits, incs, label, serialize = st
            rw = []
            if serialize and incs:
                sem0 = incs[0][0]
                rw.append((sem0, cnt.get(id(sem0), 0)))
            for sem, lab in waits:
                if isinstance(lab, int):
                    rw.append((sem, lab))
                else:
                    s2, v = self.labels[lab]
                    assert s2 is sem, f"label {lab} bound to a different sem"
                    rw.append((sem, v))
            st[2] = rw
            for sem, n in incs:
                cnt[id(sem)] = cnt.get(id(sem), 0) + n
            if label is not None:
                assert len(incs) == 1, f"labeled step {label} must have exactly one inc"
                self.labels[label] = (incs[0][0], cnt[id(incs[0][0])])

    def emit(self, eng_name, eng):
        hi = {}
        for eng2, fn, waits, incs, _label, _ser in self.steps:
            if eng2 != eng_name:
                continue
            merged = {}
            for sem, val in waits:
                k = id(sem)
                if val > merged.get(k, (None, -1))[1]:
                    merged[k] = (sem, val)
            for k, (sem, val) in merged.items():
                if val > hi.get(k, -1):
                    eng.wait_ge(sem, val)
                    hi[k] = val
            ins = fn(eng)
            for sem, n in incs:
                assert ins is not None, "step with incs must return an instruction"
                ins.then_inc(sem, n)


def _consts():
    ident = np.eye(128, dtype=BF)
    identf = np.eye(128, dtype=np.float32)
    uu = np.arange(896)[None, :] - 384
    jj = np.arange(128)[:, None]
    master = np.where(jj > uu, 0.0, 3.0e38).astype(BF)
    ones3 = np.ones((3, H * 512), dtype=BF)
    ones2 = np.ones((2, 128), dtype=BF)
    p = np.arange(512)
    comb_d = (p[None, :] // 16 == np.arange(32)[:, None]).astype(np.float32)
    p2 = np.arange(128)
    comb_g = (p2[None, :] % 16 == np.arange(16)[:, None]).astype(np.float32)
    k = np.arange(128)
    Ltri = ((k[:, None] // 8 == k[None, :] // 8) &
            (k[:, None] % 8 < k[None, :] % 8)).astype(np.float32)
    return ident, identf, master, ones3, ones2, comb_d, comb_g, Ltri


def build_nc():
    nc = bass.Bass("TRN2", target_bir_lowering=False)

    x_d = nc.dram_tensor("x", [N, D], F32, kind="ExternalInput").ap()
    wq_d = nc.dram_tensor("wq", [D, D], F32, kind="ExternalInput").ap()
    wk_d = nc.dram_tensor("wk", [D, D], F32, kind="ExternalInput").ap()
    wv_d = nc.dram_tensor("wv", [D, D], F32, kind="ExternalInput").ap()
    wo_d = nc.dram_tensor("wo", [D, D], F32, kind="ExternalInput").ap()
    wf_d = nc.dram_tensor("wf", [D, H], F32, kind="ExternalInput").ap()
    g1r_d = nc.dram_tensor("g1r", [1, 32, 4, 32], F32, kind="ExternalInput").ap()
    g2r_d = nc.dram_tensor("g2r", [4, 16, 1, 16], F32, kind="ExternalInput").ap()
    g1i_d = nc.dram_tensor("g1i", [1, 32, 4, 32], F32, kind="ExternalInput").ap()
    g2i_d = nc.dram_tensor("g2i", [4, 16, 1, 16], F32, kind="ExternalInput").ap()
    br_d = nc.dram_tensor("br", [512], F32, kind="ExternalInput").ap()
    bi_d = nc.dram_tensor("bi", [512], F32, kind="ExternalInput").ap()
    y_d = nc.dram_tensor("y", [N, D], F32, kind="ExternalOutput").ap()

    knd = "ExternalOutput" if _DBG_DUMP else "Internal"
    if _DBG_DUMP:
        dbgp_d = nc.dram_tensor("dbgp", [2, 128, G * 512], BF16, kind="ExternalOutput").ap()
        dbgo_d = nc.dram_tensor("dbgo", [128, 4096], BF16, kind="ExternalOutput").ap()
        dbgw_d = nc.dram_tensor("dbgw", [2, 128, 2048], BF16, kind="ExternalOutput").ap()
    kbuf = nc.dram_tensor("kbuf", [H, HD, N], BF16, kind=knd).ap()
    qbuf = nc.dram_tensor("qbuf", [NT, HD, H, 512], BF16, kind=knd).ap()
    cqbuf = nc.dram_tensor("cqbuf", [H, 3, N], BF16, kind=knd).ap()
    ckbuf = nc.dram_tensor("ckbuf", [H, 3, N], BF16, kind=knd).ap()
    xbuf = nc.dram_tensor("xbuf", [NBLK, 128, 8 * 512], BF16, kind=knd).ap()
    fbuf = nc.dram_tensor("fbuf", [128, 512], F32, kind=knd).ap()

    ident_np, identf_np, master_np, ones3_np, ones2_np, comb_d_np, comb_g_np, Ltri_np = _consts()
    identb_d = nc.inline_tensor(ident_np, "identb").ap()
    identf_d = nc.inline_tensor(identf_np, "identf").ap()
    master_d = nc.inline_tensor(master_np, "masterc").ap()
    ones3_d = nc.inline_tensor(ones3_np, "ones3c").ap()
    ones2_d = nc.inline_tensor(ones2_np, "ones2c").ap()
    comb_d_d = nc.inline_tensor(comb_d_np, "combdc").ap()
    comb_g_d = nc.inline_tensor(comb_g_np, "combgc").ap()
    Ltri_d = nc.inline_tensor(Ltri_np, "ltric").ap()

    es = ExitStack()
    sb = lambda name, shape, dt: es.enter_context(nc.sbuf_tensor(name, shape, dt))
    psalloc = lambda name, shape, dt: es.enter_context(nc.psum_tensor(name, shape, dt))

    vst = sb("vst", [128, 32, H * 65], BF16)
    w1 = sb("w1", [128, 8 * 1024], BF16)
    w2 = sb("w2", [128, 8 * 1024], BF16)
    wfb = sb("wfb", [128, 8 * 16], BF16)
    kwin = [sb(f"kwin{i}", [128, N], BF16) for i in range(2)]
    qwin = sb("qwin", [128, H * 512], BF16)
    xres = sb("xres", [128, 4 * 1024], F32)
    xnt2 = sb("xnt2", [128, 8 * 512], BF16)
    otok = sb("otok", [128, 4 * 1024], BF16)
    ptile = [sb(f"ptile{i}", [128, G * 512], BF16) for i in range(3)]
    ybuf = [sb(f"ybuf{i}", [128, 1024], F32) for i in range(2)]
    xblk0 = sb("xblk0", [128, 4 * 1024], F32)
    xblk = [xblk0, xblk0]
    wtr = sb("wtr", [128, 4 * 512], BF16)
    wti = sb("wti", [128, 4 * 512], BF16)
    bias2r = sb("bias2r", [2, 512], BF16)
    bias2i = sb("bias2i", [2, 512], BF16)
    identb = sb("identb_sb", [128, 128], BF16)
    master = sb("master_sb", [128, 896], BF16)
    ones2 = sb("ones2_sb", [2, 128], BF16)
    epst = sb("epst", [128, 1], F32)
    onecol = sb("onecol", [128, 1], F32)
    sum_s = [sb(f"sum_s{i}", [128, 1], F32) for i in range(4)]
    ssq_s = [sb(f"ssq_s{i}", [128, 1], F32) for i in range(4)]
    m_s = [sb(f"m_s{i}", [128, 1], F32) for i in range(4)]
    vr_s = [sb(f"vr_s{i}", [128, 1], F32) for i in range(4)]
    t_s = [sb(f"t_s{i}", [128, 1], F32) for i in range(4)]
    var_s = [sb(f"var_s{i}", [128, 1], F32) for i in range(4)]
    rstd_s = [sb(f"rstd_s{i}", [128, 1], F32) for i in range(4)]
    nb_s = [sb(f"nb_s{i}", [128, 1], F32) for i in range(4)]
    lsb = sb("lsb", [128, 4], F32)
    recs = [sb(f"recs{i}", [128, 1], F32) for i in range(4)]
    offs = sb("offs", [128, 1], F32)
    totals = sb("totals", [128, 1], F32)
    _kw0_addr = nc.lookup_mloc(kwin[0]).addr
    _kw1_addr = nc.lookup_mloc(kwin[1]).addr
    xnt1s = [nc.alloc_sbuf_tensor_at("xnt1a", [128, 8 * 512], BF16, offset=_kw0_addr).ap(),
             nc.alloc_sbuf_tensor_at("xnt1b", [128, 8 * 512], BF16, offset=_kw1_addr).ap()]
    otst = ptile[2][:, 0:512]
    xnst = otok
    uae = xres[:, 0:2048]
    ube = xres[:, 2048:4096]
    prod = xnst[:].bitcast(F32)
    xnt2f = xnt2[:].bitcast(F32)
    ua_sb = xnt2f[:, 0:512]
    wtf32 = xnt2f[:, 512:1024]
    ub_sb = xnt2f[:, 1024:1088]
    at_sb = xnt2f[0:32, 1088:1216]
    bt_sb = xnt2f[0:16, 1216:1280]
    gsta = xnt2f[:, 1280:1408]
    combd = xnt2f[0:32, 1408:1920]
    combg = xnt2f[0:16, 1920:2048]
    ybf1 = ybuf[1].bitcast(F32) if False else ybuf[1]
    ltri = ybuf[1][:, 512:640]
    identf = ybuf[1][:, 640:768]
    kst4 = [ptile[i // 2][:, (i % 2) * 512:(i % 2) * 512 + 512]
            for i in range(6)]
    qwf = qwin[:].bitcast(F32)
    bspl = qwin[0:1, 0:1024]
    bf32 = qwf[0:1, 512:1024]
    btmp = qwf[0:1, 1024:1536]
    btmp2 = qwf[0:1, 1536:2048]
    xrf = xres
    f128 = xrf[:, 0:512]
    cst = [xrf[:, 512:1024], xrf[:, 1024:1536]]
    casc = xrf[:, 1536:2048]
    ctmp = xrf[:, 2048:2560]
    csp = xres[:].bitcast(BF16)[:, 5120:6656]
    csn = xres[:].bitcast(BF16)[:, 6656:8192]
    sq_s = ybuf[0][:, 0:1024]
    fst = ybuf[1][0:16, 0:512]

    psG = psalloc("psG", [128, 3072], F32)
    pacc = psalloc("pacc", [128, 264], F32)
    pex = psalloc("pex", [128, 512], F32)
    psA = psG[:, 0:1536]
    psB = psG[:, 1536:3072]
    pstA = psG[:, 2048:2304].bitcast(BF16)
    pstB = psG[:, 2560:2816].bitcast(BF16)
    pstC = psG[:, 1024:1280].bitcast(BF16)
    pstD = psG[:, 1536:1792].bitcast(BF16)
    psY = [psG[:, 0:512], psG[:, 512:1024]]
    psF = [psG[:, 1024:1536], psG[:, 1536:2048]]

    sW = es.enter_context(nc.semaphore("sW"))
    sX = es.enter_context(nc.semaphore("sX"))
    sK0 = es.enter_context(nc.semaphore("sK0"))
    sK1 = es.enter_context(nc.semaphore("sK1"))
    sQ = es.enter_context(nc.semaphore("sQ"))
    sXR = es.enter_context(nc.semaphore("sXR"))
    sXN = es.enter_context(nc.semaphore("sXN"))
    sSt = es.enter_context(nc.semaphore("sSt"))
    sY = es.enter_context(nc.semaphore("sYm"))
    sP = es.enter_context(nc.semaphore("sP"))
    sA = es.enter_context(nc.semaphore("sA"))
    sV = es.enter_context(nc.semaphore("sV"))

    P = Plan()

    def dma(out, in_, **kw):
        return lambda eng: eng.dma_start(out=out, in_=in_, **kw)

    P.step("dve", lambda v: v.memset(qwin[64:128, :], 0.0), incs=[(sV, 1)],
           label="qw_zero")
    P.step("sync", dma(identb[:], identb_d), incs=[(sW, 16)], serialize=True)
    P.step("sync", dma(identf[:], identf_d), incs=[(sW, 16)], serialize=True)
    P.step("sync", dma(master[:], master_d), incs=[(sW, 16)], serialize=True)
    P.step("sync", dma(qwin[67:70, :], ones3_d), waits=[(sV, "qw_zero")],
           incs=[(sW, 16)], serialize=True)
    P.step("sync", dma(ones2[:], ones2_d), incs=[(sW, 16)], serialize=True)
    P.step("sync", dma(combd[:], comb_d_d), incs=[(sW, 16)], serialize=True)
    P.step("sync", dma(combg[:], comb_g_d), incs=[(sW, 16)], serialize=True)
    P.step("sync", dma(ltri[:], Ltri_d), incs=[(sW, 16)], label="ld_small", serialize=True)
    wk_r = wk_d.rearrange("(c p) n -> p c n", p=128)
    wq_r = wq_d.rearrange("(c p) n -> p c n", p=128)
    wf_r = wf_d.rearrange("(c p) n -> p c n", p=128)
    P.step("gp", dma(w1.rearrange("p (c n) -> p c n", c=8), wk_r),
           incs=[(sSt, 16)], label="ld_wk", serialize=True)
    P.step("gp", dma(w2.rearrange("p (c n) -> p c n", c=8), wq_r),
           incs=[(sSt, 16)], label="ld_wq", serialize=True)
    P.step("gp", dma(wfb.rearrange("p (c n) -> p c n", c=8), wf_r),
           incs=[(sSt, 16)], label="ld_wf", serialize=True)
    P.step("sync", dma(gsta[:, 0:32], g1r_d.rearrange("a b c d -> (a b c) d")),
           incs=[(sW, 16)], label="ld_g1r", serialize=True)
    P.step("sync", dma(gsta[:, 32:64], g1i_d.rearrange("a b c d -> (a b c) d")),
           incs=[(sW, 16)], label="ld_g1i", serialize=True)
    P.step("sync", dma(gsta[0:64, 64:80], g2r_d.rearrange("c e a g -> (c e a) g")),
           incs=[(sW, 16)], label="ld_g2r", serialize=True)
    P.step("sync", dma(gsta[0:64, 80:96], g2i_d.rearrange("c e a g -> (c e a) g")),
           incs=[(sW, 16)], label="ld_g2i", serialize=True)
    P.step("sync", dma(bf32[:], br_d.rearrange("(a n) -> a n", a=1)),
           incs=[(sW, 16)], label="ld_br", serialize=True)

    P.step("dve", lambda v: v.memset(epst[:], 1e-5), waits=[(sW, "ld_small")],
           incs=[(sV, 1)], label="eps_set")
    P.step("dve", lambda v: v.memset(onecol[:], 1.0))
    P.step("dve", lambda v: v.memset(vst[:], 1.0), incs=[(sV, 1)], label="vst_ones")
    P.step("dve", lambda v: v.tensor_scalar_mul(w2[:], w2[:], 0.125),
           waits=[(sSt, "ld_wq")], incs=[(sV, 1)], label="wq_scaled")

    def build_wt(tag, g1_cols, g2_cols, wt_tile, neg_tile):
        wtr1 = [(sW, "ld_g1" + tag), (sW, "ld_small")]
        if tag == "i":
            wtr1.append((sV, "ubcr"))
        P.step("pe", (lambda t, _a=g1_cols:
                      t.transpose(pex[0:32, 0:128], gsta[:, _a[0]:_a[1]], identf[:])),
               waits=wtr1, incs=[(sP, 1)], label=f"tr_g1{tag}")
        P.step("dve", lambda v: v.tensor_copy(at_sb[:], pex[0:32, 0:128]),
               waits=[(sP, f"tr_g1{tag}")], incs=[(sV, 1)], label=f"at{tag}")
        P.step("pe", (lambda t, _a=g2_cols:
                      t.transpose(pex[0:16, 128:192], gsta[0:64, _a[0]:_a[1]], identf[0:64, 0:64])),
               waits=[(sW, "ld_g2" + tag), (sV, f"at{tag}")],
               incs=[(sP, 1)], label=f"tr_g2{tag}")
        P.step("dve", lambda v: v.tensor_copy(bt_sb[:], pex[0:16, 128:192]),
               waits=[(sP, f"tr_g2{tag}")], incs=[(sV, 1)], label=f"bt{tag}")

        def mk_ua(t):
            ins = None
            for p4 in range(4):
                ins = t.matmul(pex[:, p4 * 128:p4 * 128 + 128],
                               combd[:, p4 * 128:p4 * 128 + 128], at_sb[:],
                               start=True, stop=True)
            return ins
        P.step("pe", mk_ua, waits=[(sV, f"at{tag}"), (sV, f"bt{tag}")],
               incs=[(sP, 1)], label=f"ua{tag}")
        P.step("dve", lambda v: v.tensor_copy(ua_sb[:], pex[:, 0:512]),
               waits=[(sP, f"ua{tag}")], incs=[(sV, 1)], label=f"uac{tag}")
        P.step("pe", lambda t: t.matmul(pex[:, 0:64], combg[:], bt_sb[:],
                                        start=True, stop=True),
               waits=[(sV, f"uac{tag}")], incs=[(sP, 1)], label=f"ub{tag}")
        P.step("dve", lambda v: v.tensor_copy(ub_sb[:], pex[:, 0:64]),
               waits=[(sP, f"ub{tag}")], incs=[(sV, 1)], label=f"ubc{tag}")

        def mk_wt_chunk(v, p4, _wt=wt_tile, _neg=neg_tile):
            ua_r = ua_sb[:, p4 * 128:p4 * 128 + 128].rearrange("p (b c) -> p b c", c=4)
            uae_r = uae.rearrange("p (b c e) -> p e b c", c=4, e=16)
            for e in range(16):
                v.tensor_copy(uae_r[:, e], ua_r)
            ub_r = ub_sb[:, 0:64].rearrange("p (c e) -> p c e", c=4)
            ube_r = ube.rearrange("p (b c e) -> p b c e", c=4, e=16)
            for b8 in range(32):
                v.tensor_copy(ube_r[:, b8], ub_r)
            ins = v.tensor_mul(prod, uae, ube)
            return ins
        for p4 in range(4):
            P.step("dve", (lambda v, _p4=p4: mk_wt_chunk(v, _p4)),
                   waits=[(sV, f"ubc{tag}")] if p4 == 0 else [(sV, f"wtred{tag}{p4-1}")],
                   incs=[(sV, 1)], label=f"wtp{tag}{p4}")
            def red(v, _p4=p4, _wt=wt_tile, _neg=neg_tile):
                pr2 = prod.rearrange("p (b c e) -> p b c e", c=4, e=16)
                wv_ = wtf32.rearrange("p (b e) -> p b e", e=16)
                v.tensor_tensor(wv_, pr2[:, :, 0], pr2[:, :, 1], ALU.add)
                v.tensor_tensor(wv_, wv_, pr2[:, :, 2], ALU.add)
                v.tensor_tensor(wv_, wv_, pr2[:, :, 3], ALU.add)
                return v.tensor_copy(_wt[:, _p4 * 512:_p4 * 512 + 512], wtf32[:])
            P.step("dve", red, waits=[(sV, f"wtp{tag}{p4}")],
                   incs=[(sV, 1)], label=f"wtred{tag}{p4}")
        P.step("dve", lambda v: v.tensor_copy(btmp[:], btmp[:]),
               waits=[(sV, f"wtred{tag}3")], incs=[(sV, 1)], label=f"wt{tag}")

    build_wt("r", (0, 32), (64, 80), wtr, None)
    build_wt("i", (32, 64), (80, 96), wti, None)
    if _DBG_DUMP:
        P.step("gp", dma(dbgw_d[0], wtr[:]), waits=[(sV, "wtr")], incs=[(sSt, 16)])
        P.step("gp", dma(dbgw_d[1], wti[:]), waits=[(sV, "wti")], incs=[(sSt, 16)])

    def bias_split(v):
        v.tensor_copy(bspl[0:1, 0:512], bf32[:])
        v.tensor_copy(btmp[:], bspl[0:1, 0:512])
        v.tensor_sub(btmp2[:], bf32[:], btmp[:])
        return v.tensor_copy(bspl[0:1, 512:1024], btmp2[:])
    P.step("dve", bias_split, waits=[(sW, "ld_br")], incs=[(sV, 1)], label="bsp_r0")
    P.step("gp", dma(bias2r[0:1, :], bspl[0:1, 0:512]), waits=[(sV, "bsp_r0")],
           incs=[(sSt, 16)], serialize=True)
    P.step("gp", dma(bias2r[1:2, :], bspl[0:1, 512:1024]), incs=[(sSt, 16)], label="bsp_r", serialize=True)
    P.step("dve", lambda v: v.tensor_copy(btmp2[:], btmp2[:]),
           waits=[(sSt, "bsp_r")], incs=[(sV, 1)], label="bsplit_r")
    P.step("sync", dma(bf32[:], bi_d.rearrange("(a n) -> a n", a=1)),
           waits=[(sV, "bsplit_r")], incs=[(sW, 16)], label="ld_bi", serialize=True)
    P.step("dve", bias_split, waits=[(sW, "ld_bi")], incs=[(sV, 1)], label="bsp_i0")
    P.step("gp", dma(bias2i[0:1, :], bspl[0:1, 0:512]), waits=[(sV, "bsp_i0")],
           incs=[(sSt, 16)], serialize=True)
    P.step("gp", dma(bias2i[1:2, :], bspl[0:1, 512:1024]), incs=[(sSt, 16)], label="bsp_i", serialize=True)

    P._ph0 = len(P.steps)
    xr = x_d.rearrange("(b s p) d -> b p s d", p=128, s=4)

    def emit_front(b):
        bb = b % 2
        w = [(sV, f"aff{b-1}")] if b >= 1 else []
        P.step("sync", dma(xblk[bb].rearrange("p (s d) -> p s d", s=4), xr[b]),
               waits=w, incs=[(sX, 16)], label=f"xld{b}")

        def ln_stats(v, _bb=bb):
            ins = None
            for s in range(4):
                xsub = xblk[_bb][:, s * 1024:(s + 1) * 1024]
                v.reduce_sum(sum_s[s][:], xsub, axis=AX.X)
                v.tensor_mul(sq_s[:], xsub, xsub)
                ins = v.reduce_sum(ssq_s[s][:], sq_s[:], axis=AX.X)
            return ins
        P.step("dve", ln_stats, waits=[(sX, 16 * (b + 1))], incs=[(sV, 1)],
               label=f"lnst{b}")

        def ln_ab(v):
            ins = None
            for s in range(4):
                v.tensor_scalar_mul(m_s[s][:], sum_s[s][:], 1.0 / 1024)
                ins = v.tensor_scalar_mul(vr_s[s][:], ssq_s[s][:], 1.0 / 1024)
            return ins
        P.step("dve", ln_ab, waits=[(sV, f"lnst{b}")], incs=[(sV, 1)], label=f"lnab{b}")

        def ln_c(v):
            ins = None
            for s in range(4):
                ins = v.tensor_mul(t_s[s][:], m_s[s][:], m_s[s][:])
            return ins
        P.step("dve", ln_c, waits=[(sV, f"lnab{b}")], incs=[(sV, 1)], label=f"lnc{b}")

        def ln_d(v):
            ins = None
            for s in range(4):
                ins = v.tensor_sub(var_s[s][:], vr_s[s][:], t_s[s][:])
            return ins
        P.step("dve", ln_d, waits=[(sV, f"lnc{b}")], incs=[(sV, 1)], label=f"lnd{b}")

        def ln_ln(sc):
            ins = None
            for s in range(4):
                ins = sc.activation(rstd_s[s][:], var_s[s][:], AF.Ln, bias=epst[:, 0:1])
            return ins
        P.step("act", ln_ln, waits=[(sV, f"lnd{b}"), (sV, "eps_set")],
               incs=[(sA, 1)], label=f"lnln{b}")

        def ln_exp(sc):
            ins = None
            for s in range(4):
                ins = sc.activation(rstd_s[s][:], rstd_s[s][:], AF.Exp, scale=-0.5)
            return ins
        P.step("act", ln_exp, waits=[(sA, f"lnln{b}")], incs=[(sA, 1)], label=f"rstd{b}")

        def ln_nb(v):
            ins = None
            for s in range(4):
                ins = v.scalar_tensor_tensor(nb_s[s][:], m_s[s][:], -1.0, rstd_s[s][:],
                                             ALU.mult, ALU.mult)
            return ins
        P.step("dve", ln_nb, waits=[(sA, f"rstd{b}")], incs=[(sV, 1)], label=f"lnnb{b}")

        def ln_aff(v, _bb=bb):
            ins = None
            for s in range(4):
                ins = v.tensor_scalar(xnst[:, s * 1024:(s + 1) * 1024],
                                      xblk[_bb][:, s * 1024:(s + 1) * 1024],
                                      rstd_s[s][:, 0:1], nb_s[s][:, 0:1], ALU.mult, ALU.add)
            return ins
        waff = [(sV, f"lnnb{b}")]
        if b >= 1:
            waff.append((sP, f"trxend{b-1}"))
        P.step("dve", ln_aff, waits=waff, incs=[(sV, 1)], label=f"aff{b}")

        for s in range(4):
            for c in range(8):
                i = s * 8 + c
                pst = pstA if i % 2 == 0 else pstB
                wtt = [(sV, f"aff{b}")]
                if b > 0 or i >= 2:
                    wtt.append((sA, f"evx{b}_{i-2}" if i >= 2 else f"evx{b-1}_{30+i}"))
                P.step("pe", (lambda t, _s=s, _c=c, _p=pst:
                              t.transpose(_p[:, 0:128],
                                          xnst[:, _s * 1024 + _c * 128:_s * 1024 + _c * 128 + 128],
                                          identb[:])),
                       waits=wtt, incs=[(sP, 1)], label=f"trx{b}_{i}")
                P.step("act", (lambda sc, _s=s, _c=c, _p=pst, _x=xnt1s[b % 2]:
                               sc.copy(_x[:, _c * 512 + _s * 128:_c * 512 + _s * 128 + 128],
                                       _p[:, 0:128])),
                       waits=[(sP, f"trx{b}_{i}")], incs=[(sA, 1)], label=f"evx{b}_{i}")
        P.step("pe", lambda t: t.nop(), waits=[(sP, f"trx{b}_31")],
               incs=[(sP, 1)], label=f"trxend{b}")
        P.step("gp", dma(xbuf[b], xnt1s[b % 2][:]), waits=[(sA, f"evx{b}_31")],
               incs=[(sSt, 16)], label=f"xbst{b}", serialize=True)

        continue_marker = None
    def emit_mmblock(b):
        xnt1 = xnt1s[b % 2]
        for wi in range(2):
            wt_tile = w1 if wi == 0 else w2
            buf = kbuf if wi == 0 else qbuf
            pko = 0 if wi == 0 else 512
            for m in range(8):
                def mmk(t, _wt=wt_tile, _m=m, _o=pko):
                    ins = None
                    for c in range(8):
                        ins = t.matmul(psA[:, _o:_o + 512],
                                       _wt[:, c * 1024 + _m * 128:c * 1024 + _m * 128 + 128],
                                       xnt1[:, c * 512:c * 512 + 512],
                                       start=(c == 0), stop=(c == 7))
                    return ins
                wts = [(sA, f"evx{b}_31"), (sSt, "ld_wk" if wi == 0 else "ld_wq")]
                if wi == 1:
                    wts.append((sV, "wq_scaled"))
                if b == 0 and wi == 0 and m == 0:
                    wts.append((sV, "wtr"))
                    wts.append((sV, "wti"))
                if b > 0 or m > 0:
                    prev = f"evk{b}_{wi}_{m-1}" if m > 0 else f"evk{b-1}_{wi}_7"
                    wts.append((sA, prev))
                P.step("pe", mmk, waits=wts, incs=[(sP, 1)], label=f"mmk{b}_{wi}_{m}")
                jj = (b * 16 + wi * 8 + m)
                wev = [(sP, f"mmk{b}_{wi}_{m}")]
                if jj >= 6:
                    pj = jj - 6
                    wev.append((sSt, f"kqst{pj // 16}_{(pj % 16) // 8}_{pj % 8}"))
                P.step("act", (lambda sc, _j=jj, _o=pko:
                               sc.copy(kst4[_j % 6][:], psA[:, _o:_o + 512])),
                       waits=wev, incs=[(sA, 1)],
                       label=f"evk{b}_{wi}_{m}")
                if wi == 0:
                    P.step("gp", dma(buf[2 * m:2 * m + 2, :, b * 512:(b + 1) * 512]
                                     .rearrange("h r t -> (h r) t"), kst4[jj % 6][:]),
                           waits=[(sA, f"evk{b}_{wi}_{m}")], incs=[(sSt, 16)],
                           label=f"kqst{b}_{wi}_{m}", serialize=(m == 0))
                else:
                    P.step("gp", dma(qbuf[b, :, 2 * m, :], kst4[jj % 6][0:64, :]),
                           waits=[(sA, f"evk{b}_{wi}_{m}")], incs=[(sSt, 16)])
                    P.step("gp", dma(qbuf[b, :, 2 * m + 1, :], kst4[jj % 6][64:128, :]),
                           incs=[(sSt, 16)], label=f"kqst{b}_{wi}_{m}")

        def mmf(t):
            ins = None
            for c in range(8):
                ins = t.matmul(pex[0:16, 0:512], wfb[:, c * 16:c * 16 + 16],
                               xnt1[:, c * 512:c * 512 + 512], start=(c == 0), stop=(c == 7))
            return ins
        wtsf = [(sA, f"evx{b}_31"), (sSt, "ld_wf")]
        if b == 0:
            wtsf += [(sV, "wtr"), (sV, "wti")]
        else:
            wtsf.append((sA, f"evf{b-1}"))
        P.step("pe", mmf, waits=wtsf, incs=[(sP, 1)], label=f"mmf{b}")
        wevf = [(sP, f"mmf{b}")]
        if b > 0:
            wevf.append((sSt, f"fst{b-1}"))
        P.step("act", lambda sc: sc.copy(fst[:], pex[0:16, 0:512]),
               waits=wevf, incs=[(sA, 1)], label=f"evf{b}")
        P.step("gp", dma(fbuf.rearrange("(h s) t -> h s t", s=8)[:, b, :], fst[:]),
               waits=[(sA, f"evf{b}")], incs=[(sSt, 16)], label=f"fst{b}")

    for b in range(NBLK):
        emit_front(b)
        if b >= 1:
            emit_mmblock(b - 1)
    emit_mmblock(NBLK - 1)

    def kw_init(v, kb):
        v.memset(kwin[kb][64:128, :], 0.0)
        return v.memset(kwin[kb][64:67, :], 1.0)
    P.step("dve", lambda v: kw_init(v, 0),
           waits=[(sP, f"mmf{NBLK-2}"), (sSt, f"xbst{NBLK-2}")],
           incs=[(sV, 1)], label="kw0_ones")
    P.step("dve", lambda v: kw_init(v, 1),
           waits=[(sP, f"mmf{NBLK-1}"), (sSt, f"xbst{NBLK-1}")],
           incs=[(sV, 1)], label="kw1_ones")

    P._ph1 = len(P.steps)
    wv_r = wv_d.rearrange("(c p) n -> p c n", p=128)
    wo_r = wo_d.rearrange("(c p) n -> p c n", p=128)
    P.step("gp", dma(w1.rearrange("p (c n) -> p c n", c=8), wv_r),
           waits=[(sP, f"mmk{NBLK-1}_0_7")], incs=[(sSt, 16)], label="ld_wv", serialize=True)
    P.step("gp", dma(w2.rearrange("p (c n) -> p c n", c=8), wo_r),
           waits=[(sP, f"mmk{NBLK-1}_1_7")], incs=[(sSt, 16)], label="ld_wo", serialize=True)

    for b in range(NBLK):
        wts = [(sSt, f"xbst{b}")]
        if b > 0:
            wts.append((sP, f"mmv{b-1}_3_1"))
        P.step("sync", dma(xnt2[:], xbuf[b]), waits=wts, incs=[(sX, 16)], label=f"xrld{b}")
        for s in range(4):
            for n in range(2):
                def mmv(t, _s=s, _n=n):
                    ins = None
                    for c in range(8):
                        ins = t.matmul(psG[:, 1024 + _n * 512:1024 + _n * 512 + 512],
                                       xnt2[:, c * 512 + _s * 128:c * 512 + _s * 128 + 128],
                                       w1[:, c * 1024 + _n * 512:c * 1024 + _n * 512 + 512],
                                       start=(c == 0), stop=(c == 7))
                    return ins
                wts = [(sX, 16 * (NBLK + b + 1)), (sSt, "ld_wv")]
                if b == 0 and s == 0 and n == 0:
                    wts.append((sA, f"evx{NBLK-1}_31"))
                if b > 0 or s > 0 or n > 0:
                    pv = (f"evv{b}_{s}_{n-1}" if n > 0 else
                          (f"evv{b}_{s-1}_1" if s > 0 else f"evv{b-1}_3_1"))
                    wts.append((sA, pv))
                P.step("pe", mmv, waits=wts, incs=[(sP, 1)], label=f"mmv{b}_{s}_{n}")
                kap = 4 * b + s
                P.step("act", (lambda sc, _k=kap, _n=n:
                               sc.copy(vst.rearrange("p k (h c) -> p k h c", c=65)
                                       [:, _k, 8 * _n:8 * _n + 8, 0:64],
                                       (psG[:, 1024 + _n * 512:1024 + _n * 512 + 512]
                                        ).rearrange("p (h c) -> p h c", c=64))),
                       waits=[(sP, f"mmv{b}_{s}_{n}"), (sV, "vst_ones")],
                       incs=[(sA, 1)], label=f"evv{b}_{s}_{n}")

    P._ph2 = len(P.steps)
    P.step("sync", dma(f128[:], fbuf), waits=[(sSt, f"fst{NBLK-1}")],
           incs=[(sX, 16)], label="fld")
    P.step("act", lambda sc: sc.activation(casc[:], f128[:], AF.Abs),
           waits=[(sX, 16 * (2 * NBLK + 1))], incs=[(sA, 1)], label="c_abs")
    P.step("act", lambda sc: sc.activation(casc[:], casc[:], AF.Exp, scale=-1.0),
           waits=[(sA, "c_abs")], incs=[(sA, 1)], label="c_exp")
    P.step("act", lambda sc: sc.activation(casc[:], casc[:], AF.Ln, bias=onecol[:, 0:1]),
           waits=[(sA, "c_exp")], incs=[(sA, 1)], label="c_ln")
    P.step("dve", lambda v: v.tensor_scalar_min(ctmp[:], f128[:], 0.0),
           waits=[(sA, "c_ln")], incs=[(sV, 1)], label="c_min")
    P.step("dve", lambda v: v.tensor_sub(cst[0][:], ctmp[:], casc[:]),
           waits=[(sV, "c_min")], incs=[(sV, 1)], label="c_lf")
    src_i = 0
    for st_i, sh in enumerate([1, 2, 4, 8, 16, 32, 64, 128, 256]):
        dst = 1 - src_i
        def csum(v, _s=src_i, _d=dst, _sh=sh):
            v.tensor_copy(cst[_d][:, 0:_sh], cst[_s][:, 0:_sh])
            return v.tensor_add(cst[_d][:, _sh:512], cst[_s][:, _sh:512],
                                cst[_s][:, 0:512 - _sh])
        P.step("dve", csum, waits=[(sV, "c_lf" if st_i == 0 else f"cs{st_i-1}")],
               incs=[(sV, 1)], label=f"cs{st_i}")
        src_i = dst
    cfin = cst[src_i]
    P.step("dve", lambda v: v.tensor_copy(totals[:], cfin[:, 511:512]),
           waits=[(sV, "cs8")], incs=[(sV, 1)], label="c_tot")
    P.step("pe", lambda t: t.matmul(pex[:, 0:1], ltri[:], totals[:], start=True, stop=True),
           waits=[(sV, "c_tot"), (sA, f"evf{NBLK-1}")], incs=[(sP, 1)], label="c_off")
    P.step("dve", lambda v: v.tensor_copy(offs[:], pex[:, 0:1]),
           waits=[(sP, "c_off")], incs=[(sV, 1)], label="c_offc")
    P.step("dve", lambda v: v.tensor_scalar_add(ctmp[:], cfin[:], offs[:, 0:1]),
           waits=[(sV, "c_offc")], incs=[(sV, 1)], label="c_full")
    def csplit1(v):
        v.tensor_copy(csp[:, 0:512], ctmp[:])
        return v.tensor_copy(casc[:], csp[:, 0:512])
    P.step("dve", csplit1, waits=[(sV, "c_full")], incs=[(sV, 1)], label="csp1")
    def csplit2(v):
        v.tensor_sub(f128[:], ctmp[:], casc[:])
        return v.tensor_copy(csp[:, 512:1024], f128[:])
    P.step("dve", csplit2, waits=[(sV, "csp1")], incs=[(sV, 1)], label="csp2")
    def csplit3(v):
        v.tensor_copy(casc[:], csp[:, 512:1024])
        return v.tensor_sub(f128[:], f128[:], casc[:])
    P.step("dve", csplit3, waits=[(sV, "csp2")], incs=[(sV, 1)], label="csp3")
    P.step("dve", lambda v: v.tensor_copy(csp[:, 1024:1536], f128[:]),
           waits=[(sV, "csp3")], incs=[(sV, 1)], label="csp4")
    P.step("dve", lambda v: v.tensor_scalar_mul(csn[:], csp[:], -1.0),
           waits=[(sV, "csp4")], incs=[(sV, 1)], label="csp5")
    for hh in range(H):
        for sp in range(3):
            P.step("gp", dma(cqbuf[hh, sp, :].rearrange("(s t) -> s t", s=8),
                             csp[8 * hh:8 * hh + 8, sp * 512:sp * 512 + 512]),
                   waits=[(sV, "csp5")] if (hh == 0 and sp == 0) else [],
                   incs=[(sSt, 16)], label=f"cq{hh}_{sp}")
            P.step("gp", dma(ckbuf[hh, sp, :].rearrange("(s t) -> s t", s=8),
                             csn[8 * hh:8 * hh + 8, sp * 512:sp * 512 + 512]),
                   incs=[(sSt, 16)], label=f"ck{hh}_{sp}")

    P._ph3 = len(P.steps)
    xr2 = x_d.rearrange("(t s p) d -> t p s d", p=128, s=4)
    yr = y_d.rearrange("(t s p) d -> t s p d", p=128, s=4)

    gg_ctr = [0]
    nrm_ctr = [0]
    msk_ctr = [0]
    sg_last = {}
    pend_nrm = [None]
    pend_av = []
    pend_wo = []
    def flush_av(keep=0):
        while len(pend_av) > keep:
            _fn, _w, _lab = pend_av.pop(0)
            P.step("pe", _fn, waits=_w, incs=[(sP, 1)], label=_lab)

    set_off = [0]
    for t in range(NT):
        w = 512 * (t + 1)
        nch = 4 * (t + 1)
        tile_gg0 = [gg_ctr[0]]
        if t > 0:
            set_off[0] = (2 - gg_ctr[0]) % 3
        wq_ = [(sSt, f"ck{H-1}_2")] if t == 0 else [(sP, sg_last[(t - 1, H - 1)])]
        P.step("gp", dma(qwin[0:64, :], qbuf[t].rearrange("r h t -> r (h t)")),
               waits=wq_, incs=[(sQ, 16)], label=f"qld{t}")
        P.step("gp", dma(qwin[64:67, :].rearrange("i (h t) -> i h t", h=H),
                         cqbuf[:, :, 512 * t:512 * t + 512].rearrange("h i t -> i h t")),
               incs=[(sQ, 16)], label=f"qldc{t}")
        wx = [(sSt, f"ck{H-1}_2")] if t == 0 else [(sV, f"comb{t-1}_3_1")]
        P.step("gp", dma(xres.rearrange("p (s d) -> p s d", s=4), xr2[t]),
               waits=wx, incs=[(sXR, 16)], label=f"xres{t}")
        wxn = [(sP, f"mmv{NBLK-1}_3_1")] if t == 0 else [(sP, f"ff{t-1}_3")]
        P.step("gp", dma(xnt2[:], xbuf[t]), waits=wxn, incs=[(sXN, 16)], label=f"xnld{t}")

        for hh in range(H):
            kb = hh % 2
            wk_ = [(sV, "kw0_ones" if kb == 0 else "kw1_ones"), (sSt, f"ck{H-1}_2")]
            if t * H + hh >= 2:
                if hh >= 2:
                    wk_.append((sP, sg_last[(t, hh - 2)]))
                else:
                    wk_.append((sP, sg_last[(t - 1, H - 2 + hh)]))
            sKp = sK0 if kb == 0 else sK1
            P.step("sync", dma(kwin[kb][0:64, 0:w], kbuf[hh, :, 0:w]),
                   waits=wk_, incs=[(sKp, 16)])
            P.step("sync", dma(kwin[kb][67:70, 0:w], ckbuf[hh, :, 0:w]),
                   incs=[(sKp, 16)], label=f"kld{t}_{hh}")

            ngr = (nch + G - 1) // G
            for g in range(ngr):
                k0, k1 = G * g, min(G * g + G, nch)
                gg = gg_ctr[0]
                gset = (gg + set_off[0]) % 3
                pgrp = psG[:, 1024 * gset:1024 * gset + 1024]
                ptl = ptile[gset]

                def mms(ten, _k0=k0, _k1=k1, _p=pgrp, _hh=hh):
                    ins = None
                    for kk in range(_k0, _k1):
                        ins = ten.matmul(_p[:, 512 * (kk - _k0):512 * (kk - _k0) + 512],
                                         kwin[_hh % 2][:, 128 * kk:128 * kk + 128],
                                         qwin[:, 512 * _hh:512 * _hh + 512],
                                         start=True, stop=True)
                    return ins
                wts = []
                if g == 0:
                    wts += [(sK0 if hh % 2 == 0 else sK1, f"kld{t}_{hh}"),
                            (sQ, f"qldc{t}")]
                if gg >= 3:
                    wts.append((sA, f"exp{gg-3}"))
                if gg == 0:
                    wts.append((sA, f"evk{NBLK-1}_1_7"))
                    wts.append((sA, f"evf{NBLK-1}"))
                if gg == 1:
                    wts.append((sA, f"evv{NBLK-1}_3_1"))
                if gg == 2:
                    wts.append((sA, f"evx{NBLK-1}_31"))
                if t > 0 and (gg - tile_gg0[0]) < 3 and True:
                    wts.append((sV, f"comb{t-1}_3_1"))
                P.step("pe", mms, waits=wts, incs=[(sP, 1)], label=f"sg{gg}")
                flush_av(keep=2)
                if g == 0:
                    flush_av(keep=0)
                    if pend_nrm[0] is not None:
                        _enrm, _avlab = pend_nrm[0]
                        pend_nrm[0] = None
                        _enrm(_avlab=_avlab)
                if g == ngr - 1:
                    sg_last[(t, hh)] = f"sg{gg}"

                def expg(sc, _p=pgrp, _pt=ptl, _n=512 * (k1 - k0)):
                    return sc.activation(_pt[:, 0:_n], _p[:, 0:_n], AF.Exp)
                we = [(sP, f"sg{gg}")]
                if gg >= 3:
                    we.append((sP, f"avg{gg-3}"))
                P.step("act", expg, waits=we, incs=[(sA, 1)], label=f"exp{gg}")

                mlabs = []
                for kk in range(k0, k1):
                    m = kk - 4 * t
                    if m >= 0:
                        msk_ctr[0] += 1
                        def msk(v, _pt=ptl, _off=512 * (kk - k0), _m=m):
                            return v.tensor_tensor(
                                _pt[:, _off:_off + 512], _pt[:, _off:_off + 512],
                                master[:, 384 - 128 * _m:896 - 128 * _m], ALU.min)
                        P.step("dve", msk, waits=[(sA, f"exp{gg}")],
                               incs=[(sV, 1)], label=f"msk{msk_ctr[0]}")
                        mlabs.append(f"msk{msk_ctr[0]}")

                def av(ten, _k0=k0, _k1=k1, _pt=ptl, _hh=hh, _nch=nch):
                    pa = pacc if _hh % 2 == 0 else pex[:, 0:264]
                    ins = None
                    for kk in range(_k0, _k1):
                        for s4 in range(4):
                            ins = ten.matmul(
                                pa[:, 66 * s4:66 * s4 + 65],
                                _pt[:, 512 * (kk - _k0) + 128 * s4:
                                    512 * (kk - _k0) + 128 * s4 + 128],
                                vst[:, kk, 65 * _hh:65 * _hh + 65],
                                start=(kk == 0 and s4 == 0),
                                stop=(kk == _nch - 1 and s4 == 3),
                                skip_group_check=True)
                    return ins
                wav = [(sA, f"exp{gg}")] + [(sV, ml) for ml in mlabs]
                if g == 0:
                    if nrm_ctr[0] >= 1 + (1 if hh >= 2 else 0):
                        pass
                    if hh >= 2:
                        wav.append((sV, f"nrm{(t * H + hh) - 2 + 1}"))
                    elif t > 0:
                        if hh == 0:
                            wav.append((sV, f"nrm{(t - 1) * H + H - 2 + 1}"))
                        else:
                            wav.append((sV, f"comb{t-1}_3_1"))
                    elif t == 0 and hh == 1:
                        wav.append((sV, "c_offc"))
                pend_av.append((av, wav, f"avg{gg}"))
                gg_ctr[0] += 1

            def emit_nrm(_t=t, _hh=hh, _avlab=None):
                pa = pacc if _hh % 2 == 0 else pex[:, 0:264]
                P.step("dve", (lambda v, _pa=pa: v.tensor_copy(
                           lsb[:], _pa.rearrange("p (s c) -> p s c", c=66)[:, :, 64])),
                       waits=[(sP, _avlab)], incs=[(sV, 1)],
                       label=f"nrmA{_t}_{_hh}")
                def nrm_b(v):
                    ins = None
                    for s4 in range(4):
                        ins = v.reciprocal(recs[s4][:], lsb[:, s4:s4 + 1])
                    return ins
                P.step("dve", nrm_b, waits=[(sV, f"nrmA{_t}_{_hh}")], incs=[(sV, 1)],
                       label=f"nrmB{_t}_{_hh}")
                def nrm_c(v):
                    ins = None
                    for s4 in range(4):
                        ins = v.tensor_scalar_mul(
                            otok[:, 1024 * s4 + 64 * _hh:1024 * s4 + 64 * _hh + 64],
                            pa[:, 66 * s4:66 * s4 + 64], recs[s4][:, 0:1])
                    return ins
                wn = [(sV, f"nrmB{_t}_{_hh}")]
                if _t == 0 and _hh == 0:
                    wn.append((sP, f"trxend{NBLK-1}"))
                if _t > 0 and _hh == 0:
                    wn.append((sP, f"odone{_t-1}"))
                nrm_ctr[0] += 1
                P.step("dve", nrm_c, waits=wn, incs=[(sV, 1)], label=f"nrm{nrm_ctr[0]}")
            pend_nrm[0] = (emit_nrm, f"avg{gg_ctr[0]-1}")

        if _DBG_DUMP and t == 0:
            P.step("gp", dma(dbgo_d, otok[:]), waits=[(sV, f"nrm{nrm_ctr[0]}")],
                   incs=[(sSt, 16)])
        flush_av(keep=0)
        if pend_nrm[0] is not None:
            _enrm, _avlab = pend_nrm[0]
            pend_nrm[0] = None
            _enrm(_avlab=_avlab)
        nrm_all = f"nrm{nrm_ctr[0]}"
        for s4 in range(4):
            for c in range(8):
                pst = [pstA, pstB, pstC, pstD][c % 4]
                wtt = [(sV, nrm_all)]
                if s4 * 8 + c >= 4:
                    wtt.append((sV, f"oev{t}_{s4}_{c-4}" if c >= 4
                                else f"oev{t}_{s4-1}_{4+c}"))
                if c % 4 in (2, 3):
                    if s4 > 0:
                        wtt.append((sV, f"comb{t}_{s4-1}_1"))
                    elif t > 0:
                        wtt.append((sV, f"comb{t-1}_3_1"))
                P.step("pe", (lambda ten, _s=s4, _c=c, _p=pst:
                              ten.transpose(_p[:, 0:128],
                                            otok[:, 1024 * _s + 128 * _c:
                                                 1024 * _s + 128 * _c + 128],
                                            identb[:])),
                       waits=wtt, incs=[(sP, 1)], label=f"otr{t}_{s4}_{c}")
                woev = [(sP, f"otr{t}_{s4}_{c}")]
                if s4 * 8 + c >= 4:
                    woev.append((sP, f"wo{t}_{s4}_{c-4}" if c >= 4
                                 else f"wo{t}_{s4-1}_{4+c}"))
                P.step("dve", (lambda v, _c=c, _p=pst:
                               v.tensor_copy(otst[:, 128 * (_c % 4):128 * (_c % 4) + 128],
                                             _p[:, 0:128])),
                       waits=woev, incs=[(sV, 1)],
                       label=f"oev{t}_{s4}_{c}")
                def mwo(ten, _c=c):
                    ins = None
                    for n in range(2):
                        ins = ten.matmul(psY[n],
                                         otst[:, 128 * (_c % 4):128 * (_c % 4) + 128],
                                         w2[:, 1024 * _c + 512 * n:1024 * _c + 512 * n + 512],
                                         start=(_c == 0), stop=(_c == 7),
                                         skip_group_check=True)
                    return ins
                wwo = [(sV, f"oev{t}_{s4}_{c}"), (sSt, "ld_wo")]
                if c == 0:
                    if s4 > 0:
                        wwo.append((sV, f"comb{t}_{s4-1}_1"))
                    elif t > 0:
                        wwo.append((sV, f"comb{t-1}_3_1"))
                while len(pend_wo) >= 2:
                    _f, _w, _l = pend_wo.pop(0)
                    P.step("pe", _f, waits=_w, incs=[(sP, 1)], label=_l)
                pend_wo.append((mwo, wwo, f"wo{t}_{s4}_{c}"))
            while pend_wo:
                _f, _w, _l = pend_wo.pop(0)
                P.step("pe", _f, waits=_w, incs=[(sP, 1)], label=_l)
            if s4 == 3:
                P.step("pe", lambda ten: ten.nop(), waits=[(sP, f"otr{t}_3_7")],
                       incs=[(sP, 1)], label=f"odone{t}")

            def ffn(ten, _s=s4):
                ins = None
                for c in range(4):
                    ten.matmul(psF[0],
                               xnt2[:, c * 512 + _s * 128:c * 512 + _s * 128 + 128],
                               wtr[:, c * 512:c * 512 + 512],
                               start=(c == 0), stop=False, skip_group_check=True)
                ten.matmul(psF[0], ones2[:], bias2r[:], start=False, stop=True,
                           skip_group_check=True)
                for c in range(4):
                    ten.matmul(pex[:, 0:512],
                               xnt2[:, (c + 4) * 512 + _s * 128:(c + 4) * 512 + _s * 128 + 128],
                               wti[:, c * 512:c * 512 + 512],
                               start=(c == 0), stop=(c == 3), skip_group_check=True)
                for c in range(8):
                    wt_t = wti if c < 4 else wtr
                    ten.matmul(psF[1],
                               xnt2[:, c * 512 + _s * 128:c * 512 + _s * 128 + 128],
                               wt_t[:, (c % 4) * 512:(c % 4) * 512 + 512],
                               start=(c == 0), stop=False, skip_group_check=True)
                ins = ten.matmul(psF[1], ones2[:], bias2i[:], start=False, stop=True,
                                 skip_group_check=True)
                return ins
            wff = [(sV, "wtr"), (sV, "wti"), (sXN, f"xnld{t}"), (sSt, "bsp_i"),
                   (sV, f"oev{t}_{s4}_7"), (sV, nrm_all)]
            if s4 > 0:
                wff.append((sV, f"comb{t}_{s4-1}_1"))
            elif t > 0:
                wff.append((sV, f"comb{t-1}_3_1"))
            else:
                wff.append((sV, "c_offc"))
            P.step("pe", ffn, waits=wff, incs=[(sP, 1)], label=f"ff{t}_{s4}")

            for n in range(2):
                def comb(v, _s=s4, _n=n):
                    yb = ybuf[_s % 2]
                    v.scalar_tensor_tensor(yb[:, 512 * _n:512 * _n + 512],
                                           psY[_n], 1.0,
                                           xres[:, 1024 * _s + 512 * _n:
                                                1024 * _s + 512 * _n + 512],
                                           ALU.mult, ALU.add)
                    if _n == 0:
                        v.scalar_tensor_tensor(yb[:, 0:512], pex[:, 0:512], -1.0,
                                               yb[:, 0:512], ALU.mult, ALU.add)
                    return v.scalar_tensor_tensor(yb[:, 512 * _n:512 * _n + 512],
                                                  psF[_n], 1.0,
                                                  yb[:, 512 * _n:512 * _n + 512],
                                                  ALU.mult, ALU.add)
                wc = [(sP, f"wo{t}_{s4}_7"), (sP, f"ff{t}_{s4}"), (sXR, f"xres{t}")]
                idx = 4 * t + s4
                if idx >= 2:
                    wc.append((sY, 16 * (idx - 1)))
                P.step("dve", comb, waits=wc, incs=[(sV, 1)], label=f"comb{t}_{s4}_{n}")
            P.step("gp", dma(yr[t, s4], ybuf[s4 % 2][:]),
                   waits=[(sV, f"comb{t}_{s4}_1")], incs=[(sY, 16)],
                   label=f"yst{t}_{s4}")

    P.step("gp", lambda g_: g_.nop(), waits=[(sY, 16 * 4 * NT)])

    if _DBG_PHASE < 4:
        cut = [P._ph0, P._ph1, P._ph2, P._ph3][_DBG_PHASE]
        P.steps = P.steps[:cut]
    if _DBG_NSTEPS is not None:
        P.steps = P.steps[:_DBG_NSTEPS]
    P.resolve()

    with nc.Block() as block:
        @block.sync
        def _(sync):
            P.emit("sync", sync)

        @block.vector
        def _(vector):
            P.emit("dve", vector)

        @block.scalar
        def _(scalar):
            P.emit("act", scalar)

        @block.tensor
        def _(tensor):
            P.emit("pe", tensor)

        @block.gpsimd
        def _(gpsimd):
            P.emit("gp", gpsimd)

    es.close()
    return nc


def kernel(L0_attn_x, L0_attn_Wq, L0_attn_Wk, L0_attn_Wv, L0_attn_Wo, L0_attn_Wf,
           L0_ffn_G1_R, L0_ffn_G2_R, L0_ffn_G1_I, L0_ffn_G2_I,
           L0_ffn_bias_real, L0_ffn_bias_imag):
    x = np.ascontiguousarray(np.asarray(L0_attn_x, dtype=np.float32))
    com = {
        "wq": np.ascontiguousarray(np.asarray(L0_attn_Wq, dtype=np.float32)),
        "wk": np.ascontiguousarray(np.asarray(L0_attn_Wk, dtype=np.float32)),
        "wv": np.ascontiguousarray(np.asarray(L0_attn_Wv, dtype=np.float32)),
        "wo": np.ascontiguousarray(np.asarray(L0_attn_Wo, dtype=np.float32)),
        "wf": np.ascontiguousarray(np.asarray(L0_attn_Wf, dtype=np.float32)),
        "g1r": np.ascontiguousarray(np.asarray(L0_ffn_G1_R, dtype=np.float32)),
        "g2r": np.ascontiguousarray(np.asarray(L0_ffn_G2_R, dtype=np.float32)),
        "g1i": np.ascontiguousarray(np.asarray(L0_ffn_G1_I, dtype=np.float32)),
        "g2i": np.ascontiguousarray(np.asarray(L0_ffn_G2_I, dtype=np.float32)),
        "br": np.ascontiguousarray(np.asarray(L0_ffn_bias_real, dtype=np.float32)),
        "bi": np.ascontiguousarray(np.asarray(L0_ffn_bias_imag, dtype=np.float32)),
    }
    nc = build_nc()
    in_maps = [dict(com, x=np.ascontiguousarray(x[b])) for b in range(8)]
    res = run_bass_kernel_spmd(nc, in_maps, core_ids=list(range(8)))
    return np.stack([res.results[b]["y"] for b in range(8)], axis=0)


# version 25
# speedup vs baseline: 1.1775x; 1.1775x over previous
import numpy as np
import ml_dtypes
from contextlib import ExitStack

import concourse.bass as bass
import concourse.mybir as mybir
from concourse.bass_utils import run_bass_kernel_spmd

F32 = mybir.dt.float32
BF16 = mybir.dt.bfloat16
AF = mybir.ActivationFunctionType
ALU = mybir.AluOpType
AX = mybir.AxisListType
BF = ml_dtypes.bfloat16

_DBG_PHASE = 4
_DBG_DUMP = False
_DBG_NSTEPS = None

N = 4096
D = 1024
H = 16
HD = 64
NBLK = 8
NT = 8
G = 2


class Plan:

    def __init__(self):
        self.steps = []
        self.labels = {}

    def step(self, eng, fn, waits=(), incs=(), label=None, serialize=False):
        self.steps.append([eng, fn, list(waits), list(incs), label, serialize])

    def resolve(self):
        cnt = {}
        for st in self.steps:
            _eng, _fn, waits, incs, label, serialize = st
            rw = []
            if serialize and incs:
                sem0 = incs[0][0]
                rw.append((sem0, cnt.get(id(sem0), 0)))
            for sem, lab in waits:
                if isinstance(lab, int):
                    rw.append((sem, lab))
                else:
                    s2, v = self.labels[lab]
                    assert s2 is sem, f"label {lab} bound to a different sem"
                    rw.append((sem, v))
            st[2] = rw
            for sem, n in incs:
                cnt[id(sem)] = cnt.get(id(sem), 0) + n
            if label is not None:
                assert len(incs) == 1, f"labeled step {label} must have exactly one inc"
                self.labels[label] = (incs[0][0], cnt[id(incs[0][0])])

    def emit(self, eng_name, eng):
        hi = {}
        for eng2, fn, waits, incs, _label, _ser in self.steps:
            if eng2 != eng_name:
                continue
            merged = {}
            for sem, val in waits:
                k = id(sem)
                if val > merged.get(k, (None, -1))[1]:
                    merged[k] = (sem, val)
            for k, (sem, val) in merged.items():
                if val > hi.get(k, -1):
                    eng.wait_ge(sem, val)
                    hi[k] = val
            ins = fn(eng)
            for sem, n in incs:
                assert ins is not None, "step with incs must return an instruction"
                ins.then_inc(sem, n)


def _consts():
    ident = np.eye(128, dtype=BF)
    identf = np.eye(128, dtype=np.float32)
    uu = np.arange(896)[None, :] - 384
    jj = np.arange(128)[:, None]
    master = np.where(jj > uu, 0.0, 3.0e38).astype(BF)
    ones3 = np.ones((3, H * 512), dtype=BF)
    ones2 = np.ones((2, 128), dtype=BF)
    p = np.arange(512)
    comb_d = (p[None, :] // 16 == np.arange(32)[:, None]).astype(np.float32)
    p2 = np.arange(128)
    comb_g = (p2[None, :] % 16 == np.arange(16)[:, None]).astype(np.float32)
    k = np.arange(128)
    Ltri = ((k[:, None] // 8 == k[None, :] // 8) &
            (k[:, None] % 8 < k[None, :] % 8)).astype(np.float32)
    return ident, identf, master, ones3, ones2, comb_d, comb_g, Ltri


def build_nc():
    nc = bass.Bass("TRN2", target_bir_lowering=False)

    x_d = nc.dram_tensor("x", [N, D], F32, kind="ExternalInput").ap()
    wq_d = nc.dram_tensor("wq", [D, D], F32, kind="ExternalInput").ap()
    wk_d = nc.dram_tensor("wk", [D, D], F32, kind="ExternalInput").ap()
    wv_d = nc.dram_tensor("wv", [D, D], F32, kind="ExternalInput").ap()
    wo_d = nc.dram_tensor("wo", [D, D], F32, kind="ExternalInput").ap()
    wf_d = nc.dram_tensor("wf", [D, H], F32, kind="ExternalInput").ap()
    g1r_d = nc.dram_tensor("g1r", [1, 32, 4, 32], F32, kind="ExternalInput").ap()
    g2r_d = nc.dram_tensor("g2r", [4, 16, 1, 16], F32, kind="ExternalInput").ap()
    g1i_d = nc.dram_tensor("g1i", [1, 32, 4, 32], F32, kind="ExternalInput").ap()
    g2i_d = nc.dram_tensor("g2i", [4, 16, 1, 16], F32, kind="ExternalInput").ap()
    br_d = nc.dram_tensor("br", [512], F32, kind="ExternalInput").ap()
    bi_d = nc.dram_tensor("bi", [512], F32, kind="ExternalInput").ap()
    y_d = nc.dram_tensor("y", [N, D], F32, kind="ExternalOutput").ap()

    knd = "ExternalOutput" if _DBG_DUMP else "Internal"
    if _DBG_DUMP:
        dbgp_d = nc.dram_tensor("dbgp", [2, 128, G * 512], BF16, kind="ExternalOutput").ap()
        dbgo_d = nc.dram_tensor("dbgo", [128, 4096], BF16, kind="ExternalOutput").ap()
        dbgw_d = nc.dram_tensor("dbgw", [2, 128, 2048], BF16, kind="ExternalOutput").ap()
    kbuf = nc.dram_tensor("kbuf", [H, HD, N], BF16, kind=knd).ap()
    qbuf = nc.dram_tensor("qbuf", [NT, HD, H, 512], BF16, kind=knd).ap()
    cqbuf = nc.dram_tensor("cqbuf", [H, 3, N], BF16, kind=knd).ap()
    ckbuf = nc.dram_tensor("ckbuf", [H, 3, N], BF16, kind=knd).ap()
    xbuf = nc.dram_tensor("xbuf", [NBLK, 128, 8 * 512], BF16, kind=knd).ap()
    fbuf = nc.dram_tensor("fbuf", [128, 512], F32, kind=knd).ap()

    ident_np, identf_np, master_np, ones3_np, ones2_np, comb_d_np, comb_g_np, Ltri_np = _consts()
    identb_d = nc.inline_tensor(ident_np, "identb").ap()
    identf_d = nc.inline_tensor(identf_np, "identf").ap()
    master_d = nc.inline_tensor(master_np, "masterc").ap()
    ones3_d = nc.inline_tensor(ones3_np, "ones3c").ap()
    ones2_d = nc.inline_tensor(ones2_np, "ones2c").ap()
    comb_d_d = nc.inline_tensor(comb_d_np, "combdc").ap()
    comb_g_d = nc.inline_tensor(comb_g_np, "combgc").ap()
    Ltri_d = nc.inline_tensor(Ltri_np, "ltric").ap()

    es = ExitStack()
    sb = lambda name, shape, dt: es.enter_context(nc.sbuf_tensor(name, shape, dt))
    psalloc = lambda name, shape, dt: es.enter_context(nc.psum_tensor(name, shape, dt))

    vst = sb("vst", [128, 32, H * 65], BF16)
    w1 = sb("w1", [128, 8 * 1024], BF16)
    w2 = sb("w2", [128, 8 * 1024], BF16)
    wfb = sb("wfb", [128, 8 * 16], BF16)
    kwin = [sb(f"kwin{i}", [128, N], BF16) for i in range(2)]
    qwin = sb("qwin", [128, H * 512], BF16)
    xres = sb("xres", [128, 4 * 1024], F32)
    xnt2 = sb("xnt2", [128, 8 * 512], BF16)
    otok = sb("otok", [128, 4 * 1024], BF16)
    ptile = [sb(f"ptile{i}", [128, G * 512], BF16) for i in range(3)]
    ybuf = [sb(f"ybuf{i}", [128, 1024], F32) for i in range(2)]
    xblk0 = sb("xblk0", [128, 4 * 1024], F32)
    xblk = [xblk0, xblk0]
    wtr = sb("wtr", [128, 4 * 512], BF16)
    wti = sb("wti", [128, 4 * 512], BF16)
    bias2r = sb("bias2r", [2, 512], BF16)
    bias2i = sb("bias2i", [2, 512], BF16)
    identb = sb("identb_sb", [128, 128], BF16)
    master = sb("master_sb", [128, 896], BF16)
    ones2 = sb("ones2_sb", [2, 128], BF16)
    epst = sb("epst", [128, 1], F32)
    onecol = sb("onecol", [128, 1], F32)
    sum_s = [sb(f"sum_s{i}", [128, 1], F32) for i in range(4)]
    ssq_s = [sb(f"ssq_s{i}", [128, 1], F32) for i in range(4)]
    m_s = [sb(f"m_s{i}", [128, 1], F32) for i in range(4)]
    vr_s = [sb(f"vr_s{i}", [128, 1], F32) for i in range(4)]
    t_s = [sb(f"t_s{i}", [128, 1], F32) for i in range(4)]
    var_s = [sb(f"var_s{i}", [128, 1], F32) for i in range(4)]
    rstd_s = [sb(f"rstd_s{i}", [128, 1], F32) for i in range(4)]
    nb_s = [sb(f"nb_s{i}", [128, 1], F32) for i in range(4)]
    lsb = sb("lsb", [128, 4], F32)
    recs = [sb(f"recs{i}", [128, 1], F32) for i in range(4)]
    offs = sb("offs", [128, 1], F32)
    totals = sb("totals", [128, 1], F32)
    _kw0_addr = nc.lookup_mloc(kwin[0]).addr
    _kw1_addr = nc.lookup_mloc(kwin[1]).addr
    xnt1s = [nc.alloc_sbuf_tensor_at("xnt1a", [128, 8 * 512], BF16, offset=_kw0_addr).ap(),
             nc.alloc_sbuf_tensor_at("xnt1b", [128, 8 * 512], BF16, offset=_kw1_addr).ap()]
    otst = ptile[2][:, 0:512]
    xnst = otok
    uae = xres[:, 0:2048]
    ube = xres[:, 2048:4096]
    prod = xnst[:].bitcast(F32)
    xnt2f = xnt2[:].bitcast(F32)
    ua_sb = xnt2f[:, 0:512]
    wtf32 = xnt2f[:, 512:1024]
    ub_sb = xnt2f[:, 1024:1088]
    at_sb = xnt2f[0:32, 1088:1216]
    bt_sb = xnt2f[0:16, 1216:1280]
    gsta = xnt2f[:, 1280:1408]
    combd = xnt2f[0:32, 1408:1920]
    combg = xnt2f[0:16, 1920:2048]
    ybf1 = ybuf[1].bitcast(F32) if False else ybuf[1]
    ltri = ybuf[1][:, 512:640]
    identf = ybuf[1][:, 640:768]
    kst4 = [ptile[i // 2][:, (i % 2) * 512:(i % 2) * 512 + 512]
            for i in range(6)]
    qwf = qwin[:].bitcast(F32)
    bspl = qwin[0:1, 0:1024]
    bf32 = qwf[0:1, 512:1024]
    btmp = qwf[0:1, 1024:1536]
    btmp2 = qwf[0:1, 1536:2048]
    xrf = xres
    f128 = xrf[:, 0:512]
    cst = [xrf[:, 512:1024], xrf[:, 1024:1536]]
    casc = xrf[:, 1536:2048]
    ctmp = xrf[:, 2048:2560]
    csp = xres[:].bitcast(BF16)[:, 5120:6656]
    csn = xres[:].bitcast(BF16)[:, 6656:8192]
    sq_s = ybuf[0][:, 0:1024]
    fst = ybuf[1][0:16, 0:512]

    psG = psalloc("psG", [128, 3072], F32)
    pacc = psalloc("pacc", [128, 264], F32)
    pex = psalloc("pex", [128, 512], F32)
    psA = psG[:, 0:1536]
    psB = psG[:, 1536:3072]
    pstA = psG[:, 2048:2304].bitcast(BF16)
    pstB = psG[:, 2560:2816].bitcast(BF16)
    pstC = psG[:, 1024:1280].bitcast(BF16)
    pstD = psG[:, 1536:1792].bitcast(BF16)
    psY = [psG[:, 0:512], psG[:, 512:1024]]
    psF = [psG[:, 1024:1536], psG[:, 1536:2048]]

    sW = es.enter_context(nc.semaphore("sW"))
    sX = es.enter_context(nc.semaphore("sX"))
    sK0 = es.enter_context(nc.semaphore("sK0"))
    sK1 = es.enter_context(nc.semaphore("sK1"))
    sQ = es.enter_context(nc.semaphore("sQ"))
    sXR = es.enter_context(nc.semaphore("sXR"))
    sXN = es.enter_context(nc.semaphore("sXN"))
    sSt = es.enter_context(nc.semaphore("sSt"))
    sY = es.enter_context(nc.semaphore("sYm"))
    sP = es.enter_context(nc.semaphore("sP"))
    sA = es.enter_context(nc.semaphore("sA"))
    sV = es.enter_context(nc.semaphore("sV"))

    P = Plan()

    def dma(out, in_, **kw):
        return lambda eng: eng.dma_start(out=out, in_=in_, **kw)

    P.step("dve", lambda v: v.memset(qwin[64:128, :], 0.0), incs=[(sV, 1)],
           label="qw_zero")
    P.step("sync", dma(identb[:], identb_d), incs=[(sW, 16)], serialize=True)
    P.step("sync", dma(identf[:], identf_d), incs=[(sW, 16)], serialize=True)
    P.step("sync", dma(master[:], master_d), incs=[(sW, 16)], serialize=True)
    P.step("sync", dma(qwin[67:70, :], ones3_d), waits=[(sV, "qw_zero")],
           incs=[(sW, 16)], serialize=True)
    P.step("sync", dma(ones2[:], ones2_d), incs=[(sW, 16)], serialize=True)
    P.step("sync", dma(combd[:], comb_d_d), incs=[(sW, 16)], serialize=True)
    P.step("sync", dma(combg[:], comb_g_d), incs=[(sW, 16)], serialize=True)
    P.step("sync", dma(ltri[:], Ltri_d), incs=[(sW, 16)], label="ld_small", serialize=True)
    wk_r = wk_d.rearrange("(c p) n -> p c n", p=128)
    wq_r = wq_d.rearrange("(c p) n -> p c n", p=128)
    wf_r = wf_d.rearrange("(c p) n -> p c n", p=128)
    P.step("gp", dma(w1.rearrange("p (c n) -> p c n", c=8), wk_r),
           incs=[(sSt, 16)], label="ld_wk", serialize=True)
    P.step("gp", dma(w2.rearrange("p (c n) -> p c n", c=8), wq_r),
           incs=[(sSt, 16)], label="ld_wq", serialize=True)
    P.step("gp", dma(wfb.rearrange("p (c n) -> p c n", c=8), wf_r),
           incs=[(sSt, 16)], label="ld_wf", serialize=True)
    P.step("sync", dma(gsta[:, 0:32], g1r_d.rearrange("a b c d -> (a b c) d")),
           incs=[(sW, 16)], label="ld_g1r", serialize=True)
    P.step("sync", dma(gsta[:, 32:64], g1i_d.rearrange("a b c d -> (a b c) d")),
           incs=[(sW, 16)], label="ld_g1i", serialize=True)
    P.step("sync", dma(gsta[0:64, 64:80], g2r_d.rearrange("c e a g -> (c e a) g")),
           incs=[(sW, 16)], label="ld_g2r", serialize=True)
    P.step("sync", dma(gsta[0:64, 80:96], g2i_d.rearrange("c e a g -> (c e a) g")),
           incs=[(sW, 16)], label="ld_g2i", serialize=True)
    P.step("sync", dma(bf32[:], br_d.rearrange("(a n) -> a n", a=1)),
           incs=[(sW, 16)], label="ld_br", serialize=True)

    P.step("dve", lambda v: v.memset(epst[:], 1e-5), waits=[(sW, "ld_small")],
           incs=[(sV, 1)], label="eps_set")
    P.step("dve", lambda v: v.memset(onecol[:], 1.0))
    P.step("dve", lambda v: v.memset(vst[:], 1.0), incs=[(sV, 1)], label="vst_ones")
    P.step("dve", lambda v: v.tensor_scalar_mul(w2[:], w2[:], 0.125),
           waits=[(sSt, "ld_wq")], incs=[(sV, 1)], label="wq_scaled")

    def build_wt(tag, g1_cols, g2_cols, wt_tile, neg_tile):
        wtr1 = [(sW, "ld_g1" + tag), (sW, "ld_small")]
        if tag == "i":
            wtr1.append((sV, "ubcr"))
        P.step("pe", (lambda t, _a=g1_cols:
                      t.transpose(pex[0:32, 0:128], gsta[:, _a[0]:_a[1]], identf[:])),
               waits=wtr1, incs=[(sP, 1)], label=f"tr_g1{tag}")
        P.step("dve", lambda v: v.tensor_copy(at_sb[:], pex[0:32, 0:128]),
               waits=[(sP, f"tr_g1{tag}")], incs=[(sV, 1)], label=f"at{tag}")
        P.step("pe", (lambda t, _a=g2_cols:
                      t.transpose(pex[0:16, 128:192], gsta[0:64, _a[0]:_a[1]], identf[0:64, 0:64])),
               waits=[(sW, "ld_g2" + tag), (sV, f"at{tag}")],
               incs=[(sP, 1)], label=f"tr_g2{tag}")
        P.step("dve", lambda v: v.tensor_copy(bt_sb[:], pex[0:16, 128:192]),
               waits=[(sP, f"tr_g2{tag}")], incs=[(sV, 1)], label=f"bt{tag}")

        def mk_ua(t):
            ins = None
            for p4 in range(4):
                ins = t.matmul(pex[:, p4 * 128:p4 * 128 + 128],
                               combd[:, p4 * 128:p4 * 128 + 128], at_sb[:],
                               start=True, stop=True)
            return ins
        P.step("pe", mk_ua, waits=[(sV, f"at{tag}"), (sV, f"bt{tag}")],
               incs=[(sP, 1)], label=f"ua{tag}")
        P.step("dve", lambda v: v.tensor_copy(ua_sb[:], pex[:, 0:512]),
               waits=[(sP, f"ua{tag}")], incs=[(sV, 1)], label=f"uac{tag}")
        P.step("pe", lambda t: t.matmul(pex[:, 0:64], combg[:], bt_sb[:],
                                        start=True, stop=True),
               waits=[(sV, f"uac{tag}")], incs=[(sP, 1)], label=f"ub{tag}")
        P.step("dve", lambda v: v.tensor_copy(ub_sb[:], pex[:, 0:64]),
               waits=[(sP, f"ub{tag}")], incs=[(sV, 1)], label=f"ubc{tag}")

        def mk_wt_chunk(v, p4, _wt=wt_tile, _neg=neg_tile):
            ua_r = ua_sb[:, p4 * 128:p4 * 128 + 128].rearrange("p (b c) -> p b c", c=4)
            uae_r = uae.rearrange("p (b c e) -> p e b c", c=4, e=16)
            for e in range(16):
                v.tensor_copy(uae_r[:, e], ua_r)
            ub_r = ub_sb[:, 0:64].rearrange("p (c e) -> p c e", c=4)
            ube_r = ube.rearrange("p (b c e) -> p b c e", c=4, e=16)
            for b8 in range(32):
                v.tensor_copy(ube_r[:, b8], ub_r)
            ins = v.tensor_mul(prod, uae, ube)
            return ins
        for p4 in range(4):
            P.step("dve", (lambda v, _p4=p4: mk_wt_chunk(v, _p4)),
                   waits=[(sV, f"ubc{tag}")] if p4 == 0 else [(sV, f"wtred{tag}{p4-1}")],
                   incs=[(sV, 1)], label=f"wtp{tag}{p4}")
            def red(v, _p4=p4, _wt=wt_tile, _neg=neg_tile):
                pr2 = prod.rearrange("p (b c e) -> p b c e", c=4, e=16)
                wv_ = wtf32.rearrange("p (b e) -> p b e", e=16)
                v.tensor_tensor(wv_, pr2[:, :, 0], pr2[:, :, 1], ALU.add)
                v.tensor_tensor(wv_, wv_, pr2[:, :, 2], ALU.add)
                v.tensor_tensor(wv_, wv_, pr2[:, :, 3], ALU.add)
                return v.tensor_copy(_wt[:, _p4 * 512:_p4 * 512 + 512], wtf32[:])
            P.step("dve", red, waits=[(sV, f"wtp{tag}{p4}")],
                   incs=[(sV, 1)], label=f"wtred{tag}{p4}")
        P.step("dve", lambda v: v.tensor_copy(btmp[:], btmp[:]),
               waits=[(sV, f"wtred{tag}3")], incs=[(sV, 1)], label=f"wt{tag}")

    build_wt("r", (0, 32), (64, 80), wtr, None)
    build_wt("i", (32, 64), (80, 96), wti, None)
    if _DBG_DUMP:
        P.step("gp", dma(dbgw_d[0], wtr[:]), waits=[(sV, "wtr")], incs=[(sSt, 16)])
        P.step("gp", dma(dbgw_d[1], wti[:]), waits=[(sV, "wti")], incs=[(sSt, 16)])

    def bias_split(v):
        v.tensor_copy(bspl[0:1, 0:512], bf32[:])
        v.tensor_copy(btmp[:], bspl[0:1, 0:512])
        v.tensor_sub(btmp2[:], bf32[:], btmp[:])
        return v.tensor_copy(bspl[0:1, 512:1024], btmp2[:])
    P.step("dve", bias_split, waits=[(sW, "ld_br")], incs=[(sV, 1)], label="bsp_r0")
    P.step("gp", dma(bias2r[0:1, :], bspl[0:1, 0:512]), waits=[(sV, "bsp_r0")],
           incs=[(sSt, 16)], serialize=True)
    P.step("gp", dma(bias2r[1:2, :], bspl[0:1, 512:1024]), incs=[(sSt, 16)], label="bsp_r", serialize=True)
    P.step("dve", lambda v: v.tensor_copy(btmp2[:], btmp2[:]),
           waits=[(sSt, "bsp_r")], incs=[(sV, 1)], label="bsplit_r")
    P.step("sync", dma(bf32[:], bi_d.rearrange("(a n) -> a n", a=1)),
           waits=[(sV, "bsplit_r")], incs=[(sW, 16)], label="ld_bi", serialize=True)
    P.step("dve", bias_split, waits=[(sW, "ld_bi")], incs=[(sV, 1)], label="bsp_i0")
    P.step("gp", dma(bias2i[0:1, :], bspl[0:1, 0:512]), waits=[(sV, "bsp_i0")],
           incs=[(sSt, 16)], serialize=True)
    P.step("gp", dma(bias2i[1:2, :], bspl[0:1, 512:1024]), incs=[(sSt, 16)], label="bsp_i", serialize=True)

    P._ph0 = len(P.steps)
    xr = x_d.rearrange("(b s p) d -> b p s d", p=128, s=4)

    def emit_front(b):
        bb = b % 2
        w = [(sV, f"aff{b-1}")] if b >= 1 else []
        P.step("sync", dma(xblk[bb].rearrange("p (s d) -> p s d", s=4), xr[b]),
               waits=w, incs=[(sX, 16)], label=f"xld{b}")

        def ln_stats(v, _bb=bb):
            ins = None
            for s in range(4):
                xsub = xblk[_bb][:, s * 1024:(s + 1) * 1024]
                v.reduce_sum(sum_s[s][:], xsub, axis=AX.X)
                v.tensor_mul(sq_s[:], xsub, xsub)
                ins = v.reduce_sum(ssq_s[s][:], sq_s[:], axis=AX.X)
            return ins
        P.step("dve", ln_stats, waits=[(sX, 16 * (b + 1))], incs=[(sV, 1)],
               label=f"lnst{b}")

        def ln_ab(v):
            ins = None
            for s in range(4):
                v.tensor_scalar_mul(m_s[s][:], sum_s[s][:], 1.0 / 1024)
                ins = v.tensor_scalar_mul(vr_s[s][:], ssq_s[s][:], 1.0 / 1024)
            return ins
        P.step("dve", ln_ab, waits=[(sV, f"lnst{b}")], incs=[(sV, 1)], label=f"lnab{b}")

        def ln_c(v):
            ins = None
            for s in range(4):
                ins = v.tensor_mul(t_s[s][:], m_s[s][:], m_s[s][:])
            return ins
        P.step("dve", ln_c, waits=[(sV, f"lnab{b}")], incs=[(sV, 1)], label=f"lnc{b}")

        def ln_d(v):
            ins = None
            for s in range(4):
                ins = v.tensor_sub(var_s[s][:], vr_s[s][:], t_s[s][:])
            return ins
        P.step("dve", ln_d, waits=[(sV, f"lnc{b}")], incs=[(sV, 1)], label=f"lnd{b}")

        def ln_ln(sc):
            ins = None
            for s in range(4):
                ins = sc.activation(rstd_s[s][:], var_s[s][:], AF.Ln, bias=epst[:, 0:1])
            return ins
        P.step("act", ln_ln, waits=[(sV, f"lnd{b}"), (sV, "eps_set")],
               incs=[(sA, 1)], label=f"lnln{b}")

        def ln_exp(sc):
            ins = None
            for s in range(4):
                ins = sc.activation(rstd_s[s][:], rstd_s[s][:], AF.Exp, scale=-0.5)
            return ins
        P.step("act", ln_exp, waits=[(sA, f"lnln{b}")], incs=[(sA, 1)], label=f"rstd{b}")

        def ln_nb(v):
            ins = None
            for s in range(4):
                ins = v.scalar_tensor_tensor(nb_s[s][:], m_s[s][:], -1.0, rstd_s[s][:],
                                             ALU.mult, ALU.mult)
            return ins
        P.step("dve", ln_nb, waits=[(sA, f"rstd{b}")], incs=[(sV, 1)], label=f"lnnb{b}")

        def ln_aff(v, _bb=bb):
            ins = None
            for s in range(4):
                ins = v.tensor_scalar(xnst[:, s * 1024:(s + 1) * 1024],
                                      xblk[_bb][:, s * 1024:(s + 1) * 1024],
                                      rstd_s[s][:, 0:1], nb_s[s][:, 0:1], ALU.mult, ALU.add)
            return ins
        waff = [(sV, f"lnnb{b}")]
        if b >= 1:
            waff.append((sP, f"trxend{b-1}"))
        P.step("dve", ln_aff, waits=waff, incs=[(sV, 1)], label=f"aff{b}")

        for s in range(4):
            for c in range(8):
                i = s * 8 + c
                pst = pstA if i % 2 == 0 else pstB
                wtt = [(sV, f"aff{b}")]
                if b > 0 or i >= 2:
                    wtt.append((sA, f"evx{b}_{i-2}" if i >= 2 else f"evx{b-1}_{30+i}"))
                P.step("pe", (lambda t, _s=s, _c=c, _p=pst:
                              t.transpose(_p[:, 0:128],
                                          xnst[:, _s * 1024 + _c * 128:_s * 1024 + _c * 128 + 128],
                                          identb[:])),
                       waits=wtt, incs=[(sP, 1)], label=f"trx{b}_{i}")
                P.step("act", (lambda sc, _s=s, _c=c, _p=pst, _x=xnt1s[b % 2]:
                               sc.copy(_x[:, _c * 512 + _s * 128:_c * 512 + _s * 128 + 128],
                                       _p[:, 0:128])),
                       waits=[(sP, f"trx{b}_{i}")], incs=[(sA, 1)], label=f"evx{b}_{i}")
        P.step("pe", lambda t: t.nop(), waits=[(sP, f"trx{b}_31")],
               incs=[(sP, 1)], label=f"trxend{b}")
        P.step("gp", dma(xbuf[b], xnt1s[b % 2][:]), waits=[(sA, f"evx{b}_31")],
               incs=[(sSt, 16)], label=f"xbst{b}", serialize=True)

        continue_marker = None
    def emit_mmblock(b):
        xnt1 = xnt1s[b % 2]
        for wi in range(2):
            wt_tile = w1 if wi == 0 else w2
            buf = kbuf if wi == 0 else qbuf
            pko = 0 if wi == 0 else 512
            for m in range(8):
                def mmk(t, _wt=wt_tile, _m=m, _o=pko):
                    ins = None
                    for c in range(8):
                        ins = t.matmul(psA[:, _o:_o + 512],
                                       _wt[:, c * 1024 + _m * 128:c * 1024 + _m * 128 + 128],
                                       xnt1[:, c * 512:c * 512 + 512],
                                       start=(c == 0), stop=(c == 7))
                    return ins
                wts = [(sA, f"evx{b}_31"), (sSt, "ld_wk" if wi == 0 else "ld_wq")]
                if wi == 1:
                    wts.append((sV, "wq_scaled"))
                if b == 0 and wi == 0 and m == 0:
                    wts.append((sV, "wtr"))
                    wts.append((sV, "wti"))
                if b > 0 or m > 0:
                    prev = f"evk{b}_{wi}_{m-1}" if m > 0 else f"evk{b-1}_{wi}_7"
                    wts.append((sA, prev))
                P.step("pe", mmk, waits=wts, incs=[(sP, 1)], label=f"mmk{b}_{wi}_{m}")
                jj = (b * 16 + wi * 8 + m)
                wev = [(sP, f"mmk{b}_{wi}_{m}")]
                if jj >= 6:
                    pj = jj - 6
                    wev.append((sSt, f"kqst{pj // 16}_{(pj % 16) // 8}_{pj % 8}"))
                P.step("act", (lambda sc, _j=jj, _o=pko:
                               sc.copy(kst4[_j % 6][:], psA[:, _o:_o + 512])),
                       waits=wev, incs=[(sA, 1)],
                       label=f"evk{b}_{wi}_{m}")
                if wi == 0:
                    P.step("gp", dma(buf[2 * m:2 * m + 2, :, b * 512:(b + 1) * 512]
                                     .rearrange("h r t -> (h r) t"), kst4[jj % 6][:]),
                           waits=[(sA, f"evk{b}_{wi}_{m}")], incs=[(sSt, 16)],
                           label=f"kqst{b}_{wi}_{m}", serialize=(m == 0))
                else:
                    P.step("gp", dma(qbuf[b, :, 2 * m, :], kst4[jj % 6][0:64, :]),
                           waits=[(sA, f"evk{b}_{wi}_{m}")], incs=[(sSt, 16)])
                    P.step("gp", dma(qbuf[b, :, 2 * m + 1, :], kst4[jj % 6][64:128, :]),
                           incs=[(sSt, 16)], label=f"kqst{b}_{wi}_{m}")

        def mmf(t):
            ins = None
            for c in range(8):
                ins = t.matmul(pex[0:16, 0:512], wfb[:, c * 16:c * 16 + 16],
                               xnt1[:, c * 512:c * 512 + 512], start=(c == 0), stop=(c == 7))
            return ins
        wtsf = [(sA, f"evx{b}_31"), (sSt, "ld_wf")]
        if b == 0:
            wtsf += [(sV, "wtr"), (sV, "wti")]
        else:
            wtsf.append((sA, f"evf{b-1}"))
        P.step("pe", mmf, waits=wtsf, incs=[(sP, 1)], label=f"mmf{b}")
        wevf = [(sP, f"mmf{b}")]
        if b > 0:
            wevf.append((sSt, f"fst{b-1}"))
        P.step("act", lambda sc: sc.copy(fst[:], pex[0:16, 0:512]),
               waits=wevf, incs=[(sA, 1)], label=f"evf{b}")
        P.step("gp", dma(fbuf.rearrange("(h s) t -> h s t", s=8)[:, b, :], fst[:]),
               waits=[(sA, f"evf{b}")], incs=[(sSt, 16)], label=f"fst{b}")

    for b in range(NBLK):
        emit_front(b)
        if b >= 1:
            emit_mmblock(b - 1)
    emit_mmblock(NBLK - 1)

    def kw_init(v, kb):
        v.memset(kwin[kb][64:128, :], 0.0)
        return v.memset(kwin[kb][64:67, :], 1.0)
    P.step("dve", lambda v: kw_init(v, 0),
           waits=[(sP, f"mmf{NBLK-2}"), (sSt, f"xbst{NBLK-2}")],
           incs=[(sV, 1)], label="kw0_ones")
    P.step("dve", lambda v: kw_init(v, 1),
           waits=[(sP, f"mmf{NBLK-1}"), (sSt, f"xbst{NBLK-1}")],
           incs=[(sV, 1)], label="kw1_ones")

    P._ph1 = len(P.steps)
    wv_r = wv_d.rearrange("(c p) n -> p c n", p=128)
    wo_r = wo_d.rearrange("(c p) n -> p c n", p=128)
    P.step("gp", dma(w1.rearrange("p (c n) -> p c n", c=8), wv_r),
           waits=[(sP, f"mmk{NBLK-1}_0_7")], incs=[(sSt, 16)], label="ld_wv", serialize=True)
    P.step("gp", dma(w2.rearrange("p (c n) -> p c n", c=8), wo_r),
           waits=[(sP, f"mmk{NBLK-1}_1_7")], incs=[(sSt, 16)], label="ld_wo", serialize=True)

    for b in range(NBLK):
        wts = [(sSt, f"xbst{b}")]
        if b > 0:
            wts.append((sP, f"mmv{b-1}_3_1"))
        P.step("sync", dma(xnt2[:], xbuf[b]), waits=wts, incs=[(sX, 16)], label=f"xrld{b}")
        for s in range(4):
            for n in range(2):
                def mmv(t, _s=s, _n=n):
                    ins = None
                    for c in range(8):
                        ins = t.matmul(psG[:, 1024 + _n * 512:1024 + _n * 512 + 512],
                                       xnt2[:, c * 512 + _s * 128:c * 512 + _s * 128 + 128],
                                       w1[:, c * 1024 + _n * 512:c * 1024 + _n * 512 + 512],
                                       start=(c == 0), stop=(c == 7))
                    return ins
                wts = [(sX, 16 * (NBLK + b + 1)), (sSt, "ld_wv")]
                if b == 0 and s == 0 and n == 0:
                    wts.append((sA, f"evx{NBLK-1}_31"))
                if b > 0 or s > 0 or n > 0:
                    pv = (f"evv{b}_{s}_{n-1}" if n > 0 else
                          (f"evv{b}_{s-1}_1" if s > 0 else f"evv{b-1}_3_1"))
                    wts.append((sA, pv))
                P.step("pe", mmv, waits=wts, incs=[(sP, 1)], label=f"mmv{b}_{s}_{n}")
                kap = 4 * b + s
                P.step("act", (lambda sc, _k=kap, _n=n:
                               sc.copy(vst.rearrange("p k (h c) -> p k h c", c=65)
                                       [:, _k, 8 * _n:8 * _n + 8, 0:64],
                                       (psG[:, 1024 + _n * 512:1024 + _n * 512 + 512]
                                        ).rearrange("p (h c) -> p h c", c=64))),
                       waits=[(sP, f"mmv{b}_{s}_{n}"), (sV, "vst_ones")],
                       incs=[(sA, 1)], label=f"evv{b}_{s}_{n}")

    P._ph2 = len(P.steps)
    P.step("sync", dma(f128[:], fbuf), waits=[(sSt, f"fst{NBLK-1}")],
           incs=[(sX, 16)], label="fld")
    P.step("act", lambda sc: sc.activation(casc[:], f128[:], AF.Abs),
           waits=[(sX, 16 * (2 * NBLK + 1))], incs=[(sA, 1)], label="c_abs")
    P.step("act", lambda sc: sc.activation(casc[:], casc[:], AF.Exp, scale=-1.0),
           waits=[(sA, "c_abs")], incs=[(sA, 1)], label="c_exp")
    P.step("act", lambda sc: sc.activation(casc[:], casc[:], AF.Ln, bias=onecol[:, 0:1]),
           waits=[(sA, "c_exp")], incs=[(sA, 1)], label="c_ln")
    P.step("dve", lambda v: v.tensor_scalar_min(ctmp[:], f128[:], 0.0),
           waits=[(sA, "c_ln")], incs=[(sV, 1)], label="c_min")
    P.step("dve", lambda v: v.tensor_sub(cst[0][:], ctmp[:], casc[:]),
           waits=[(sV, "c_min")], incs=[(sV, 1)], label="c_lf")
    src_i = 0
    for st_i, sh in enumerate([1, 2, 4, 8, 16, 32, 64, 128, 256]):
        dst = 1 - src_i
        def csum(v, _s=src_i, _d=dst, _sh=sh):
            v.tensor_copy(cst[_d][:, 0:_sh], cst[_s][:, 0:_sh])
            return v.tensor_add(cst[_d][:, _sh:512], cst[_s][:, _sh:512],
                                cst[_s][:, 0:512 - _sh])
        P.step("dve", csum, waits=[(sV, "c_lf" if st_i == 0 else f"cs{st_i-1}")],
               incs=[(sV, 1)], label=f"cs{st_i}")
        src_i = dst
    cfin = cst[src_i]
    P.step("dve", lambda v: v.tensor_copy(totals[:], cfin[:, 511:512]),
           waits=[(sV, "cs8")], incs=[(sV, 1)], label="c_tot")
    P.step("pe", lambda t: t.matmul(pex[:, 0:1], ltri[:], totals[:], start=True, stop=True),
           waits=[(sV, "c_tot"), (sA, f"evf{NBLK-1}")], incs=[(sP, 1)], label="c_off")
    P.step("dve", lambda v: v.tensor_copy(offs[:], pex[:, 0:1]),
           waits=[(sP, "c_off")], incs=[(sV, 1)], label="c_offc")
    P.step("dve", lambda v: v.tensor_scalar_add(ctmp[:], cfin[:], offs[:, 0:1]),
           waits=[(sV, "c_offc")], incs=[(sV, 1)], label="c_full")
    def csplit1(v):
        v.tensor_copy(csp[:, 0:512], ctmp[:])
        return v.tensor_copy(casc[:], csp[:, 0:512])
    P.step("dve", csplit1, waits=[(sV, "c_full")], incs=[(sV, 1)], label="csp1")
    def csplit2(v):
        v.tensor_sub(f128[:], ctmp[:], casc[:])
        return v.tensor_copy(csp[:, 512:1024], f128[:])
    P.step("dve", csplit2, waits=[(sV, "csp1")], incs=[(sV, 1)], label="csp2")
    def csplit3(v):
        v.tensor_copy(casc[:], csp[:, 512:1024])
        return v.tensor_sub(f128[:], f128[:], casc[:])
    P.step("dve", csplit3, waits=[(sV, "csp2")], incs=[(sV, 1)], label="csp3")
    P.step("dve", lambda v: v.tensor_copy(csp[:, 1024:1536], f128[:]),
           waits=[(sV, "csp3")], incs=[(sV, 1)], label="csp4")
    P.step("dve", lambda v: v.tensor_scalar_mul(csn[:], csp[:], -1.0),
           waits=[(sV, "csp4")], incs=[(sV, 1)], label="csp5")
    for hh in range(H):
        for sp in range(3):
            P.step("gp", dma(cqbuf[hh, sp, :].rearrange("(s t) -> s t", s=8),
                             csp[8 * hh:8 * hh + 8, sp * 512:sp * 512 + 512]),
                   waits=[(sV, "csp5")] if (hh == 0 and sp == 0) else [],
                   incs=[(sSt, 16)], label=f"cq{hh}_{sp}")
            P.step("gp", dma(ckbuf[hh, sp, :].rearrange("(s t) -> s t", s=8),
                             csn[8 * hh:8 * hh + 8, sp * 512:sp * 512 + 512]),
                   incs=[(sSt, 16)], label=f"ck{hh}_{sp}")

    P._ph3 = len(P.steps)
    xr2 = x_d.rearrange("(t s p) d -> t p s d", p=128, s=4)
    yr = y_d.rearrange("(t s p) d -> t s p d", p=128, s=4)

    gg_ctr = [0]
    nrm_ctr = [0]
    msk_ctr = [0]
    sg_last = {}
    pend_nrm = [None]
    pend_av = []
    pend_wo = []
    def flush_av(keep=0):
        while len(pend_av) > keep:
            _fn, _w, _lab = pend_av.pop(0)
            P.step("pe", _fn, waits=_w, incs=[(sP, 1)], label=_lab)

    set_off = [0]
    for t in range(NT):
        w = 512 * (t + 1)
        nch = 4 * (t + 1)
        tile_gg0 = [gg_ctr[0]]
        if t > 0:
            set_off[0] = (2 - gg_ctr[0]) % 3
        def emit_qld(_t):
            wq_ = ([(sSt, f"ck{H-1}_2")] if _t == 0
                   else [(sP, sg_last[(_t - 1, H - 1)])])
            P.step("gp", dma(qwin[0:64, :], qbuf[_t].rearrange("r h t -> r (h t)")),
                   waits=wq_, incs=[(sQ, 16)], label=f"qld{_t}")
            P.step("gp", dma(qwin[64:67, :].rearrange("i (h t) -> i h t", h=H),
                             cqbuf[:, :, 512 * _t:512 * _t + 512]
                             .rearrange("h i t -> i h t")),
                   incs=[(sQ, 16)], label=f"qldc{_t}")
        if t == 0:
            emit_qld(0)
        wx = [(sSt, f"ck{H-1}_2")] if t == 0 else [(sV, f"comb{t-1}_3_1")]
        P.step("gp", dma(xres.rearrange("p (s d) -> p s d", s=4), xr2[t]),
               waits=wx, incs=[(sXR, 16)], label=f"xres{t}")
        wxn = [(sP, f"mmv{NBLK-1}_3_1")] if t == 0 else [(sP, f"ff{t-1}_3")]
        P.step("gp", dma(xnt2[:], xbuf[t]), waits=wxn, incs=[(sXN, 16)], label=f"xnld{t}")

        for hh in range(H):
            kb = hh % 2
            wk_ = [(sV, "kw0_ones" if kb == 0 else "kw1_ones"), (sSt, f"ck{H-1}_2")]
            if t * H + hh >= 2:
                if hh >= 2:
                    wk_.append((sP, sg_last[(t, hh - 2)]))
                else:
                    wk_.append((sP, sg_last[(t - 1, H - 2 + hh)]))
            sKp = sK0 if kb == 0 else sK1
            P.step("sync", dma(kwin[kb][0:64, 0:w], kbuf[hh, :, 0:w]),
                   waits=wk_, incs=[(sKp, 16)])
            P.step("sync", dma(kwin[kb][67:70, 0:w], ckbuf[hh, :, 0:w]),
                   incs=[(sKp, 16)], label=f"kld{t}_{hh}")

            ngr = (nch + G - 1) // G
            for g in range(ngr):
                k0, k1 = G * g, min(G * g + G, nch)
                gg = gg_ctr[0]
                gset = (gg + set_off[0]) % 3
                pgrp = psG[:, 1024 * gset:1024 * gset + 1024]
                ptl = ptile[gset]

                def mms(ten, _k0=k0, _k1=k1, _p=pgrp, _hh=hh):
                    ins = None
                    for kk in range(_k0, _k1):
                        ins = ten.matmul(_p[:, 512 * (kk - _k0):512 * (kk - _k0) + 512],
                                         kwin[_hh % 2][:, 128 * kk:128 * kk + 128],
                                         qwin[:, 512 * _hh:512 * _hh + 512],
                                         start=True, stop=True)
                    return ins
                wts = []
                if g == 0:
                    wts += [(sK0 if hh % 2 == 0 else sK1, f"kld{t}_{hh}"),
                            (sQ, f"qldc{t}")]
                if gg >= 3:
                    wts.append((sA, f"exp{gg-3}"))
                if gg == 0:
                    wts.append((sA, f"evk{NBLK-1}_1_7"))
                    wts.append((sA, f"evf{NBLK-1}"))
                if gg == 1:
                    wts.append((sA, f"evv{NBLK-1}_3_1"))
                if gg == 2:
                    wts.append((sA, f"evx{NBLK-1}_31"))
                if t > 0 and (gg - tile_gg0[0]) < 3 and True:
                    wts.append((sV, f"comb{t-1}_3_1"))
                P.step("pe", mms, waits=wts, incs=[(sP, 1)], label=f"sg{gg}")
                flush_av(keep=2)
                if g == 0:
                    flush_av(keep=0)
                    if pend_nrm[0] is not None:
                        _enrm, _avlab = pend_nrm[0]
                        pend_nrm[0] = None
                        _enrm(_avlab=_avlab)
                if g == ngr - 1:
                    sg_last[(t, hh)] = f"sg{gg}"

                def expg(sc, _p=pgrp, _pt=ptl, _n=512 * (k1 - k0)):
                    return sc.activation(_pt[:, 0:_n], _p[:, 0:_n], AF.Exp)
                we = [(sP, f"sg{gg}")]
                if gg >= 3:
                    we.append((sP, f"avg{gg-3}"))
                P.step("act", expg, waits=we, incs=[(sA, 1)], label=f"exp{gg}")

                mlabs = []
                for kk in range(k0, k1):
                    m = kk - 4 * t
                    if m >= 0:
                        msk_ctr[0] += 1
                        def msk(v, _pt=ptl, _off=512 * (kk - k0), _m=m):
                            return v.tensor_tensor(
                                _pt[:, _off:_off + 512], _pt[:, _off:_off + 512],
                                master[:, 384 - 128 * _m:896 - 128 * _m], ALU.min)
                        P.step("dve", msk, waits=[(sA, f"exp{gg}")],
                               incs=[(sV, 1)], label=f"msk{msk_ctr[0]}")
                        mlabs.append(f"msk{msk_ctr[0]}")

                def av(ten, _k0=k0, _k1=k1, _pt=ptl, _hh=hh, _nch=nch):
                    pa = pacc if _hh % 2 == 0 else pex[:, 0:264]
                    ins = None
                    for kk in range(_k0, _k1):
                        for s4 in range(4):
                            ins = ten.matmul(
                                pa[:, 66 * s4:66 * s4 + 65],
                                _pt[:, 512 * (kk - _k0) + 128 * s4:
                                    512 * (kk - _k0) + 128 * s4 + 128],
                                vst[:, kk, 65 * _hh:65 * _hh + 65],
                                start=(kk == 0 and s4 == 0),
                                stop=(kk == _nch - 1 and s4 == 3),
                                skip_group_check=True)
                    return ins
                wav = [(sA, f"exp{gg}")] + [(sV, ml) for ml in mlabs]
                if g == 0:
                    if nrm_ctr[0] >= 1 + (1 if hh >= 2 else 0):
                        pass
                    if hh >= 2:
                        wav.append((sV, f"nrm{(t * H + hh) - 2 + 1}"))
                    elif t > 0:
                        if hh == 0:
                            wav.append((sV, f"nrm{(t - 1) * H + H - 2 + 1}"))
                        else:
                            wav.append((sV, f"comb{t-1}_3_1"))
                    elif t == 0 and hh == 1:
                        wav.append((sV, "c_offc"))
                pend_av.append((av, wav, f"avg{gg}"))
                gg_ctr[0] += 1

            def emit_nrm(_t=t, _hh=hh, _avlab=None):
                pa = pacc if _hh % 2 == 0 else pex[:, 0:264]
                P.step("dve", (lambda v, _pa=pa: v.tensor_copy(
                           lsb[:], _pa.rearrange("p (s c) -> p s c", c=66)[:, :, 64])),
                       waits=[(sP, _avlab)], incs=[(sV, 1)],
                       label=f"nrmA{_t}_{_hh}")
                def nrm_b(v):
                    ins = None
                    for s4 in range(4):
                        ins = v.reciprocal(recs[s4][:], lsb[:, s4:s4 + 1])
                    return ins
                P.step("dve", nrm_b, waits=[(sV, f"nrmA{_t}_{_hh}")], incs=[(sV, 1)],
                       label=f"nrmB{_t}_{_hh}")
                def nrm_c(v):
                    ins = None
                    for s4 in range(4):
                        ins = v.tensor_scalar_mul(
                            otok[:, 1024 * s4 + 64 * _hh:1024 * s4 + 64 * _hh + 64],
                            pa[:, 66 * s4:66 * s4 + 64], recs[s4][:, 0:1])
                    return ins
                wn = [(sV, f"nrmB{_t}_{_hh}")]
                if _t == 0 and _hh == 0:
                    wn.append((sP, f"trxend{NBLK-1}"))
                if _t > 0 and _hh == 0:
                    wn.append((sP, f"odone{_t-1}"))
                nrm_ctr[0] += 1
                P.step("dve", nrm_c, waits=wn, incs=[(sV, 1)], label=f"nrm{nrm_ctr[0]}")
            pend_nrm[0] = (emit_nrm, f"avg{gg_ctr[0]-1}")

        if _DBG_DUMP and t == 0:
            P.step("gp", dma(dbgo_d, otok[:]), waits=[(sV, f"nrm{nrm_ctr[0]}")],
                   incs=[(sSt, 16)])
        flush_av(keep=0)
        if pend_nrm[0] is not None:
            _enrm, _avlab = pend_nrm[0]
            pend_nrm[0] = None
            _enrm(_avlab=_avlab)
        nrm_all = f"nrm{nrm_ctr[0]}"
        if t + 1 < NT:
            emit_qld(t + 1)
        pst4 = [pstC, pstD, pstA, pstB]
        pstall = psG[:, 1024:3072].bitcast(BF16).rearrange("p (b k) -> p b k",
                                                           k=1024)
        for s4 in range(4):
            for h in range(2):
                rid = s4 * 2 + h
                for j in (2, 3, 0, 1):
                    c = 4 * h + j
                    wtt = [(sV, nrm_all)]
                    if rid >= 1:
                        wtt.append((sV, f"oev{t}_{s4}_{h-1}" if h == 1
                                    else f"oev{t}_{s4-1}_1"))
                    if h == 0 and j in (0, 1):
                        if s4 > 0:
                            wtt.append((sV, f"comb{t}_{s4-1}_1"))
                        elif t > 0:
                            wtt.append((sV, f"comb{t-1}_3_1"))
                    P.step("pe", (lambda ten, _s=s4, _c=c, _p=pst4[j]:
                                  ten.transpose(_p[:, 0:128],
                                                otok[:, 1024 * _s + 128 * _c:
                                                     1024 * _s + 128 * _c + 128],
                                                identb[:])),
                           waits=wtt, incs=[(sP, 1)], label=f"otr{t}_{s4}_{h}_{j}")
                if pend_wo:
                    _f, _w, _l = pend_wo.pop(0)
                    P.step("pe", _f, waits=_w, incs=[(sP, 1)], label=_l)
                woev = [(sP, f"otr{t}_{s4}_{h}_1")]
                if rid >= 1:
                    woev.append((sP, f"wo{t}_{s4}_{h-1}" if h == 1
                                 else f"wo{t}_{s4-1}_1"))
                P.step("dve", (lambda v:
                               v.tensor_copy(otst.rearrange("p (s k) -> p s k",
                                                            k=128),
                                             pstall[:, :, 0:128])),
                       waits=woev, incs=[(sV, 1)], label=f"oev{t}_{s4}_{h}")
                def mwo(ten, _h=h):
                    ins = None
                    for j2 in range(4):
                        cc = 4 * _h + j2
                        for n in range(2):
                            ins = ten.matmul(psY[n],
                                             otst[:, 128 * j2:128 * j2 + 128],
                                             w2[:, 1024 * cc + 512 * n:
                                                1024 * cc + 512 * n + 512],
                                             start=(_h == 0 and j2 == 0),
                                             stop=(_h == 1 and j2 == 3),
                                             skip_group_check=True)
                    return ins
                wwo = [(sV, f"oev{t}_{s4}_{h}"), (sSt, "ld_wo")]
                if h == 0:
                    if s4 > 0:
                        wwo.append((sV, f"comb{t}_{s4-1}_1"))
                    elif t > 0:
                        wwo.append((sV, f"comb{t-1}_3_1"))
                pend_wo.append((mwo, wwo, f"wo{t}_{s4}_{h}"))
            while pend_wo:
                _f, _w, _l = pend_wo.pop(0)
                P.step("pe", _f, waits=_w, incs=[(sP, 1)], label=_l)
            if s4 == 3:
                P.step("pe", lambda ten: ten.nop(), waits=[(sP, f"otr{t}_3_1_1")],
                       incs=[(sP, 1)], label=f"odone{t}")

            def ffn(ten, _s=s4):
                ins = None
                for c in range(4):
                    ten.matmul(psF[0],
                               xnt2[:, c * 512 + _s * 128:c * 512 + _s * 128 + 128],
                               wtr[:, c * 512:c * 512 + 512],
                               start=(c == 0), stop=False, skip_group_check=True)
                ten.matmul(psF[0], ones2[:], bias2r[:], start=False, stop=True,
                           skip_group_check=True)
                for c in range(4):
                    ten.matmul(pex[:, 0:512],
                               xnt2[:, (c + 4) * 512 + _s * 128:(c + 4) * 512 + _s * 128 + 128],
                               wti[:, c * 512:c * 512 + 512],
                               start=(c == 0), stop=(c == 3), skip_group_check=True)
                for c in range(8):
                    wt_t = wti if c < 4 else wtr
                    ten.matmul(psF[1],
                               xnt2[:, c * 512 + _s * 128:c * 512 + _s * 128 + 128],
                               wt_t[:, (c % 4) * 512:(c % 4) * 512 + 512],
                               start=(c == 0), stop=False, skip_group_check=True)
                ins = ten.matmul(psF[1], ones2[:], bias2i[:], start=False, stop=True,
                                 skip_group_check=True)
                return ins
            wff = [(sV, "wtr"), (sV, "wti"), (sXN, f"xnld{t}"), (sSt, "bsp_i"),
                   (sV, f"oev{t}_{s4}_1"), (sV, nrm_all)]
            if s4 > 0:
                wff.append((sV, f"comb{t}_{s4-1}_1"))
            elif t > 0:
                wff.append((sV, f"comb{t-1}_3_1"))
            else:
                wff.append((sV, "c_offc"))
            P.step("pe", ffn, waits=wff, incs=[(sP, 1)], label=f"ff{t}_{s4}")

            for n in range(2):
                def comb(v, _s=s4, _n=n):
                    yb = ybuf[_s % 2]
                    v.scalar_tensor_tensor(yb[:, 512 * _n:512 * _n + 512],
                                           psY[_n], 1.0,
                                           xres[:, 1024 * _s + 512 * _n:
                                                1024 * _s + 512 * _n + 512],
                                           ALU.mult, ALU.add)
                    if _n == 0:
                        v.scalar_tensor_tensor(yb[:, 0:512], pex[:, 0:512], -1.0,
                                               yb[:, 0:512], ALU.mult, ALU.add)
                    return v.scalar_tensor_tensor(yb[:, 512 * _n:512 * _n + 512],
                                                  psF[_n], 1.0,
                                                  yb[:, 512 * _n:512 * _n + 512],
                                                  ALU.mult, ALU.add)
                wc = [(sP, f"wo{t}_{s4}_1"), (sP, f"ff{t}_{s4}"), (sXR, f"xres{t}")]
                idx = 4 * t + s4
                if idx >= 2:
                    wc.append((sY, 16 * (idx - 1)))
                P.step("dve", comb, waits=wc, incs=[(sV, 1)], label=f"comb{t}_{s4}_{n}")
            P.step("gp", dma(yr[t, s4], ybuf[s4 % 2][:]),
                   waits=[(sV, f"comb{t}_{s4}_1")], incs=[(sY, 16)],
                   label=f"yst{t}_{s4}")

    P.step("gp", lambda g_: g_.nop(), waits=[(sY, 16 * 4 * NT)])

    if _DBG_PHASE < 4:
        cut = [P._ph0, P._ph1, P._ph2, P._ph3][_DBG_PHASE]
        P.steps = P.steps[:cut]
    if _DBG_NSTEPS is not None:
        P.steps = P.steps[:_DBG_NSTEPS]
    P.resolve()

    with nc.Block() as block:
        @block.sync
        def _(sync):
            P.emit("sync", sync)

        @block.vector
        def _(vector):
            P.emit("dve", vector)

        @block.scalar
        def _(scalar):
            P.emit("act", scalar)

        @block.tensor
        def _(tensor):
            P.emit("pe", tensor)

        @block.gpsimd
        def _(gpsimd):
            P.emit("gp", gpsimd)

    es.close()
    return nc


def kernel(L0_attn_x, L0_attn_Wq, L0_attn_Wk, L0_attn_Wv, L0_attn_Wo, L0_attn_Wf,
           L0_ffn_G1_R, L0_ffn_G2_R, L0_ffn_G1_I, L0_ffn_G2_I,
           L0_ffn_bias_real, L0_ffn_bias_imag):
    x = np.ascontiguousarray(np.asarray(L0_attn_x, dtype=np.float32))
    com = {
        "wq": np.ascontiguousarray(np.asarray(L0_attn_Wq, dtype=np.float32)),
        "wk": np.ascontiguousarray(np.asarray(L0_attn_Wk, dtype=np.float32)),
        "wv": np.ascontiguousarray(np.asarray(L0_attn_Wv, dtype=np.float32)),
        "wo": np.ascontiguousarray(np.asarray(L0_attn_Wo, dtype=np.float32)),
        "wf": np.ascontiguousarray(np.asarray(L0_attn_Wf, dtype=np.float32)),
        "g1r": np.ascontiguousarray(np.asarray(L0_ffn_G1_R, dtype=np.float32)),
        "g2r": np.ascontiguousarray(np.asarray(L0_ffn_G2_R, dtype=np.float32)),
        "g1i": np.ascontiguousarray(np.asarray(L0_ffn_G1_I, dtype=np.float32)),
        "g2i": np.ascontiguousarray(np.asarray(L0_ffn_G2_I, dtype=np.float32)),
        "br": np.ascontiguousarray(np.asarray(L0_ffn_bias_real, dtype=np.float32)),
        "bi": np.ascontiguousarray(np.asarray(L0_ffn_bias_imag, dtype=np.float32)),
    }
    nc = build_nc()
    in_maps = [dict(com, x=np.ascontiguousarray(x[b])) for b in range(8)]
    res = run_bass_kernel_spmd(nc, in_maps, core_ids=list(range(8)))
    return np.stack([res.results[b]["y"] for b in range(8)], axis=0)


# version 26
# speedup vs baseline: 1.2468x; 1.0589x over previous
import numpy as np
import ml_dtypes
from contextlib import ExitStack

import concourse.bass as bass
import concourse.mybir as mybir
from concourse.bass_utils import run_bass_kernel_spmd

F32 = mybir.dt.float32
BF16 = mybir.dt.bfloat16
AF = mybir.ActivationFunctionType
ALU = mybir.AluOpType
AX = mybir.AxisListType
BF = ml_dtypes.bfloat16

_DBG_PHASE = 4
_DBG_DUMP = False
_DBG_NSTEPS = None

N = 4096
D = 1024
H = 16
HD = 64
NBLK = 8
NT = 8
G = 2


class Plan:

    def __init__(self):
        self.steps = []
        self.labels = {}

    def step(self, eng, fn, waits=(), incs=(), label=None, serialize=False):
        self.steps.append([eng, fn, list(waits), list(incs), label, serialize])

    def resolve(self):
        cnt = {}
        for st in self.steps:
            _eng, _fn, waits, incs, label, serialize = st
            rw = []
            if serialize and incs:
                sem0 = incs[0][0]
                rw.append((sem0, cnt.get(id(sem0), 0)))
            for sem, lab in waits:
                if isinstance(lab, int):
                    rw.append((sem, lab))
                else:
                    s2, v = self.labels[lab]
                    assert s2 is sem, f"label {lab} bound to a different sem"
                    rw.append((sem, v))
            st[2] = rw
            for sem, n in incs:
                cnt[id(sem)] = cnt.get(id(sem), 0) + n
            if label is not None:
                assert len(incs) == 1, f"labeled step {label} must have exactly one inc"
                self.labels[label] = (incs[0][0], cnt[id(incs[0][0])])

    def emit(self, eng_name, eng):
        hi = {}
        for eng2, fn, waits, incs, _label, _ser in self.steps:
            if eng2 != eng_name:
                continue
            merged = {}
            for sem, val in waits:
                k = id(sem)
                if val > merged.get(k, (None, -1))[1]:
                    merged[k] = (sem, val)
            for k, (sem, val) in merged.items():
                if val > hi.get(k, -1):
                    eng.wait_ge(sem, val)
                    hi[k] = val
            ins = fn(eng)
            for sem, n in incs:
                assert ins is not None, "step with incs must return an instruction"
                ins.then_inc(sem, n)


def _consts():
    ident = np.eye(128, dtype=BF)
    identf = np.eye(128, dtype=np.float32)
    uu = np.arange(896)[None, :] - 384
    jj = np.arange(128)[:, None]
    master = np.where(jj > uu, 0.0, 3.0e38).astype(BF)
    ones3 = np.ones((3, H * 512), dtype=BF)
    ones2 = np.ones((2, 128), dtype=BF)
    p = np.arange(512)
    comb_d = (p[None, :] // 16 == np.arange(32)[:, None]).astype(np.float32)
    p2 = np.arange(128)
    comb_g = (p2[None, :] % 16 == np.arange(16)[:, None]).astype(np.float32)
    k = np.arange(128)
    Ltri = ((k[:, None] // 8 == k[None, :] // 8) &
            (k[:, None] % 8 < k[None, :] % 8)).astype(np.float32)
    return ident, identf, master, ones3, ones2, comb_d, comb_g, Ltri


def build_nc():
    nc = bass.Bass("TRN2", target_bir_lowering=False)

    x_d = nc.dram_tensor("x", [N, D], F32, kind="ExternalInput").ap()
    wq_d = nc.dram_tensor("wq", [D, D], F32, kind="ExternalInput").ap()
    wk_d = nc.dram_tensor("wk", [D, D], F32, kind="ExternalInput").ap()
    wv_d = nc.dram_tensor("wv", [D, D], F32, kind="ExternalInput").ap()
    wo_d = nc.dram_tensor("wo", [D, D], F32, kind="ExternalInput").ap()
    wf_d = nc.dram_tensor("wf", [D, H], F32, kind="ExternalInput").ap()
    g1r_d = nc.dram_tensor("g1r", [1, 32, 4, 32], F32, kind="ExternalInput").ap()
    g2r_d = nc.dram_tensor("g2r", [4, 16, 1, 16], F32, kind="ExternalInput").ap()
    g1i_d = nc.dram_tensor("g1i", [1, 32, 4, 32], F32, kind="ExternalInput").ap()
    g2i_d = nc.dram_tensor("g2i", [4, 16, 1, 16], F32, kind="ExternalInput").ap()
    br_d = nc.dram_tensor("br", [512], F32, kind="ExternalInput").ap()
    bi_d = nc.dram_tensor("bi", [512], F32, kind="ExternalInput").ap()
    y_d = nc.dram_tensor("y", [N, D], F32, kind="ExternalOutput").ap()

    knd = "ExternalOutput" if _DBG_DUMP else "Internal"
    if _DBG_DUMP:
        dbgp_d = nc.dram_tensor("dbgp", [2, 128, G * 512], BF16, kind="ExternalOutput").ap()
        dbgo_d = nc.dram_tensor("dbgo", [128, 4096], BF16, kind="ExternalOutput").ap()
        dbgw_d = nc.dram_tensor("dbgw", [2, 128, 2048], BF16, kind="ExternalOutput").ap()
    kbuf = nc.dram_tensor("kbuf", [H, HD, N], BF16, kind=knd).ap()
    qbuf = nc.dram_tensor("qbuf", [NT, HD, H, 512], BF16, kind=knd).ap()
    cqbuf = nc.dram_tensor("cqbuf", [H, 3, N], BF16, kind=knd).ap()
    ckbuf = nc.dram_tensor("ckbuf", [H, 3, N], BF16, kind=knd).ap()
    xbuf = nc.dram_tensor("xbuf", [NBLK, 128, 8 * 512], BF16, kind=knd).ap()
    fbuf = nc.dram_tensor("fbuf", [128, 512], F32, kind=knd).ap()

    ident_np, identf_np, master_np, ones3_np, ones2_np, comb_d_np, comb_g_np, Ltri_np = _consts()
    identb_d = nc.inline_tensor(ident_np, "identb").ap()
    identf_d = nc.inline_tensor(identf_np, "identf").ap()
    master_d = nc.inline_tensor(master_np, "masterc").ap()
    ones3_d = nc.inline_tensor(ones3_np, "ones3c").ap()
    ones2_d = nc.inline_tensor(ones2_np, "ones2c").ap()
    comb_d_d = nc.inline_tensor(comb_d_np, "combdc").ap()
    comb_g_d = nc.inline_tensor(comb_g_np, "combgc").ap()
    Ltri_d = nc.inline_tensor(Ltri_np, "ltric").ap()

    es = ExitStack()
    sb = lambda name, shape, dt: es.enter_context(nc.sbuf_tensor(name, shape, dt))
    psalloc = lambda name, shape, dt: es.enter_context(nc.psum_tensor(name, shape, dt))

    vst = sb("vst", [128, 32, H * 65], BF16)
    w1 = sb("w1", [128, 8 * 1024], BF16)
    w2 = sb("w2", [128, 8 * 1024], BF16)
    wfb = sb("wfb", [128, 8 * 16], BF16)
    kwin = [sb(f"kwin{i}", [128, N], BF16) for i in range(2)]
    qwin = sb("qwin", [128, H * 512], BF16)
    xres = sb("xres", [128, 4 * 1024], F32)
    xnt2 = sb("xnt2", [128, 8 * 512], BF16)
    otok = sb("otok", [128, 4 * 1024], BF16)
    ptile = [sb(f"ptile{i}", [128, G * 512], BF16) for i in range(3)]
    ybuf = [sb(f"ybuf{i}", [128, 1024], F32) for i in range(2)]
    xblk0 = sb("xblk0", [128, 4 * 1024], F32)
    xblk = [xblk0, xblk0]
    wtr = sb("wtr", [128, 4 * 512], BF16)
    wti = sb("wti", [128, 4 * 512], BF16)
    bias2r = sb("bias2r", [2, 512], BF16)
    bias2i = sb("bias2i", [2, 512], BF16)
    identb = sb("identb_sb", [128, 128], BF16)
    master = sb("master_sb", [128, 896], BF16)
    ones2 = sb("ones2_sb", [2, 128], BF16)
    epst = sb("epst", [128, 1], F32)
    onecol = sb("onecol", [128, 1], F32)
    sum_s = [sb(f"sum_s{i}", [128, 1], F32) for i in range(4)]
    ssq_s = [sb(f"ssq_s{i}", [128, 1], F32) for i in range(4)]
    m_s = [sb(f"m_s{i}", [128, 1], F32) for i in range(4)]
    vr_s = [sb(f"vr_s{i}", [128, 1], F32) for i in range(4)]
    t_s = [sb(f"t_s{i}", [128, 1], F32) for i in range(4)]
    var_s = [sb(f"var_s{i}", [128, 1], F32) for i in range(4)]
    rstd_s = [sb(f"rstd_s{i}", [128, 1], F32) for i in range(4)]
    nb_s = [sb(f"nb_s{i}", [128, 1], F32) for i in range(4)]
    lsb = sb("lsb", [128, 4], F32)
    recs = [sb(f"recs{i}", [128, 1], F32) for i in range(4)]
    offs = sb("offs", [128, 1], F32)
    totals = sb("totals", [128, 1], F32)
    _kw0_addr = nc.lookup_mloc(kwin[0]).addr
    _kw1_addr = nc.lookup_mloc(kwin[1]).addr
    xnt1s = [nc.alloc_sbuf_tensor_at("xnt1a", [128, 8 * 512], BF16, offset=_kw0_addr).ap(),
             nc.alloc_sbuf_tensor_at("xnt1b", [128, 8 * 512], BF16, offset=_kw1_addr).ap()]
    otst = ptile[2][:, 0:512]
    xnst = otok
    uae = xres[:, 0:2048]
    ube = xres[:, 2048:4096]
    prod = xnst[:].bitcast(F32)
    xnt2f = xnt2[:].bitcast(F32)
    ua_sb = xnt2f[:, 0:512]
    wtf32 = xnt2f[:, 512:1024]
    ub_sb = xnt2f[:, 1024:1088]
    at_sb = xnt2f[0:32, 1088:1216]
    bt_sb = xnt2f[0:16, 1216:1280]
    gsta = xnt2f[:, 1280:1408]
    combd = xnt2f[0:32, 1408:1920]
    combg = xnt2f[0:16, 1920:2048]
    ybf1 = ybuf[1].bitcast(F32) if False else ybuf[1]
    ltri = ybuf[1][:, 512:640]
    identf = ybuf[1][:, 640:768]
    kst4 = [ptile[i // 2][:, (i % 2) * 512:(i % 2) * 512 + 512]
            for i in range(6)]
    qwf = qwin[:].bitcast(F32)
    bspl = qwin[0:1, 0:1024]
    bf32 = qwf[0:1, 512:1024]
    btmp = qwf[0:1, 1024:1536]
    btmp2 = qwf[0:1, 1536:2048]
    xrf = xres
    f128 = xrf[:, 0:512]
    cst = [xrf[:, 512:1024], xrf[:, 1024:1536]]
    casc = xrf[:, 1536:2048]
    ctmp = xrf[:, 2048:2560]
    csp = xres[:].bitcast(BF16)[:, 5120:6656]
    csn = xres[:].bitcast(BF16)[:, 6656:8192]
    sq_s = ybuf[0][:, 0:1024]
    fst = ybuf[1][0:16, 0:512]

    psG = psalloc("psG", [128, 3072], F32)
    pacc = psalloc("pacc", [128, 264], F32)
    pex = psalloc("pex", [128, 512], F32)
    psA = psG[:, 0:1536]
    psB = psG[:, 1536:3072]
    pstA = psG[:, 2048:2304].bitcast(BF16)
    pstB = psG[:, 2560:2816].bitcast(BF16)
    pstC = psG[:, 1024:1280].bitcast(BF16)
    pstD = psG[:, 1536:1792].bitcast(BF16)
    psY = [psG[:, 0:512], psG[:, 512:1024]]
    psF = [psG[:, 1024:1536], psG[:, 1536:2048]]

    sW = es.enter_context(nc.semaphore("sW"))
    sX = es.enter_context(nc.semaphore("sX"))
    sK0 = es.enter_context(nc.semaphore("sK0"))
    sK1 = es.enter_context(nc.semaphore("sK1"))
    sQ = es.enter_context(nc.semaphore("sQ"))
    sXR = es.enter_context(nc.semaphore("sXR"))
    sXN = es.enter_context(nc.semaphore("sXN"))
    sSt = es.enter_context(nc.semaphore("sSt"))
    sY = es.enter_context(nc.semaphore("sYm"))
    sP = es.enter_context(nc.semaphore("sP"))
    sA = es.enter_context(nc.semaphore("sA"))
    sV = es.enter_context(nc.semaphore("sV"))

    P = Plan()

    def dma(out, in_, **kw):
        return lambda eng: eng.dma_start(out=out, in_=in_, **kw)

    P.step("dve", lambda v: v.memset(qwin[64:128, :], 0.0), incs=[(sV, 1)],
           label="qw_zero")
    P.step("sync", dma(identb[:], identb_d), incs=[(sW, 16)], serialize=True)
    P.step("sync", dma(identf[:], identf_d), incs=[(sW, 16)], serialize=True)
    P.step("sync", dma(master[:], master_d), incs=[(sW, 16)], serialize=True)
    P.step("sync", dma(qwin[67:70, :], ones3_d), waits=[(sV, "qw_zero")],
           incs=[(sW, 16)], serialize=True)
    P.step("sync", dma(ones2[:], ones2_d), incs=[(sW, 16)], serialize=True)
    P.step("sync", dma(combd[:], comb_d_d), incs=[(sW, 16)], serialize=True)
    P.step("sync", dma(combg[:], comb_g_d), incs=[(sW, 16)], serialize=True)
    P.step("sync", dma(ltri[:], Ltri_d), incs=[(sW, 16)], label="ld_small", serialize=True)
    wk_r = wk_d.rearrange("(c p) n -> p c n", p=128)
    wq_r = wq_d.rearrange("(c p) n -> p c n", p=128)
    wf_r = wf_d.rearrange("(c p) n -> p c n", p=128)
    P.step("gp", dma(w1.rearrange("p (c n) -> p c n", c=8), wk_r),
           incs=[(sSt, 16)], label="ld_wk", serialize=True)
    P.step("gp", dma(w2.rearrange("p (c n) -> p c n", c=8), wq_r),
           incs=[(sSt, 16)], label="ld_wq", serialize=True)
    P.step("gp", dma(wfb.rearrange("p (c n) -> p c n", c=8), wf_r),
           incs=[(sSt, 16)], label="ld_wf", serialize=True)
    P.step("sync", dma(gsta[:, 0:32], g1r_d.rearrange("a b c d -> (a b c) d")),
           incs=[(sW, 16)], label="ld_g1r", serialize=True)
    P.step("sync", dma(gsta[:, 32:64], g1i_d.rearrange("a b c d -> (a b c) d")),
           incs=[(sW, 16)], label="ld_g1i", serialize=True)
    P.step("sync", dma(gsta[0:64, 64:80], g2r_d.rearrange("c e a g -> (c e a) g")),
           incs=[(sW, 16)], label="ld_g2r", serialize=True)
    P.step("sync", dma(gsta[0:64, 80:96], g2i_d.rearrange("c e a g -> (c e a) g")),
           incs=[(sW, 16)], label="ld_g2i", serialize=True)
    P.step("sync", dma(bf32[:], br_d.rearrange("(a n) -> a n", a=1)),
           incs=[(sW, 16)], label="ld_br", serialize=True)

    P.step("dve", lambda v: v.memset(epst[:], 1e-5), waits=[(sW, "ld_small")],
           incs=[(sV, 1)], label="eps_set")
    P.step("dve", lambda v: v.memset(onecol[:], 1.0))
    P.step("dve", lambda v: v.memset(vst[:], 1.0), incs=[(sV, 1)], label="vst_ones")
    P.step("dve", lambda v: v.tensor_scalar_mul(w2[:], w2[:], 0.125),
           waits=[(sSt, "ld_wq")], incs=[(sV, 1)], label="wq_scaled")

    def build_wt(tag, g1_cols, g2_cols, wt_tile, neg_tile):
        wtr1 = [(sW, "ld_g1" + tag), (sW, "ld_small")]
        if tag == "i":
            wtr1.append((sV, "ubcr"))
        P.step("pe", (lambda t, _a=g1_cols:
                      t.transpose(pex[0:32, 0:128], gsta[:, _a[0]:_a[1]], identf[:])),
               waits=wtr1, incs=[(sP, 1)], label=f"tr_g1{tag}")
        P.step("dve", lambda v: v.tensor_copy(at_sb[:], pex[0:32, 0:128]),
               waits=[(sP, f"tr_g1{tag}")], incs=[(sV, 1)], label=f"at{tag}")
        P.step("pe", (lambda t, _a=g2_cols:
                      t.transpose(pex[0:16, 128:192], gsta[0:64, _a[0]:_a[1]], identf[0:64, 0:64])),
               waits=[(sW, "ld_g2" + tag), (sV, f"at{tag}")],
               incs=[(sP, 1)], label=f"tr_g2{tag}")
        P.step("dve", lambda v: v.tensor_copy(bt_sb[:], pex[0:16, 128:192]),
               waits=[(sP, f"tr_g2{tag}")], incs=[(sV, 1)], label=f"bt{tag}")

        def mk_ua(t):
            ins = None
            for p4 in range(4):
                ins = t.matmul(pex[:, p4 * 128:p4 * 128 + 128],
                               combd[:, p4 * 128:p4 * 128 + 128], at_sb[:],
                               start=True, stop=True)
            return ins
        P.step("pe", mk_ua, waits=[(sV, f"at{tag}"), (sV, f"bt{tag}")],
               incs=[(sP, 1)], label=f"ua{tag}")
        P.step("dve", lambda v: v.tensor_copy(ua_sb[:], pex[:, 0:512]),
               waits=[(sP, f"ua{tag}")], incs=[(sV, 1)], label=f"uac{tag}")
        P.step("pe", lambda t: t.matmul(pex[:, 0:64], combg[:], bt_sb[:],
                                        start=True, stop=True),
               waits=[(sV, f"uac{tag}")], incs=[(sP, 1)], label=f"ub{tag}")
        P.step("dve", lambda v: v.tensor_copy(ub_sb[:], pex[:, 0:64]),
               waits=[(sP, f"ub{tag}")], incs=[(sV, 1)], label=f"ubc{tag}")

        def mk_wt_chunk(v, p4, _wt=wt_tile, _neg=neg_tile):
            ua_r = ua_sb[:, p4 * 128:p4 * 128 + 128].rearrange("p (b c) -> p b c", c=4)
            uae_r = uae.rearrange("p (b c e) -> p e b c", c=4, e=16)
            for e in range(16):
                v.tensor_copy(uae_r[:, e], ua_r)
            ub_r = ub_sb[:, 0:64].rearrange("p (c e) -> p c e", c=4)
            ube_r = ube.rearrange("p (b c e) -> p b c e", c=4, e=16)
            for b8 in range(32):
                v.tensor_copy(ube_r[:, b8], ub_r)
            ins = v.tensor_mul(prod, uae, ube)
            return ins
        for p4 in range(4):
            P.step("dve", (lambda v, _p4=p4: mk_wt_chunk(v, _p4)),
                   waits=[(sV, f"ubc{tag}")] if p4 == 0 else [(sV, f"wtred{tag}{p4-1}")],
                   incs=[(sV, 1)], label=f"wtp{tag}{p4}")
            def red(v, _p4=p4, _wt=wt_tile, _neg=neg_tile):
                pr2 = prod.rearrange("p (b c e) -> p b c e", c=4, e=16)
                wv_ = wtf32.rearrange("p (b e) -> p b e", e=16)
                v.tensor_tensor(wv_, pr2[:, :, 0], pr2[:, :, 1], ALU.add)
                v.tensor_tensor(wv_, wv_, pr2[:, :, 2], ALU.add)
                v.tensor_tensor(wv_, wv_, pr2[:, :, 3], ALU.add)
                return v.tensor_copy(_wt[:, _p4 * 512:_p4 * 512 + 512], wtf32[:])
            P.step("dve", red, waits=[(sV, f"wtp{tag}{p4}")],
                   incs=[(sV, 1)], label=f"wtred{tag}{p4}")
        P.step("dve", lambda v: v.tensor_copy(btmp[:], btmp[:]),
               waits=[(sV, f"wtred{tag}3")], incs=[(sV, 1)], label=f"wt{tag}")

    build_wt("r", (0, 32), (64, 80), wtr, None)
    build_wt("i", (32, 64), (80, 96), wti, None)
    if _DBG_DUMP:
        P.step("gp", dma(dbgw_d[0], wtr[:]), waits=[(sV, "wtr")], incs=[(sSt, 16)])
        P.step("gp", dma(dbgw_d[1], wti[:]), waits=[(sV, "wti")], incs=[(sSt, 16)])

    def bias_split(v):
        v.tensor_copy(bspl[0:1, 0:512], bf32[:])
        v.tensor_copy(btmp[:], bspl[0:1, 0:512])
        v.tensor_sub(btmp2[:], bf32[:], btmp[:])
        return v.tensor_copy(bspl[0:1, 512:1024], btmp2[:])
    P.step("dve", bias_split, waits=[(sW, "ld_br")], incs=[(sV, 1)], label="bsp_r0")
    P.step("gp", dma(bias2r[0:1, :], bspl[0:1, 0:512]), waits=[(sV, "bsp_r0")],
           incs=[(sSt, 16)], serialize=True)
    P.step("gp", dma(bias2r[1:2, :], bspl[0:1, 512:1024]), incs=[(sSt, 16)], label="bsp_r", serialize=True)
    P.step("dve", lambda v: v.tensor_copy(btmp2[:], btmp2[:]),
           waits=[(sSt, "bsp_r")], incs=[(sV, 1)], label="bsplit_r")
    P.step("sync", dma(bf32[:], bi_d.rearrange("(a n) -> a n", a=1)),
           waits=[(sV, "bsplit_r")], incs=[(sW, 16)], label="ld_bi", serialize=True)
    P.step("dve", bias_split, waits=[(sW, "ld_bi")], incs=[(sV, 1)], label="bsp_i0")
    P.step("gp", dma(bias2i[0:1, :], bspl[0:1, 0:512]), waits=[(sV, "bsp_i0")],
           incs=[(sSt, 16)], serialize=True)
    P.step("gp", dma(bias2i[1:2, :], bspl[0:1, 512:1024]), incs=[(sSt, 16)], label="bsp_i", serialize=True)

    P._ph0 = len(P.steps)
    xr = x_d.rearrange("(b s p) d -> b p s d", p=128, s=4)

    def emit_front(b):
        bb = b % 2
        w = [(sV, f"aff{b-1}")] if b >= 1 else []
        P.step("sync", dma(xblk[bb].rearrange("p (s d) -> p s d", s=4), xr[b]),
               waits=w, incs=[(sX, 16)], label=f"xld{b}")

        def ln_stats(v, _bb=bb):
            ins = None
            for s in range(4):
                xsub = xblk[_bb][:, s * 1024:(s + 1) * 1024]
                v.reduce_sum(sum_s[s][:], xsub, axis=AX.X)
                v.tensor_mul(sq_s[:], xsub, xsub)
                ins = v.reduce_sum(ssq_s[s][:], sq_s[:], axis=AX.X)
            return ins
        P.step("dve", ln_stats, waits=[(sX, 16 * (b + 1))], incs=[(sV, 1)],
               label=f"lnst{b}")

        def ln_ab(v):
            ins = None
            for s in range(4):
                v.tensor_scalar_mul(m_s[s][:], sum_s[s][:], 1.0 / 1024)
                ins = v.tensor_scalar_mul(vr_s[s][:], ssq_s[s][:], 1.0 / 1024)
            return ins
        P.step("dve", ln_ab, waits=[(sV, f"lnst{b}")], incs=[(sV, 1)], label=f"lnab{b}")

        def ln_c(v):
            ins = None
            for s in range(4):
                ins = v.tensor_mul(t_s[s][:], m_s[s][:], m_s[s][:])
            return ins
        P.step("dve", ln_c, waits=[(sV, f"lnab{b}")], incs=[(sV, 1)], label=f"lnc{b}")

        def ln_d(v):
            ins = None
            for s in range(4):
                ins = v.tensor_sub(var_s[s][:], vr_s[s][:], t_s[s][:])
            return ins
        P.step("dve", ln_d, waits=[(sV, f"lnc{b}")], incs=[(sV, 1)], label=f"lnd{b}")

        def ln_ln(sc):
            ins = None
            for s in range(4):
                ins = sc.activation(rstd_s[s][:], var_s[s][:], AF.Ln, bias=epst[:, 0:1])
            return ins
        P.step("act", ln_ln, waits=[(sV, f"lnd{b}"), (sV, "eps_set")],
               incs=[(sA, 1)], label=f"lnln{b}")

        def ln_exp(sc):
            ins = None
            for s in range(4):
                ins = sc.activation(rstd_s[s][:], rstd_s[s][:], AF.Exp, scale=-0.5)
            return ins
        P.step("act", ln_exp, waits=[(sA, f"lnln{b}")], incs=[(sA, 1)], label=f"rstd{b}")

        def ln_nb(v):
            ins = None
            for s in range(4):
                ins = v.scalar_tensor_tensor(nb_s[s][:], m_s[s][:], -1.0, rstd_s[s][:],
                                             ALU.mult, ALU.mult)
            return ins
        P.step("dve", ln_nb, waits=[(sA, f"rstd{b}")], incs=[(sV, 1)], label=f"lnnb{b}")

        def ln_aff(v, _bb=bb):
            ins = None
            for s in range(4):
                ins = v.tensor_scalar(xnst[:, s * 1024:(s + 1) * 1024],
                                      xblk[_bb][:, s * 1024:(s + 1) * 1024],
                                      rstd_s[s][:, 0:1], nb_s[s][:, 0:1], ALU.mult, ALU.add)
            return ins
        waff = [(sV, f"lnnb{b}")]
        if b >= 1:
            waff.append((sP, f"trxend{b-1}"))
        P.step("dve", ln_aff, waits=waff, incs=[(sV, 1)], label=f"aff{b}")

        for s in range(4):
            for c in range(8):
                i = s * 8 + c
                pst = pstA if i % 2 == 0 else pstB
                wtt = [(sV, f"aff{b}")]
                if b > 0 or i >= 2:
                    wtt.append((sA, f"evx{b}_{i-2}" if i >= 2 else f"evx{b-1}_{30+i}"))
                P.step("pe", (lambda t, _s=s, _c=c, _p=pst:
                              t.transpose(_p[:, 0:128],
                                          xnst[:, _s * 1024 + _c * 128:_s * 1024 + _c * 128 + 128],
                                          identb[:])),
                       waits=wtt, incs=[(sP, 1)], label=f"trx{b}_{i}")
                P.step("act", (lambda sc, _s=s, _c=c, _p=pst, _x=xnt1s[b % 2]:
                               sc.copy(_x[:, _c * 512 + _s * 128:_c * 512 + _s * 128 + 128],
                                       _p[:, 0:128])),
                       waits=[(sP, f"trx{b}_{i}")], incs=[(sA, 1)], label=f"evx{b}_{i}")
        P.step("pe", lambda t: t.nop(), waits=[(sP, f"trx{b}_31")],
               incs=[(sP, 1)], label=f"trxend{b}")
        P.step("gp", dma(xbuf[b], xnt1s[b % 2][:]), waits=[(sA, f"evx{b}_31")],
               incs=[(sSt, 16)], label=f"xbst{b}", serialize=True)

        continue_marker = None
    def emit_mmblock(b):
        xnt1 = xnt1s[b % 2]
        for wi in range(2):
            wt_tile = w1 if wi == 0 else w2
            buf = kbuf if wi == 0 else qbuf
            pko = 0 if wi == 0 else 512
            for m in range(8):
                def mmk(t, _wt=wt_tile, _m=m, _o=pko):
                    ins = None
                    for c in range(8):
                        ins = t.matmul(psA[:, _o:_o + 512],
                                       _wt[:, c * 1024 + _m * 128:c * 1024 + _m * 128 + 128],
                                       xnt1[:, c * 512:c * 512 + 512],
                                       start=(c == 0), stop=(c == 7))
                    return ins
                wts = [(sA, f"evx{b}_31"), (sSt, "ld_wk" if wi == 0 else "ld_wq")]
                if wi == 1:
                    wts.append((sV, "wq_scaled"))
                if b == 0 and wi == 0 and m == 0:
                    wts.append((sV, "wtr"))
                    wts.append((sV, "wti"))
                if b > 0 or m > 0:
                    prev = f"evk{b}_{wi}_{m-1}" if m > 0 else f"evk{b-1}_{wi}_7"
                    wts.append((sA, prev))
                P.step("pe", mmk, waits=wts, incs=[(sP, 1)], label=f"mmk{b}_{wi}_{m}")
                jj = (b * 16 + wi * 8 + m)
                wev = [(sP, f"mmk{b}_{wi}_{m}")]
                if jj >= 6:
                    pj = jj - 6
                    wev.append((sSt, f"kqst{pj // 16}_{(pj % 16) // 8}_{pj % 8}"))
                P.step("act", (lambda sc, _j=jj, _o=pko:
                               sc.copy(kst4[_j % 6][:], psA[:, _o:_o + 512])),
                       waits=wev, incs=[(sA, 1)],
                       label=f"evk{b}_{wi}_{m}")
                if wi == 0:
                    P.step("gp", dma(buf[2 * m:2 * m + 2, :, b * 512:(b + 1) * 512]
                                     .rearrange("h r t -> (h r) t"), kst4[jj % 6][:]),
                           waits=[(sA, f"evk{b}_{wi}_{m}")], incs=[(sSt, 16)],
                           label=f"kqst{b}_{wi}_{m}", serialize=(m == 0))
                else:
                    P.step("gp", dma(qbuf[b, :, 2 * m, :], kst4[jj % 6][0:64, :]),
                           waits=[(sA, f"evk{b}_{wi}_{m}")], incs=[(sSt, 16)])
                    P.step("gp", dma(qbuf[b, :, 2 * m + 1, :], kst4[jj % 6][64:128, :]),
                           incs=[(sSt, 16)], label=f"kqst{b}_{wi}_{m}")

        def mmf(t):
            ins = None
            for c in range(8):
                ins = t.matmul(pex[0:16, 0:512], wfb[:, c * 16:c * 16 + 16],
                               xnt1[:, c * 512:c * 512 + 512], start=(c == 0), stop=(c == 7))
            return ins
        wtsf = [(sA, f"evx{b}_31"), (sSt, "ld_wf")]
        if b == 0:
            wtsf += [(sV, "wtr"), (sV, "wti")]
        else:
            wtsf.append((sA, f"evf{b-1}"))
        P.step("pe", mmf, waits=wtsf, incs=[(sP, 1)], label=f"mmf{b}")
        wevf = [(sP, f"mmf{b}")]
        if b > 0:
            wevf.append((sSt, f"fst{b-1}"))
        P.step("act", lambda sc: sc.copy(fst[:], pex[0:16, 0:512]),
               waits=wevf, incs=[(sA, 1)], label=f"evf{b}")
        P.step("gp", dma(fbuf.rearrange("(h s) t -> h s t", s=8)[:, b, :], fst[:]),
               waits=[(sA, f"evf{b}")], incs=[(sSt, 16)], label=f"fst{b}")

    for b in range(NBLK):
        emit_front(b)
        if b >= 1:
            emit_mmblock(b - 1)
    emit_mmblock(NBLK - 1)

    def kw_init(v, kb):
        v.memset(kwin[kb][64:128, :], 0.0)
        return v.memset(kwin[kb][64:67, :], 1.0)
    P.step("dve", lambda v: kw_init(v, 0),
           waits=[(sP, f"mmf{NBLK-2}"), (sSt, f"xbst{NBLK-2}")],
           incs=[(sV, 1)], label="kw0_ones")
    P.step("dve", lambda v: kw_init(v, 1),
           waits=[(sP, f"mmf{NBLK-1}"), (sSt, f"xbst{NBLK-1}")],
           incs=[(sV, 1)], label="kw1_ones")

    P._ph1 = len(P.steps)
    wv_r = wv_d.rearrange("(c p) n -> p c n", p=128)
    wo_r = wo_d.rearrange("(c p) n -> p c n", p=128)
    P.step("gp", dma(w1.rearrange("p (c n) -> p c n", c=8), wv_r),
           waits=[(sP, f"mmk{NBLK-1}_0_7")], incs=[(sSt, 16)], label="ld_wv", serialize=True)
    P.step("gp", dma(w2.rearrange("p (c n) -> p c n", c=8), wo_r),
           waits=[(sP, f"mmk{NBLK-1}_1_7")], incs=[(sSt, 16)], label="ld_wo", serialize=True)

    for b in range(NBLK):
        wts = [(sSt, f"xbst{b}")]
        if b > 0:
            wts.append((sP, f"mmv{b-1}_3_1"))
        P.step("sync", dma(xnt2[:], xbuf[b]), waits=wts, incs=[(sX, 16)], label=f"xrld{b}")
        for s in range(4):
            for n in range(2):
                def mmv(t, _s=s, _n=n):
                    ins = None
                    for c in range(8):
                        ins = t.matmul(psG[:, 1024 + _n * 512:1024 + _n * 512 + 512],
                                       xnt2[:, c * 512 + _s * 128:c * 512 + _s * 128 + 128],
                                       w1[:, c * 1024 + _n * 512:c * 1024 + _n * 512 + 512],
                                       start=(c == 0), stop=(c == 7))
                    return ins
                wts = [(sX, 16 * (NBLK + b + 1)), (sSt, "ld_wv")]
                if b == 0 and s == 0 and n == 0:
                    wts.append((sA, f"evx{NBLK-1}_31"))
                if b > 0 or s > 0 or n > 0:
                    pv = (f"evv{b}_{s}_{n-1}" if n > 0 else
                          (f"evv{b}_{s-1}_1" if s > 0 else f"evv{b-1}_3_1"))
                    wts.append((sA, pv))
                P.step("pe", mmv, waits=wts, incs=[(sP, 1)], label=f"mmv{b}_{s}_{n}")
                kap = 4 * b + s
                P.step("act", (lambda sc, _k=kap, _n=n:
                               sc.copy(vst.rearrange("p k (h c) -> p k h c", c=65)
                                       [:, _k, 8 * _n:8 * _n + 8, 0:64],
                                       (psG[:, 1024 + _n * 512:1024 + _n * 512 + 512]
                                        ).rearrange("p (h c) -> p h c", c=64))),
                       waits=[(sP, f"mmv{b}_{s}_{n}"), (sV, "vst_ones")],
                       incs=[(sA, 1)], label=f"evv{b}_{s}_{n}")

    P._ph2 = len(P.steps)
    P.step("sync", dma(f128[:], fbuf), waits=[(sSt, f"fst{NBLK-1}")],
           incs=[(sX, 16)], label="fld")
    P.step("act", lambda sc: sc.activation(casc[:], f128[:], AF.Abs),
           waits=[(sX, 16 * (2 * NBLK + 1))], incs=[(sA, 1)], label="c_abs")
    P.step("act", lambda sc: sc.activation(casc[:], casc[:], AF.Exp, scale=-1.0),
           waits=[(sA, "c_abs")], incs=[(sA, 1)], label="c_exp")
    P.step("act", lambda sc: sc.activation(casc[:], casc[:], AF.Ln, bias=onecol[:, 0:1]),
           waits=[(sA, "c_exp")], incs=[(sA, 1)], label="c_ln")
    P.step("dve", lambda v: v.tensor_scalar_min(ctmp[:], f128[:], 0.0),
           waits=[(sA, "c_ln")], incs=[(sV, 1)], label="c_min")
    P.step("dve", lambda v: v.tensor_sub(cst[0][:], ctmp[:], casc[:]),
           waits=[(sV, "c_min")], incs=[(sV, 1)], label="c_lf")
    src_i = 0
    for st_i, sh in enumerate([1, 2, 4, 8, 16, 32, 64, 128, 256]):
        dst = 1 - src_i
        def csum(v, _s=src_i, _d=dst, _sh=sh):
            v.tensor_copy(cst[_d][:, 0:_sh], cst[_s][:, 0:_sh])
            return v.tensor_add(cst[_d][:, _sh:512], cst[_s][:, _sh:512],
                                cst[_s][:, 0:512 - _sh])
        P.step("dve", csum, waits=[(sV, "c_lf" if st_i == 0 else f"cs{st_i-1}")],
               incs=[(sV, 1)], label=f"cs{st_i}")
        src_i = dst
    cfin = cst[src_i]
    P.step("dve", lambda v: v.tensor_copy(totals[:], cfin[:, 511:512]),
           waits=[(sV, "cs8")], incs=[(sV, 1)], label="c_tot")
    P.step("pe", lambda t: t.matmul(pex[:, 0:1], ltri[:], totals[:], start=True, stop=True),
           waits=[(sV, "c_tot"), (sA, f"evf{NBLK-1}")], incs=[(sP, 1)], label="c_off")
    P.step("dve", lambda v: v.tensor_copy(offs[:], pex[:, 0:1]),
           waits=[(sP, "c_off")], incs=[(sV, 1)], label="c_offc")
    P.step("dve", lambda v: v.tensor_scalar_add(ctmp[:], cfin[:], offs[:, 0:1]),
           waits=[(sV, "c_offc")], incs=[(sV, 1)], label="c_full")
    def csplit1(v):
        v.tensor_copy(csp[:, 0:512], ctmp[:])
        return v.tensor_copy(casc[:], csp[:, 0:512])
    P.step("dve", csplit1, waits=[(sV, "c_full")], incs=[(sV, 1)], label="csp1")
    def csplit2(v):
        v.tensor_sub(f128[:], ctmp[:], casc[:])
        return v.tensor_copy(csp[:, 512:1024], f128[:])
    P.step("dve", csplit2, waits=[(sV, "csp1")], incs=[(sV, 1)], label="csp2")
    def csplit3(v):
        v.tensor_copy(casc[:], csp[:, 512:1024])
        return v.tensor_sub(f128[:], f128[:], casc[:])
    P.step("dve", csplit3, waits=[(sV, "csp2")], incs=[(sV, 1)], label="csp3")
    P.step("dve", lambda v: v.tensor_copy(csp[:, 1024:1536], f128[:]),
           waits=[(sV, "csp3")], incs=[(sV, 1)], label="csp4")
    P.step("dve", lambda v: v.tensor_scalar_mul(csn[:], csp[:], -1.0),
           waits=[(sV, "csp4")], incs=[(sV, 1)], label="csp5")
    for hh in range(H):
        for sp in range(3):
            P.step("gp", dma(cqbuf[hh, sp, :].rearrange("(s t) -> s t", s=8),
                             csp[8 * hh:8 * hh + 8, sp * 512:sp * 512 + 512]),
                   waits=[(sV, "csp5")] if (hh == 0 and sp == 0) else [],
                   incs=[(sSt, 16)], label=f"cq{hh}_{sp}")
            P.step("gp", dma(ckbuf[hh, sp, :].rearrange("(s t) -> s t", s=8),
                             csn[8 * hh:8 * hh + 8, sp * 512:sp * 512 + 512]),
                   incs=[(sSt, 16)], label=f"ck{hh}_{sp}")

    P._ph3 = len(P.steps)
    xr2 = x_d.rearrange("(t s p) d -> t p s d", p=128, s=4)
    yr = y_d.rearrange("(t s p) d -> t s p d", p=128, s=4)

    gg_ctr = [0]
    nrm_ctr = [0]
    msk_ctr = [0]
    sg_last = {}
    pend_nrm = [None]
    pend_av = []
    pend_wo = []
    def flush_av(keep=0):
        while len(pend_av) > keep:
            _fn, _w, _lab = pend_av.pop(0)
            P.step("pe", _fn, waits=_w, incs=[(sP, 1)], label=_lab)

    set_off = [0]
    for t in range(NT):
        w = 512 * (t + 1)
        nch = 4 * (t + 1)
        tile_gg0 = [gg_ctr[0]]
        if t > 0:
            set_off[0] = (2 - gg_ctr[0]) % 3
        def emit_qld(_t):
            wq_ = ([(sSt, f"ck{H-1}_2")] if _t == 0
                   else [(sP, sg_last[(_t - 1, H - 1)])])
            P.step("gp", dma(qwin[0:64, :], qbuf[_t].rearrange("r h t -> r (h t)")),
                   waits=wq_, incs=[(sQ, 16)], label=f"qld{_t}")
            P.step("gp", dma(qwin[64:67, :].rearrange("i (h t) -> i h t", h=H),
                             cqbuf[:, :, 512 * _t:512 * _t + 512]
                             .rearrange("h i t -> i h t")),
                   incs=[(sQ, 16)], label=f"qldc{_t}")
        if t == 0:
            emit_qld(0)
        wx = [(sSt, f"ck{H-1}_2")] if t == 0 else [(sV, f"comb{t-1}_3_1")]
        P.step("gp", dma(xres.rearrange("p (s d) -> p s d", s=4), xr2[t]),
               waits=wx, incs=[(sXR, 16)], label=f"xres{t}")
        wxn = [(sP, f"mmv{NBLK-1}_3_1")] if t == 0 else [(sP, f"ff{t-1}_3B")]
        P.step("gp", dma(xnt2[:], xbuf[t]), waits=wxn, incs=[(sXN, 16)], label=f"xnld{t}")

        for hh in range(H):
            kb = hh % 2
            wk_ = [(sV, "kw0_ones" if kb == 0 else "kw1_ones"), (sSt, f"ck{H-1}_2")]
            if t * H + hh >= 2:
                if hh >= 2:
                    wk_.append((sP, sg_last[(t, hh - 2)]))
                else:
                    wk_.append((sP, sg_last[(t - 1, H - 2 + hh)]))
            sKp = sK0 if kb == 0 else sK1
            P.step("sync", dma(kwin[kb][0:64, 0:w], kbuf[hh, :, 0:w]),
                   waits=wk_, incs=[(sKp, 16)])
            P.step("sync", dma(kwin[kb][67:70, 0:w], ckbuf[hh, :, 0:w]),
                   incs=[(sKp, 16)], label=f"kld{t}_{hh}")

            ngr = (nch + G - 1) // G
            for g in range(ngr):
                k0, k1 = G * g, min(G * g + G, nch)
                gg = gg_ctr[0]
                gset = (gg + set_off[0]) % 3
                pgrp = psG[:, 1024 * gset:1024 * gset + 1024]
                ptl = ptile[gset]

                def mms(ten, _k0=k0, _k1=k1, _p=pgrp, _hh=hh):
                    ins = None
                    for kk in range(_k0, _k1):
                        ins = ten.matmul(_p[:, 512 * (kk - _k0):512 * (kk - _k0) + 512],
                                         kwin[_hh % 2][:, 128 * kk:128 * kk + 128],
                                         qwin[:, 512 * _hh:512 * _hh + 512],
                                         start=True, stop=True)
                    return ins
                wts = []
                if g == 0:
                    wts += [(sK0 if hh % 2 == 0 else sK1, f"kld{t}_{hh}"),
                            (sQ, f"qldc{t}")]
                if gg >= 3:
                    wts.append((sA, f"exp{gg-3}"))
                if gg == 0:
                    wts.append((sA, f"evk{NBLK-1}_1_7"))
                    wts.append((sA, f"evf{NBLK-1}"))
                if gg == 1:
                    wts.append((sA, f"evv{NBLK-1}_3_1"))
                if gg == 2:
                    wts.append((sA, f"evx{NBLK-1}_31"))
                if t > 0 and (gg - tile_gg0[0]) < 3 and True:
                    wts.append((sV, f"comb{t-1}_3_1"))
                P.step("pe", mms, waits=wts, incs=[(sP, 1)], label=f"sg{gg}")
                flush_av(keep=2)
                if g == 0:
                    flush_av(keep=0)
                    if pend_nrm[0] is not None:
                        _enrm, _avlab = pend_nrm[0]
                        pend_nrm[0] = None
                        _enrm(_avlab=_avlab)
                if g == ngr - 1:
                    sg_last[(t, hh)] = f"sg{gg}"

                def expg(sc, _p=pgrp, _pt=ptl, _n=512 * (k1 - k0)):
                    return sc.activation(_pt[:, 0:_n], _p[:, 0:_n], AF.Exp)
                we = [(sP, f"sg{gg}")]
                if gg >= 3:
                    we.append((sP, f"avg{gg-3}"))
                P.step("act", expg, waits=we, incs=[(sA, 1)], label=f"exp{gg}")

                mlabs = []
                for kk in range(k0, k1):
                    m = kk - 4 * t
                    if m >= 0:
                        msk_ctr[0] += 1
                        def msk(v, _pt=ptl, _off=512 * (kk - k0), _m=m):
                            return v.tensor_tensor(
                                _pt[:, _off:_off + 512], _pt[:, _off:_off + 512],
                                master[:, 384 - 128 * _m:896 - 128 * _m], ALU.min)
                        P.step("dve", msk, waits=[(sA, f"exp{gg}")],
                               incs=[(sV, 1)], label=f"msk{msk_ctr[0]}")
                        mlabs.append(f"msk{msk_ctr[0]}")

                def av(ten, _k0=k0, _k1=k1, _pt=ptl, _hh=hh, _nch=nch):
                    pa = pacc if _hh % 2 == 0 else pex[:, 0:264]
                    ins = None
                    for kk in range(_k0, _k1):
                        for s4 in range(4):
                            ins = ten.matmul(
                                pa[:, 66 * s4:66 * s4 + 65],
                                _pt[:, 512 * (kk - _k0) + 128 * s4:
                                    512 * (kk - _k0) + 128 * s4 + 128],
                                vst[:, kk, 65 * _hh:65 * _hh + 65],
                                start=(kk == 0 and s4 == 0),
                                stop=(kk == _nch - 1 and s4 == 3),
                                skip_group_check=True)
                    return ins
                wav = [(sA, f"exp{gg}")] + [(sV, ml) for ml in mlabs]
                if g == 0:
                    if nrm_ctr[0] >= 1 + (1 if hh >= 2 else 0):
                        pass
                    if hh >= 2:
                        wav.append((sV, f"nrm{(t * H + hh) - 2 + 1}"))
                    elif t > 0:
                        if hh == 0:
                            wav.append((sV, f"nrm{(t - 1) * H + H - 2 + 1}"))
                        else:
                            wav.append((sV, f"comb{t-1}_3_1"))
                    elif t == 0 and hh == 1:
                        wav.append((sV, "c_offc"))
                pend_av.append((av, wav, f"avg{gg}"))
                gg_ctr[0] += 1

            def emit_nrm(_t=t, _hh=hh, _avlab=None):
                pa = pacc if _hh % 2 == 0 else pex[:, 0:264]
                P.step("dve", (lambda v, _pa=pa: v.tensor_copy(
                           lsb[:], _pa.rearrange("p (s c) -> p s c", c=66)[:, :, 64])),
                       waits=[(sP, _avlab)], incs=[(sV, 1)],
                       label=f"nrmA{_t}_{_hh}")
                def nrm_b(v):
                    ins = None
                    for s4 in range(4):
                        ins = v.reciprocal(recs[s4][:], lsb[:, s4:s4 + 1])
                    return ins
                P.step("dve", nrm_b, waits=[(sV, f"nrmA{_t}_{_hh}")], incs=[(sV, 1)],
                       label=f"nrmB{_t}_{_hh}")
                def nrm_c(v):
                    ins = None
                    for s4 in range(4):
                        ins = v.tensor_scalar_mul(
                            otok[:, 1024 * s4 + 64 * _hh:1024 * s4 + 64 * _hh + 64],
                            pa[:, 66 * s4:66 * s4 + 64], recs[s4][:, 0:1])
                    return ins
                wn = [(sV, f"nrmB{_t}_{_hh}")]
                if _t == 0 and _hh == 0:
                    wn.append((sP, f"trxend{NBLK-1}"))
                if _t > 0 and _hh == 0:
                    wn.append((sP, f"odone{_t-1}"))
                nrm_ctr[0] += 1
                P.step("dve", nrm_c, waits=wn, incs=[(sV, 1)], label=f"nrm{nrm_ctr[0]}")
            pend_nrm[0] = (emit_nrm, f"avg{gg_ctr[0]-1}")

        if _DBG_DUMP and t == 0:
            P.step("gp", dma(dbgo_d, otok[:]), waits=[(sV, f"nrm{nrm_ctr[0]}")],
                   incs=[(sSt, 16)])
        flush_av(keep=0)
        if pend_nrm[0] is not None:
            _enrm, _avlab = pend_nrm[0]
            pend_nrm[0] = None
            _enrm(_avlab=_avlab)
        nrm_all = f"nrm{nrm_ctr[0]}"
        if t + 1 < NT:
            emit_qld(t + 1)
        pst4 = [psG[:, 2048 + 256 * j:2048 + 256 * j + 256].bitcast(BF16)
                for j in range(4)]
        pstall = psG[:, 2048:3072].bitcast(BF16).rearrange("p (b k) -> p b k",
                                                           k=512)
        for s4 in range(4):
            for h in range(2):
                rid = s4 * 2 + h
                for j in range(4):
                    c = 4 * h + j
                    wtt = [(sV, nrm_all)]
                    if rid >= 1:
                        wtt.append((sV, f"oev{t}_{s4}_{h-1}" if h == 1
                                    else f"oev{t}_{s4-1}_1"))
                    P.step("pe", (lambda ten, _s=s4, _c=c, _p=pst4[j]:
                                  ten.transpose(_p[:, 0:128],
                                                otok[:, 1024 * _s + 128 * _c:
                                                     1024 * _s + 128 * _c + 128],
                                                identb[:])),
                           waits=wtt, incs=[(sP, 1)], label=f"otr{t}_{s4}_{h}_{j}")
                if pend_wo:
                    _f, _w, _l = pend_wo.pop(0)
                    P.step("pe", _f, waits=_w, incs=[(sP, 1)], label=_l)
                woev = [(sP, f"otr{t}_{s4}_{h}_3")]
                if rid >= 1:
                    woev.append((sP, f"wo{t}_{s4}_{h-1}" if h == 1
                                 else f"wo{t}_{s4-1}_1"))
                P.step("dve", (lambda v:
                               v.tensor_copy(otst.rearrange("p (s k) -> p s k",
                                                            k=128),
                                             pstall[:, :, 0:128])),
                       waits=woev, incs=[(sV, 1)], label=f"oev{t}_{s4}_{h}")
                def mwo(ten, _h=h):
                    ins = None
                    for j2 in range(4):
                        cc = 4 * _h + j2
                        for n in range(2):
                            ins = ten.matmul(psY[n],
                                             otst[:, 128 * j2:128 * j2 + 128],
                                             w2[:, 1024 * cc + 512 * n:
                                                1024 * cc + 512 * n + 512],
                                             start=(_h == 0 and j2 == 0),
                                             stop=(_h == 1 and j2 == 3),
                                             skip_group_check=True)
                    return ins
                wwo = [(sV, f"oev{t}_{s4}_{h}"), (sSt, "ld_wo")]
                if h == 0:
                    if s4 > 0:
                        wwo.append((sV, f"comb{t}_{s4-1}_1"))
                    elif t > 0:
                        wwo.append((sV, f"comb{t-1}_3_1"))
                pend_wo.append((mwo, wwo, f"wo{t}_{s4}_{h}"))
            while pend_wo:
                _f, _w, _l = pend_wo.pop(0)
                P.step("pe", _f, waits=_w, incs=[(sP, 1)], label=_l)
            if s4 == 3:
                P.step("pe", lambda ten: ten.nop(), waits=[(sP, f"otr{t}_3_1_3")],
                       incs=[(sP, 1)], label=f"odone{t}")

            def ffa(ten, _s=s4):
                ins = None
                for c in range(4):
                    ten.matmul(psF[0],
                               xnt2[:, c * 512 + _s * 128:c * 512 + _s * 128 + 128],
                               wtr[:, c * 512:c * 512 + 512],
                               start=(c == 0), stop=False, skip_group_check=True)
                ten.matmul(psF[0], ones2[:], bias2r[:], start=False, stop=True,
                           skip_group_check=True)
                for c in range(4):
                    ins = ten.matmul(
                        pex[:, 0:512],
                        xnt2[:, (c + 4) * 512 + _s * 128:(c + 4) * 512 + _s * 128 + 128],
                        wti[:, c * 512:c * 512 + 512],
                        start=(c == 0), stop=(c == 3), skip_group_check=True)
                return ins
            def ffb(ten, _s=s4):
                for c in range(8):
                    wt_t = wti if c < 4 else wtr
                    ten.matmul(psF[1],
                               xnt2[:, c * 512 + _s * 128:c * 512 + _s * 128 + 128],
                               wt_t[:, (c % 4) * 512:(c % 4) * 512 + 512],
                               start=(c == 0), stop=False, skip_group_check=True)
                return ten.matmul(psF[1], ones2[:], bias2i[:], start=False,
                                  stop=True, skip_group_check=True)
            wff = [(sV, "wtr"), (sV, "wti"), (sXN, f"xnld{t}"), (sSt, "bsp_i"),
                   (sV, nrm_all)]
            if s4 > 0:
                wff.append((sV, f"comb{t}_{s4-1}_1"))
            elif t > 0:
                wff.append((sV, f"comb{t-1}_3_1"))
            else:
                wff.append((sV, "c_offc"))
            P.step("pe", ffa, waits=wff, incs=[(sP, 1)], label=f"ff{t}_{s4}A")
            P.step("pe", ffb, waits=[], incs=[(sP, 1)], label=f"ff{t}_{s4}B")

            for n in range(2):
                def comb(v, _s=s4, _n=n):
                    yb = ybuf[_s % 2]
                    v.scalar_tensor_tensor(yb[:, 512 * _n:512 * _n + 512],
                                           psY[_n], 1.0,
                                           xres[:, 1024 * _s + 512 * _n:
                                                1024 * _s + 512 * _n + 512],
                                           ALU.mult, ALU.add)
                    if _n == 0:
                        v.scalar_tensor_tensor(yb[:, 0:512], pex[:, 0:512], -1.0,
                                               yb[:, 0:512], ALU.mult, ALU.add)
                    return v.scalar_tensor_tensor(yb[:, 512 * _n:512 * _n + 512],
                                                  psF[_n], 1.0,
                                                  yb[:, 512 * _n:512 * _n + 512],
                                                  ALU.mult, ALU.add)
                wc = [(sP, f"wo{t}_{s4}_1"), (sXR, f"xres{t}"),
                      (sP, f"ff{t}_{s4}A" if n == 0 else f"ff{t}_{s4}B")]
                idx = 4 * t + s4
                if idx >= 2:
                    wc.append((sY, 16 * (idx - 1)))
                P.step("dve", comb, waits=wc, incs=[(sV, 1)], label=f"comb{t}_{s4}_{n}")
            P.step("gp", dma(yr[t, s4], ybuf[s4 % 2][:]),
                   waits=[(sV, f"comb{t}_{s4}_1")], incs=[(sY, 16)],
                   label=f"yst{t}_{s4}")

    P.step("gp", lambda g_: g_.nop(), waits=[(sY, 16 * 4 * NT)])

    if _DBG_PHASE < 4:
        cut = [P._ph0, P._ph1, P._ph2, P._ph3][_DBG_PHASE]
        P.steps = P.steps[:cut]
    if _DBG_NSTEPS is not None:
        P.steps = P.steps[:_DBG_NSTEPS]
    P.resolve()

    with nc.Block() as block:
        @block.sync
        def _(sync):
            P.emit("sync", sync)

        @block.vector
        def _(vector):
            P.emit("dve", vector)

        @block.scalar
        def _(scalar):
            P.emit("act", scalar)

        @block.tensor
        def _(tensor):
            P.emit("pe", tensor)

        @block.gpsimd
        def _(gpsimd):
            P.emit("gp", gpsimd)

    es.close()
    return nc


def kernel(L0_attn_x, L0_attn_Wq, L0_attn_Wk, L0_attn_Wv, L0_attn_Wo, L0_attn_Wf,
           L0_ffn_G1_R, L0_ffn_G2_R, L0_ffn_G1_I, L0_ffn_G2_I,
           L0_ffn_bias_real, L0_ffn_bias_imag):
    x = np.ascontiguousarray(np.asarray(L0_attn_x, dtype=np.float32))
    com = {
        "wq": np.ascontiguousarray(np.asarray(L0_attn_Wq, dtype=np.float32)),
        "wk": np.ascontiguousarray(np.asarray(L0_attn_Wk, dtype=np.float32)),
        "wv": np.ascontiguousarray(np.asarray(L0_attn_Wv, dtype=np.float32)),
        "wo": np.ascontiguousarray(np.asarray(L0_attn_Wo, dtype=np.float32)),
        "wf": np.ascontiguousarray(np.asarray(L0_attn_Wf, dtype=np.float32)),
        "g1r": np.ascontiguousarray(np.asarray(L0_ffn_G1_R, dtype=np.float32)),
        "g2r": np.ascontiguousarray(np.asarray(L0_ffn_G2_R, dtype=np.float32)),
        "g1i": np.ascontiguousarray(np.asarray(L0_ffn_G1_I, dtype=np.float32)),
        "g2i": np.ascontiguousarray(np.asarray(L0_ffn_G2_I, dtype=np.float32)),
        "br": np.ascontiguousarray(np.asarray(L0_ffn_bias_real, dtype=np.float32)),
        "bi": np.ascontiguousarray(np.asarray(L0_ffn_bias_imag, dtype=np.float32)),
    }
    nc = build_nc()
    in_maps = [dict(com, x=np.ascontiguousarray(x[b])) for b in range(8)]
    res = run_bass_kernel_spmd(nc, in_maps, core_ids=list(range(8)))
    return np.stack([res.results[b]["y"] for b in range(8)], axis=0)


# version 27
# speedup vs baseline: 1.2491x; 1.0018x over previous
import numpy as np
import ml_dtypes
from contextlib import ExitStack

import concourse.bass as bass
import concourse.mybir as mybir
from concourse.bass_utils import run_bass_kernel_spmd

F32 = mybir.dt.float32
BF16 = mybir.dt.bfloat16
AF = mybir.ActivationFunctionType
ALU = mybir.AluOpType
AX = mybir.AxisListType
BF = ml_dtypes.bfloat16

_DBG_PHASE = 4
_DBG_DUMP = False
_DBG_NSTEPS = None

N = 4096
D = 1024
H = 16
HD = 64
NBLK = 8
NT = 8
G = 2


class Plan:

    def __init__(self):
        self.steps = []
        self.labels = {}

    def step(self, eng, fn, waits=(), incs=(), label=None, serialize=False):
        self.steps.append([eng, fn, list(waits), list(incs), label, serialize])

    def resolve(self):
        cnt = {}
        for st in self.steps:
            _eng, _fn, waits, incs, label, serialize = st
            rw = []
            if serialize and incs:
                sem0 = incs[0][0]
                rw.append((sem0, cnt.get(id(sem0), 0)))
            for sem, lab in waits:
                if isinstance(lab, int):
                    rw.append((sem, lab))
                else:
                    s2, v = self.labels[lab]
                    assert s2 is sem, f"label {lab} bound to a different sem"
                    rw.append((sem, v))
            st[2] = rw
            for sem, n in incs:
                cnt[id(sem)] = cnt.get(id(sem), 0) + n
            if label is not None:
                assert len(incs) == 1, f"labeled step {label} must have exactly one inc"
                self.labels[label] = (incs[0][0], cnt[id(incs[0][0])])

    def emit(self, eng_name, eng):
        hi = {}
        for eng2, fn, waits, incs, _label, _ser in self.steps:
            if eng2 != eng_name:
                continue
            merged = {}
            for sem, val in waits:
                k = id(sem)
                if val > merged.get(k, (None, -1))[1]:
                    merged[k] = (sem, val)
            for k, (sem, val) in merged.items():
                if val > hi.get(k, -1):
                    eng.wait_ge(sem, val)
                    hi[k] = val
            ins = fn(eng)
            for sem, n in incs:
                assert ins is not None, "step with incs must return an instruction"
                ins.then_inc(sem, n)


def _consts():
    ident = np.eye(128, dtype=BF)
    identf = np.eye(128, dtype=np.float32)
    uu = np.arange(896)[None, :] - 384
    jj = np.arange(128)[:, None]
    master = np.where(jj > uu, 0.0, 3.0e38).astype(BF)
    ones3 = np.ones((3, H * 512), dtype=BF)
    ones2 = np.ones((2, 128), dtype=BF)
    p = np.arange(512)
    comb_d = (p[None, :] // 16 == np.arange(32)[:, None]).astype(np.float32)
    p2 = np.arange(128)
    comb_g = (p2[None, :] % 16 == np.arange(16)[:, None]).astype(np.float32)
    k = np.arange(128)
    Ltri = ((k[:, None] // 8 == k[None, :] // 8) &
            (k[:, None] % 8 < k[None, :] % 8)).astype(np.float32)
    return ident, identf, master, ones3, ones2, comb_d, comb_g, Ltri


def build_nc():
    nc = bass.Bass("TRN2", target_bir_lowering=False)

    x_d = nc.dram_tensor("x", [N, D], F32, kind="ExternalInput").ap()
    wq_d = nc.dram_tensor("wq", [D, D], F32, kind="ExternalInput").ap()
    wk_d = nc.dram_tensor("wk", [D, D], F32, kind="ExternalInput").ap()
    wv_d = nc.dram_tensor("wv", [D, D], F32, kind="ExternalInput").ap()
    wo_d = nc.dram_tensor("wo", [D, D], F32, kind="ExternalInput").ap()
    wf_d = nc.dram_tensor("wf", [D, H], F32, kind="ExternalInput").ap()
    g1r_d = nc.dram_tensor("g1r", [1, 32, 4, 32], F32, kind="ExternalInput").ap()
    g2r_d = nc.dram_tensor("g2r", [4, 16, 1, 16], F32, kind="ExternalInput").ap()
    g1i_d = nc.dram_tensor("g1i", [1, 32, 4, 32], F32, kind="ExternalInput").ap()
    g2i_d = nc.dram_tensor("g2i", [4, 16, 1, 16], F32, kind="ExternalInput").ap()
    br_d = nc.dram_tensor("br", [512], F32, kind="ExternalInput").ap()
    bi_d = nc.dram_tensor("bi", [512], F32, kind="ExternalInput").ap()
    y_d = nc.dram_tensor("y", [N, D], F32, kind="ExternalOutput").ap()

    knd = "ExternalOutput" if _DBG_DUMP else "Internal"
    if _DBG_DUMP:
        dbgp_d = nc.dram_tensor("dbgp", [2, 128, G * 512], BF16, kind="ExternalOutput").ap()
        dbgo_d = nc.dram_tensor("dbgo", [128, 4096], BF16, kind="ExternalOutput").ap()
        dbgw_d = nc.dram_tensor("dbgw", [2, 128, 2048], BF16, kind="ExternalOutput").ap()
    kbuf = nc.dram_tensor("kbuf", [H, HD, N], BF16, kind=knd).ap()
    qbuf = nc.dram_tensor("qbuf", [NT, HD, H, 512], BF16, kind=knd).ap()
    cqbuf = nc.dram_tensor("cqbuf", [H, 3, N], BF16, kind=knd).ap()
    ckbuf = nc.dram_tensor("ckbuf", [H, 3, N], BF16, kind=knd).ap()
    xbuf = nc.dram_tensor("xbuf", [NBLK, 128, 8 * 512], BF16, kind=knd).ap()
    fbuf = nc.dram_tensor("fbuf", [128, 512], F32, kind=knd).ap()

    ident_np, identf_np, master_np, ones3_np, ones2_np, comb_d_np, comb_g_np, Ltri_np = _consts()
    identb_d = nc.inline_tensor(ident_np, "identb").ap()
    identf_d = nc.inline_tensor(identf_np, "identf").ap()
    master_d = nc.inline_tensor(master_np, "masterc").ap()
    ones3_d = nc.inline_tensor(ones3_np, "ones3c").ap()
    ones2_d = nc.inline_tensor(ones2_np, "ones2c").ap()
    comb_d_d = nc.inline_tensor(comb_d_np, "combdc").ap()
    comb_g_d = nc.inline_tensor(comb_g_np, "combgc").ap()
    Ltri_d = nc.inline_tensor(Ltri_np, "ltric").ap()

    es = ExitStack()
    sb = lambda name, shape, dt: es.enter_context(nc.sbuf_tensor(name, shape, dt))
    psalloc = lambda name, shape, dt: es.enter_context(nc.psum_tensor(name, shape, dt))

    vst = sb("vst", [128, 32, H * 65], BF16)
    w1 = sb("w1", [128, 8 * 1024], BF16)
    w2 = sb("w2", [128, 8 * 1024], BF16)
    wfb = sb("wfb", [128, 8 * 16], BF16)
    kwin = [sb(f"kwin{i}", [128, N], BF16) for i in range(2)]
    qwin = sb("qwin", [128, H * 512], BF16)
    xres = sb("xres", [128, 4 * 1024], F32)
    xnt2 = sb("xnt2", [128, 8 * 512], BF16)
    otok = sb("otok", [128, 4 * 1024], BF16)
    ptile = [sb(f"ptile{i}", [128, G * 512], BF16) for i in range(3)]
    ybuf = [sb(f"ybuf{i}", [128, 1024], F32) for i in range(2)]
    xblk0 = sb("xblk0", [128, 4 * 1024], F32)
    xblk = [xblk0, xblk0]
    wtr = sb("wtr", [128, 4 * 512], BF16)
    wti = sb("wti", [128, 4 * 512], BF16)
    bias2r = sb("bias2r", [2, 512], BF16)
    bias2i = sb("bias2i", [2, 512], BF16)
    identb = sb("identb_sb", [128, 128], BF16)
    master = sb("master_sb", [128, 896], BF16)
    ones2 = sb("ones2_sb", [2, 128], BF16)
    epst = sb("epst", [128, 1], F32)
    onecol = sb("onecol", [128, 1], F32)
    sum_s = [sb(f"sum_s{i}", [128, 1], F32) for i in range(4)]
    ssq_s = [sb(f"ssq_s{i}", [128, 1], F32) for i in range(4)]
    m_s = [sb(f"m_s{i}", [128, 1], F32) for i in range(4)]
    vr_s = [sb(f"vr_s{i}", [128, 1], F32) for i in range(4)]
    t_s = [sb(f"t_s{i}", [128, 1], F32) for i in range(4)]
    var_s = [sb(f"var_s{i}", [128, 1], F32) for i in range(4)]
    rstd_s = [sb(f"rstd_s{i}", [128, 1], F32) for i in range(4)]
    nb_s = [sb(f"nb_s{i}", [128, 1], F32) for i in range(4)]
    lsb = sb("lsb", [128, 4], F32)
    recs = [sb(f"recs{i}", [128, 1], F32) for i in range(4)]
    offs = sb("offs", [128, 1], F32)
    totals = sb("totals", [128, 1], F32)
    _kw0_addr = nc.lookup_mloc(kwin[0]).addr
    _kw1_addr = nc.lookup_mloc(kwin[1]).addr
    xnt1s = [nc.alloc_sbuf_tensor_at("xnt1a", [128, 8 * 512], BF16, offset=_kw0_addr).ap(),
             nc.alloc_sbuf_tensor_at("xnt1b", [128, 8 * 512], BF16, offset=_kw1_addr).ap()]
    otst = ptile[2][:, 0:512]
    xnst = otok
    uae = xres[:, 0:2048]
    ube = xres[:, 2048:4096]
    prod = xnst[:].bitcast(F32)
    xnt2f = xnt2[:].bitcast(F32)
    ua_sb = xnt2f[:, 0:512]
    wtf32 = xnt2f[:, 512:1024]
    ub_sb = xnt2f[:, 1024:1088]
    at_sb = xnt2f[0:32, 1088:1216]
    bt_sb = xnt2f[0:16, 1216:1280]
    gsta = xnt2f[:, 1280:1408]
    combd = xnt2f[0:32, 1408:1920]
    combg = xnt2f[0:16, 1920:2048]
    ybf1 = ybuf[1].bitcast(F32) if False else ybuf[1]
    ltri = ybuf[1][:, 512:640]
    identf = ybuf[1][:, 640:768]
    kst4 = [ptile[i // 2][:, (i % 2) * 512:(i % 2) * 512 + 512]
            for i in range(6)]
    qwf = qwin[:].bitcast(F32)
    bspl = qwin[0:1, 0:1024]
    bf32 = qwf[0:1, 512:1024]
    btmp = qwf[0:1, 1024:1536]
    btmp2 = qwf[0:1, 1536:2048]
    xrf = xres
    f128 = xrf[:, 0:512]
    cst = [xrf[:, 512:1024], xrf[:, 1024:1536]]
    casc = xrf[:, 1536:2048]
    ctmp = xrf[:, 2048:2560]
    csp = xres[:].bitcast(BF16)[:, 5120:6656]
    csn = xres[:].bitcast(BF16)[:, 6656:8192]
    sq_s = ybuf[0][:, 0:1024]
    fst = ybuf[1][0:16, 0:512]

    psG = psalloc("psG", [128, 3072], F32)
    pacc = psalloc("pacc", [128, 264], F32)
    pex = psalloc("pex", [128, 512], F32)
    psA = psG[:, 0:1536]
    psB = psG[:, 1536:3072]
    pstA = psG[:, 2048:2304].bitcast(BF16)
    pstB = psG[:, 2560:2816].bitcast(BF16)
    pstC = psG[:, 1024:1280].bitcast(BF16)
    pstD = psG[:, 1536:1792].bitcast(BF16)
    psY = [psG[:, 0:512], psG[:, 512:1024]]
    psF = [psG[:, 1024:1536], psG[:, 1536:2048]]

    sW = es.enter_context(nc.semaphore("sW"))
    sX = es.enter_context(nc.semaphore("sX"))
    sK0 = es.enter_context(nc.semaphore("sK0"))
    sK1 = es.enter_context(nc.semaphore("sK1"))
    sQ = es.enter_context(nc.semaphore("sQ"))
    sXR = es.enter_context(nc.semaphore("sXR"))
    sXN = es.enter_context(nc.semaphore("sXN"))
    sXB = es.enter_context(nc.semaphore("sXB"))
    sSt = es.enter_context(nc.semaphore("sSt"))
    sY = es.enter_context(nc.semaphore("sYm"))
    sP = es.enter_context(nc.semaphore("sP"))
    sA = es.enter_context(nc.semaphore("sA"))
    sV = es.enter_context(nc.semaphore("sV"))

    P = Plan()

    def dma(out, in_, **kw):
        return lambda eng: eng.dma_start(out=out, in_=in_, **kw)

    P.step("dve", lambda v: v.memset(qwin[64:128, :], 0.0), incs=[(sV, 1)],
           label="qw_zero")
    P.step("sync", dma(identb[:], identb_d), incs=[(sW, 16)], serialize=True)
    P.step("sync", dma(identf[:], identf_d), incs=[(sW, 16)], serialize=True)
    P.step("sync", dma(master[:], master_d), incs=[(sW, 16)], serialize=True)
    P.step("sync", dma(qwin[67:70, :], ones3_d), waits=[(sV, "qw_zero")],
           incs=[(sW, 16)], serialize=True)
    P.step("sync", dma(ones2[:], ones2_d), incs=[(sW, 16)], serialize=True)
    P.step("sync", dma(combd[:], comb_d_d), incs=[(sW, 16)], serialize=True)
    P.step("sync", dma(combg[:], comb_g_d), incs=[(sW, 16)], serialize=True)
    P.step("sync", dma(ltri[:], Ltri_d), incs=[(sW, 16)], label="ld_small", serialize=True)
    wk_r = wk_d.rearrange("(c p) n -> p c n", p=128)
    wq_r = wq_d.rearrange("(c p) n -> p c n", p=128)
    wf_r = wf_d.rearrange("(c p) n -> p c n", p=128)
    P.step("gp", dma(w1.rearrange("p (c n) -> p c n", c=8), wk_r),
           incs=[(sSt, 16)], label="ld_wk", serialize=True)
    P.step("gp", dma(w2.rearrange("p (c n) -> p c n", c=8), wq_r),
           incs=[(sSt, 16)], label="ld_wq", serialize=True)
    P.step("gp", dma(wfb.rearrange("p (c n) -> p c n", c=8), wf_r),
           incs=[(sSt, 16)], label="ld_wf", serialize=True)
    P.step("sync", dma(gsta[:, 0:32], g1r_d.rearrange("a b c d -> (a b c) d")),
           incs=[(sW, 16)], label="ld_g1r", serialize=True)
    P.step("sync", dma(gsta[:, 32:64], g1i_d.rearrange("a b c d -> (a b c) d")),
           incs=[(sW, 16)], label="ld_g1i", serialize=True)
    P.step("sync", dma(gsta[0:64, 64:80], g2r_d.rearrange("c e a g -> (c e a) g")),
           incs=[(sW, 16)], label="ld_g2r", serialize=True)
    P.step("sync", dma(gsta[0:64, 80:96], g2i_d.rearrange("c e a g -> (c e a) g")),
           incs=[(sW, 16)], label="ld_g2i", serialize=True)
    P.step("sync", dma(bf32[:], br_d.rearrange("(a n) -> a n", a=1)),
           incs=[(sW, 16)], label="ld_br", serialize=True)

    P.step("dve", lambda v: v.memset(epst[:], 1e-5), waits=[(sW, "ld_small")],
           incs=[(sV, 1)], label="eps_set")
    P.step("dve", lambda v: v.memset(onecol[:], 1.0))
    P.step("dve", lambda v: v.memset(vst[:], 1.0), incs=[(sV, 1)], label="vst_ones")
    P.step("dve", lambda v: v.tensor_scalar_mul(w2[:], w2[:], 0.125),
           waits=[(sSt, "ld_wq")], incs=[(sV, 1)], label="wq_scaled")

    def build_wt(tag, g1_cols, g2_cols, wt_tile, neg_tile):
        wtr1 = [(sW, "ld_g1" + tag), (sW, "ld_small")]
        if tag == "i":
            wtr1.append((sV, "ubcr"))
        P.step("pe", (lambda t, _a=g1_cols:
                      t.transpose(pex[0:32, 0:128], gsta[:, _a[0]:_a[1]], identf[:])),
               waits=wtr1, incs=[(sP, 1)], label=f"tr_g1{tag}")
        P.step("dve", lambda v: v.tensor_copy(at_sb[:], pex[0:32, 0:128]),
               waits=[(sP, f"tr_g1{tag}")], incs=[(sV, 1)], label=f"at{tag}")
        P.step("pe", (lambda t, _a=g2_cols:
                      t.transpose(pex[0:16, 128:192], gsta[0:64, _a[0]:_a[1]], identf[0:64, 0:64])),
               waits=[(sW, "ld_g2" + tag), (sV, f"at{tag}")],
               incs=[(sP, 1)], label=f"tr_g2{tag}")
        P.step("dve", lambda v: v.tensor_copy(bt_sb[:], pex[0:16, 128:192]),
               waits=[(sP, f"tr_g2{tag}")], incs=[(sV, 1)], label=f"bt{tag}")

        def mk_ua(t):
            ins = None
            for p4 in range(4):
                ins = t.matmul(pex[:, p4 * 128:p4 * 128 + 128],
                               combd[:, p4 * 128:p4 * 128 + 128], at_sb[:],
                               start=True, stop=True)
            return ins
        P.step("pe", mk_ua, waits=[(sV, f"at{tag}"), (sV, f"bt{tag}")],
               incs=[(sP, 1)], label=f"ua{tag}")
        P.step("dve", lambda v: v.tensor_copy(ua_sb[:], pex[:, 0:512]),
               waits=[(sP, f"ua{tag}")], incs=[(sV, 1)], label=f"uac{tag}")
        P.step("pe", lambda t: t.matmul(pex[:, 0:64], combg[:], bt_sb[:],
                                        start=True, stop=True),
               waits=[(sV, f"uac{tag}")], incs=[(sP, 1)], label=f"ub{tag}")
        P.step("dve", lambda v: v.tensor_copy(ub_sb[:], pex[:, 0:64]),
               waits=[(sP, f"ub{tag}")], incs=[(sV, 1)], label=f"ubc{tag}")

        def mk_wt_chunk(v, p4, _wt=wt_tile, _neg=neg_tile):
            ua_r = ua_sb[:, p4 * 128:p4 * 128 + 128].rearrange("p (b c) -> p b c", c=4)
            uae_r = uae.rearrange("p (b c e) -> p e b c", c=4, e=16)
            for e in range(16):
                v.tensor_copy(uae_r[:, e], ua_r)
            ub_r = ub_sb[:, 0:64].rearrange("p (c e) -> p c e", c=4)
            ube_r = ube.rearrange("p (b c e) -> p b c e", c=4, e=16)
            for b8 in range(32):
                v.tensor_copy(ube_r[:, b8], ub_r)
            ins = v.tensor_mul(prod, uae, ube)
            return ins
        for p4 in range(4):
            P.step("dve", (lambda v, _p4=p4: mk_wt_chunk(v, _p4)),
                   waits=[(sV, f"ubc{tag}")] if p4 == 0 else [(sV, f"wtred{tag}{p4-1}")],
                   incs=[(sV, 1)], label=f"wtp{tag}{p4}")
            def red(v, _p4=p4, _wt=wt_tile, _neg=neg_tile):
                pr2 = prod.rearrange("p (b c e) -> p b c e", c=4, e=16)
                wv_ = wtf32.rearrange("p (b e) -> p b e", e=16)
                v.tensor_tensor(wv_, pr2[:, :, 0], pr2[:, :, 1], ALU.add)
                v.tensor_tensor(wv_, wv_, pr2[:, :, 2], ALU.add)
                v.tensor_tensor(wv_, wv_, pr2[:, :, 3], ALU.add)
                return v.tensor_copy(_wt[:, _p4 * 512:_p4 * 512 + 512], wtf32[:])
            P.step("dve", red, waits=[(sV, f"wtp{tag}{p4}")],
                   incs=[(sV, 1)], label=f"wtred{tag}{p4}")
        P.step("dve", lambda v: v.tensor_copy(btmp[:], btmp[:]),
               waits=[(sV, f"wtred{tag}3")], incs=[(sV, 1)], label=f"wt{tag}")

    build_wt("r", (0, 32), (64, 80), wtr, None)
    build_wt("i", (32, 64), (80, 96), wti, None)
    if _DBG_DUMP:
        P.step("gp", dma(dbgw_d[0], wtr[:]), waits=[(sV, "wtr")], incs=[(sSt, 16)])
        P.step("gp", dma(dbgw_d[1], wti[:]), waits=[(sV, "wti")], incs=[(sSt, 16)])

    def bias_split(v):
        v.tensor_copy(bspl[0:1, 0:512], bf32[:])
        v.tensor_copy(btmp[:], bspl[0:1, 0:512])
        v.tensor_sub(btmp2[:], bf32[:], btmp[:])
        return v.tensor_copy(bspl[0:1, 512:1024], btmp2[:])
    P.step("dve", bias_split, waits=[(sW, "ld_br")], incs=[(sV, 1)], label="bsp_r0")
    P.step("gp", dma(bias2r[0:1, :], bspl[0:1, 0:512]), waits=[(sV, "bsp_r0")],
           incs=[(sSt, 16)], serialize=True)
    P.step("gp", dma(bias2r[1:2, :], bspl[0:1, 512:1024]), incs=[(sSt, 16)], label="bsp_r", serialize=True)
    P.step("dve", lambda v: v.tensor_copy(btmp2[:], btmp2[:]),
           waits=[(sSt, "bsp_r")], incs=[(sV, 1)], label="bsplit_r")
    P.step("sync", dma(bf32[:], bi_d.rearrange("(a n) -> a n", a=1)),
           waits=[(sV, "bsplit_r")], incs=[(sW, 16)], label="ld_bi", serialize=True)
    P.step("dve", bias_split, waits=[(sW, "ld_bi")], incs=[(sV, 1)], label="bsp_i0")
    P.step("gp", dma(bias2i[0:1, :], bspl[0:1, 0:512]), waits=[(sV, "bsp_i0")],
           incs=[(sSt, 16)], serialize=True)
    P.step("gp", dma(bias2i[1:2, :], bspl[0:1, 512:1024]), incs=[(sSt, 16)], label="bsp_i", serialize=True)

    P._ph0 = len(P.steps)
    xr = x_d.rearrange("(b s p) d -> b p s d", p=128, s=4)

    pend_xbst = []
    def emit_front(b):
        bb = b % 2
        w = [(sV, f"aff{b-1}")] if b >= 1 else []
        P.step("sync", dma(xblk[bb].rearrange("p (s d) -> p s d", s=4), xr[b]),
               waits=w, incs=[(sX, 16)], label=f"xld{b}")
        while pend_xbst:
            pend_xbst.pop(0)()

        def ln_stats(v, _bb=bb):
            ins = None
            for s in range(4):
                xsub = xblk[_bb][:, s * 1024:(s + 1) * 1024]
                v.reduce_sum(sum_s[s][:], xsub, axis=AX.X)
                v.tensor_mul(sq_s[:], xsub, xsub)
                ins = v.reduce_sum(ssq_s[s][:], sq_s[:], axis=AX.X)
            return ins
        P.step("dve", ln_stats, waits=[(sX, 16 * (b + 1))], incs=[(sV, 1)],
               label=f"lnst{b}")

        def ln_ab(v):
            ins = None
            for s in range(4):
                v.tensor_scalar_mul(m_s[s][:], sum_s[s][:], 1.0 / 1024)
                ins = v.tensor_scalar_mul(vr_s[s][:], ssq_s[s][:], 1.0 / 1024)
            return ins
        P.step("dve", ln_ab, waits=[(sV, f"lnst{b}")], incs=[(sV, 1)], label=f"lnab{b}")

        def ln_c(v):
            ins = None
            for s in range(4):
                ins = v.tensor_mul(t_s[s][:], m_s[s][:], m_s[s][:])
            return ins
        P.step("dve", ln_c, waits=[(sV, f"lnab{b}")], incs=[(sV, 1)], label=f"lnc{b}")

        def ln_d(v):
            ins = None
            for s in range(4):
                ins = v.tensor_sub(var_s[s][:], vr_s[s][:], t_s[s][:])
            return ins
        P.step("dve", ln_d, waits=[(sV, f"lnc{b}")], incs=[(sV, 1)], label=f"lnd{b}")

        def ln_ln(sc):
            ins = None
            for s in range(4):
                ins = sc.activation(rstd_s[s][:], var_s[s][:], AF.Ln, bias=epst[:, 0:1])
            return ins
        P.step("act", ln_ln, waits=[(sV, f"lnd{b}"), (sV, "eps_set")],
               incs=[(sA, 1)], label=f"lnln{b}")

        def ln_exp(sc):
            ins = None
            for s in range(4):
                ins = sc.activation(rstd_s[s][:], rstd_s[s][:], AF.Exp, scale=-0.5)
            return ins
        P.step("act", ln_exp, waits=[(sA, f"lnln{b}")], incs=[(sA, 1)], label=f"rstd{b}")

        def ln_nb(v):
            ins = None
            for s in range(4):
                ins = v.scalar_tensor_tensor(nb_s[s][:], m_s[s][:], -1.0, rstd_s[s][:],
                                             ALU.mult, ALU.mult)
            return ins
        P.step("dve", ln_nb, waits=[(sA, f"rstd{b}")], incs=[(sV, 1)], label=f"lnnb{b}")

        def ln_aff(v, _bb=bb):
            ins = None
            for s in range(4):
                ins = v.tensor_scalar(xnst[:, s * 1024:(s + 1) * 1024],
                                      xblk[_bb][:, s * 1024:(s + 1) * 1024],
                                      rstd_s[s][:, 0:1], nb_s[s][:, 0:1], ALU.mult, ALU.add)
            return ins
        waff = [(sV, f"lnnb{b}")]
        if b >= 1:
            waff.append((sP, f"trxend{b-1}"))
        P.step("dve", ln_aff, waits=waff, incs=[(sV, 1)], label=f"aff{b}")

        for s in range(4):
            for c in range(8):
                i = s * 8 + c
                pst = pstA if i % 2 == 0 else pstB
                wtt = [(sV, f"aff{b}")]
                if b > 0 or i >= 2:
                    wtt.append((sA, f"evx{b}_{i-2}" if i >= 2 else f"evx{b-1}_{30+i}"))
                P.step("pe", (lambda t, _s=s, _c=c, _p=pst:
                              t.transpose(_p[:, 0:128],
                                          xnst[:, _s * 1024 + _c * 128:_s * 1024 + _c * 128 + 128],
                                          identb[:])),
                       waits=wtt, incs=[(sP, 1)], label=f"trx{b}_{i}")
                P.step("act", (lambda sc, _s=s, _c=c, _p=pst, _x=xnt1s[b % 2]:
                               sc.copy(_x[:, _c * 512 + _s * 128:_c * 512 + _s * 128 + 128],
                                       _p[:, 0:128])),
                       waits=[(sP, f"trx{b}_{i}")], incs=[(sA, 1)], label=f"evx{b}_{i}")
        P.step("pe", lambda t: t.nop(), waits=[(sP, f"trx{b}_31")],
               incs=[(sP, 1)], label=f"trxend{b}")
        pend_xbst.append(lambda _b=b, _bb=bb: P.step(
            "sync", dma(xbuf[_b], xnt1s[_b % 2][:]), waits=[(sA, f"evx{_b}_31")],
            incs=[(sXB, 16)], label=f"xbst{_b}"))

        continue_marker = None
    def emit_mmblock(b):
        xnt1 = xnt1s[b % 2]
        for wi in range(2):
            wt_tile = w1 if wi == 0 else w2
            buf = kbuf if wi == 0 else qbuf
            pko = 0 if wi == 0 else 512
            for m in range(8):
                def mmk(t, _wt=wt_tile, _m=m, _o=pko):
                    ins = None
                    for c in range(8):
                        ins = t.matmul(psA[:, _o:_o + 512],
                                       _wt[:, c * 1024 + _m * 128:c * 1024 + _m * 128 + 128],
                                       xnt1[:, c * 512:c * 512 + 512],
                                       start=(c == 0), stop=(c == 7))
                    return ins
                wts = [(sA, f"evx{b}_31"), (sSt, "ld_wk" if wi == 0 else "ld_wq")]
                if wi == 1:
                    wts.append((sV, "wq_scaled"))
                if b == 0 and wi == 0 and m == 0:
                    wts.append((sV, "wtr"))
                    wts.append((sV, "wti"))
                if b > 0 or m > 0:
                    prev = f"evk{b}_{wi}_{m-1}" if m > 0 else f"evk{b-1}_{wi}_7"
                    wts.append((sA, prev))
                P.step("pe", mmk, waits=wts, incs=[(sP, 1)], label=f"mmk{b}_{wi}_{m}")
                jj = (b * 16 + wi * 8 + m)
                wev = [(sP, f"mmk{b}_{wi}_{m}")]
                if jj >= 6:
                    pj = jj - 6
                    wev.append((sSt, f"kqst{pj // 16}_{(pj % 16) // 8}_{pj % 8}"))
                P.step("act", (lambda sc, _j=jj, _o=pko:
                               sc.copy(kst4[_j % 6][:], psA[:, _o:_o + 512])),
                       waits=wev, incs=[(sA, 1)],
                       label=f"evk{b}_{wi}_{m}")
                if wi == 0:
                    P.step("gp", dma(buf[2 * m:2 * m + 2, :, b * 512:(b + 1) * 512]
                                     .rearrange("h r t -> (h r) t"), kst4[jj % 6][:]),
                           waits=[(sA, f"evk{b}_{wi}_{m}")], incs=[(sSt, 16)],
                           label=f"kqst{b}_{wi}_{m}", serialize=(m == 0))
                else:
                    P.step("gp", dma(qbuf[b, :, 2 * m, :], kst4[jj % 6][0:64, :]),
                           waits=[(sA, f"evk{b}_{wi}_{m}")], incs=[(sSt, 16)])
                    P.step("gp", dma(qbuf[b, :, 2 * m + 1, :], kst4[jj % 6][64:128, :]),
                           incs=[(sSt, 16)], label=f"kqst{b}_{wi}_{m}")

        def mmf(t):
            ins = None
            for c in range(8):
                ins = t.matmul(pex[0:16, 0:512], wfb[:, c * 16:c * 16 + 16],
                               xnt1[:, c * 512:c * 512 + 512], start=(c == 0), stop=(c == 7))
            return ins
        wtsf = [(sA, f"evx{b}_31"), (sSt, "ld_wf")]
        if b == 0:
            wtsf += [(sV, "wtr"), (sV, "wti")]
        else:
            wtsf.append((sA, f"evf{b-1}"))
        P.step("pe", mmf, waits=wtsf, incs=[(sP, 1)], label=f"mmf{b}")
        wevf = [(sP, f"mmf{b}")]
        if b > 0:
            wevf.append((sSt, f"fst{b-1}"))
        P.step("act", lambda sc: sc.copy(fst[:], pex[0:16, 0:512]),
               waits=wevf, incs=[(sA, 1)], label=f"evf{b}")
        P.step("gp", dma(fbuf.rearrange("(h s) t -> h s t", s=8)[:, b, :], fst[:]),
               waits=[(sA, f"evf{b}")], incs=[(sSt, 16)], label=f"fst{b}")

    for b in range(NBLK):
        emit_front(b)
        if b >= 1:
            emit_mmblock(b - 1)
    emit_mmblock(NBLK - 1)
    while pend_xbst:
        pend_xbst.pop(0)()

    def kw_init(v, kb):
        v.memset(kwin[kb][64:128, :], 0.0)
        return v.memset(kwin[kb][64:67, :], 1.0)
    P.step("dve", lambda v: kw_init(v, 0),
           waits=[(sP, f"mmf{NBLK-2}"), (sXB, f"xbst{NBLK-2}")],
           incs=[(sV, 1)], label="kw0_ones")
    P.step("dve", lambda v: kw_init(v, 1),
           waits=[(sP, f"mmf{NBLK-1}"), (sXB, f"xbst{NBLK-1}")],
           incs=[(sV, 1)], label="kw1_ones")

    P._ph1 = len(P.steps)
    wv_r = wv_d.rearrange("(c p) n -> p c n", p=128)
    wo_r = wo_d.rearrange("(c p) n -> p c n", p=128)
    P.step("gp", dma(w1.rearrange("p (c n) -> p c n", c=8), wv_r),
           waits=[(sP, f"mmk{NBLK-1}_0_7")], incs=[(sSt, 16)], label="ld_wv", serialize=True)
    P.step("gp", dma(w2.rearrange("p (c n) -> p c n", c=8), wo_r),
           waits=[(sP, f"mmk{NBLK-1}_1_7")], incs=[(sSt, 16)], label="ld_wo", serialize=True)

    for b in range(NBLK):
        wts = [(sXB, f"xbst{b}")]
        if b > 0:
            wts.append((sP, f"mmv{b-1}_3_1"))
        P.step("sync", dma(xnt2[:], xbuf[b]), waits=wts, incs=[(sX, 16)], label=f"xrld{b}")
        for s in range(4):
            for n in range(2):
                def mmv(t, _s=s, _n=n):
                    ins = None
                    for c in range(8):
                        ins = t.matmul(psG[:, 1024 + _n * 512:1024 + _n * 512 + 512],
                                       xnt2[:, c * 512 + _s * 128:c * 512 + _s * 128 + 128],
                                       w1[:, c * 1024 + _n * 512:c * 1024 + _n * 512 + 512],
                                       start=(c == 0), stop=(c == 7))
                    return ins
                wts = [(sX, 16 * (NBLK + b + 1)), (sSt, "ld_wv")]
                if b == 0 and s == 0 and n == 0:
                    wts.append((sA, f"evx{NBLK-1}_31"))
                if b > 0 or s > 0 or n > 0:
                    pv = (f"evv{b}_{s}_{n-1}" if n > 0 else
                          (f"evv{b}_{s-1}_1" if s > 0 else f"evv{b-1}_3_1"))
                    wts.append((sA, pv))
                P.step("pe", mmv, waits=wts, incs=[(sP, 1)], label=f"mmv{b}_{s}_{n}")
                kap = 4 * b + s
                P.step("act", (lambda sc, _k=kap, _n=n:
                               sc.copy(vst.rearrange("p k (h c) -> p k h c", c=65)
                                       [:, _k, 8 * _n:8 * _n + 8, 0:64],
                                       (psG[:, 1024 + _n * 512:1024 + _n * 512 + 512]
                                        ).rearrange("p (h c) -> p h c", c=64))),
                       waits=[(sP, f"mmv{b}_{s}_{n}"), (sV, "vst_ones")],
                       incs=[(sA, 1)], label=f"evv{b}_{s}_{n}")

    P._ph2 = len(P.steps)
    P.step("sync", dma(f128[:], fbuf), waits=[(sSt, f"fst{NBLK-1}")],
           incs=[(sX, 16)], label="fld")
    P.step("act", lambda sc: sc.activation(casc[:], f128[:], AF.Abs),
           waits=[(sX, 16 * (2 * NBLK + 1))], incs=[(sA, 1)], label="c_abs")
    P.step("act", lambda sc: sc.activation(casc[:], casc[:], AF.Exp, scale=-1.0),
           waits=[(sA, "c_abs")], incs=[(sA, 1)], label="c_exp")
    P.step("act", lambda sc: sc.activation(casc[:], casc[:], AF.Ln, bias=onecol[:, 0:1]),
           waits=[(sA, "c_exp")], incs=[(sA, 1)], label="c_ln")
    P.step("dve", lambda v: v.tensor_scalar_min(ctmp[:], f128[:], 0.0),
           waits=[(sA, "c_ln")], incs=[(sV, 1)], label="c_min")
    P.step("dve", lambda v: v.tensor_sub(cst[0][:], ctmp[:], casc[:]),
           waits=[(sV, "c_min")], incs=[(sV, 1)], label="c_lf")
    src_i = 0
    for st_i, sh in enumerate([1, 2, 4, 8, 16, 32, 64, 128, 256]):
        dst = 1 - src_i
        def csum(v, _s=src_i, _d=dst, _sh=sh):
            v.tensor_copy(cst[_d][:, 0:_sh], cst[_s][:, 0:_sh])
            return v.tensor_add(cst[_d][:, _sh:512], cst[_s][:, _sh:512],
                                cst[_s][:, 0:512 - _sh])
        P.step("dve", csum, waits=[(sV, "c_lf" if st_i == 0 else f"cs{st_i-1}")],
               incs=[(sV, 1)], label=f"cs{st_i}")
        src_i = dst
    cfin = cst[src_i]
    P.step("dve", lambda v: v.tensor_copy(totals[:], cfin[:, 511:512]),
           waits=[(sV, "cs8")], incs=[(sV, 1)], label="c_tot")
    P.step("pe", lambda t: t.matmul(pex[:, 0:1], ltri[:], totals[:], start=True, stop=True),
           waits=[(sV, "c_tot"), (sA, f"evf{NBLK-1}")], incs=[(sP, 1)], label="c_off")
    P.step("dve", lambda v: v.tensor_copy(offs[:], pex[:, 0:1]),
           waits=[(sP, "c_off")], incs=[(sV, 1)], label="c_offc")
    P.step("dve", lambda v: v.tensor_scalar_add(ctmp[:], cfin[:], offs[:, 0:1]),
           waits=[(sV, "c_offc")], incs=[(sV, 1)], label="c_full")
    def csplit1(v):
        v.tensor_copy(csp[:, 0:512], ctmp[:])
        return v.tensor_copy(casc[:], csp[:, 0:512])
    P.step("dve", csplit1, waits=[(sV, "c_full")], incs=[(sV, 1)], label="csp1")
    def csplit2(v):
        v.tensor_sub(f128[:], ctmp[:], casc[:])
        return v.tensor_copy(csp[:, 512:1024], f128[:])
    P.step("dve", csplit2, waits=[(sV, "csp1")], incs=[(sV, 1)], label="csp2")
    def csplit3(v):
        v.tensor_copy(casc[:], csp[:, 512:1024])
        return v.tensor_sub(f128[:], f128[:], casc[:])
    P.step("dve", csplit3, waits=[(sV, "csp2")], incs=[(sV, 1)], label="csp3")
    P.step("dve", lambda v: v.tensor_copy(csp[:, 1024:1536], f128[:]),
           waits=[(sV, "csp3")], incs=[(sV, 1)], label="csp4")
    P.step("dve", lambda v: v.tensor_scalar_mul(csn[:], csp[:], -1.0),
           waits=[(sV, "csp4")], incs=[(sV, 1)], label="csp5")
    for hh in range(H):
        for sp in range(3):
            P.step("gp", dma(cqbuf[hh, sp, :].rearrange("(s t) -> s t", s=8),
                             csp[8 * hh:8 * hh + 8, sp * 512:sp * 512 + 512]),
                   waits=[(sV, "csp5")] if (hh == 0 and sp == 0) else [],
                   incs=[(sSt, 16)], label=f"cq{hh}_{sp}")
            P.step("gp", dma(ckbuf[hh, sp, :].rearrange("(s t) -> s t", s=8),
                             csn[8 * hh:8 * hh + 8, sp * 512:sp * 512 + 512]),
                   incs=[(sSt, 16)], label=f"ck{hh}_{sp}")

    P._ph3 = len(P.steps)
    xr2 = x_d.rearrange("(t s p) d -> t p s d", p=128, s=4)
    yr = y_d.rearrange("(t s p) d -> t s p d", p=128, s=4)

    gg_ctr = [0]
    nrm_ctr = [0]
    msk_ctr = [0]
    sg_last = {}
    pend_nrm = [None]
    pend_av = []
    pend_wo = []
    def flush_av(keep=0):
        while len(pend_av) > keep:
            _fn, _w, _lab = pend_av.pop(0)
            P.step("pe", _fn, waits=_w, incs=[(sP, 1)], label=_lab)

    set_off = [0]
    for t in range(NT):
        w = 512 * (t + 1)
        nch = 4 * (t + 1)
        tile_gg0 = [gg_ctr[0]]
        if t > 0:
            set_off[0] = (2 - gg_ctr[0]) % 3
        def emit_qld(_t):
            wq_ = ([(sSt, f"ck{H-1}_2")] if _t == 0
                   else [(sP, sg_last[(_t - 1, H - 1)])])
            P.step("gp", dma(qwin[0:64, :], qbuf[_t].rearrange("r h t -> r (h t)")),
                   waits=wq_, incs=[(sQ, 16)], label=f"qld{_t}")
            P.step("gp", dma(qwin[64:67, :].rearrange("i (h t) -> i h t", h=H),
                             cqbuf[:, :, 512 * _t:512 * _t + 512]
                             .rearrange("h i t -> i h t")),
                   incs=[(sQ, 16)], label=f"qldc{_t}")
        if t == 0:
            emit_qld(0)
        wx = [(sSt, f"ck{H-1}_2")] if t == 0 else [(sV, f"comb{t-1}_3_1")]
        P.step("gp", dma(xres.rearrange("p (s d) -> p s d", s=4), xr2[t]),
               waits=wx, incs=[(sXR, 16)], label=f"xres{t}")
        wxn = [(sP, f"mmv{NBLK-1}_3_1")] if t == 0 else [(sP, f"ff{t-1}_3B")]
        P.step("gp", dma(xnt2[:], xbuf[t]), waits=wxn, incs=[(sXN, 16)], label=f"xnld{t}")

        for hh in range(H):
            kb = hh % 2
            wk_ = [(sV, "kw0_ones" if kb == 0 else "kw1_ones"), (sSt, f"ck{H-1}_2")]
            if t * H + hh >= 2:
                if hh >= 2:
                    wk_.append((sP, sg_last[(t, hh - 2)]))
                else:
                    wk_.append((sP, sg_last[(t - 1, H - 2 + hh)]))
            sKp = sK0 if kb == 0 else sK1
            P.step("sync", dma(kwin[kb][0:64, 0:w], kbuf[hh, :, 0:w]),
                   waits=wk_, incs=[(sKp, 16)])
            P.step("sync", dma(kwin[kb][67:70, 0:w], ckbuf[hh, :, 0:w]),
                   incs=[(sKp, 16)], label=f"kld{t}_{hh}")

            ngr = (nch + G - 1) // G
            for g in range(ngr):
                k0, k1 = G * g, min(G * g + G, nch)
                gg = gg_ctr[0]
                gset = (gg + set_off[0]) % 3
                pgrp = psG[:, 1024 * gset:1024 * gset + 1024]
                ptl = ptile[gset]

                def mms(ten, _k0=k0, _k1=k1, _p=pgrp, _hh=hh):
                    ins = None
                    for kk in range(_k0, _k1):
                        ins = ten.matmul(_p[:, 512 * (kk - _k0):512 * (kk - _k0) + 512],
                                         kwin[_hh % 2][:, 128 * kk:128 * kk + 128],
                                         qwin[:, 512 * _hh:512 * _hh + 512],
                                         start=True, stop=True)
                    return ins
                wts = []
                if g == 0:
                    wts += [(sK0 if hh % 2 == 0 else sK1, f"kld{t}_{hh}"),
                            (sQ, f"qldc{t}")]
                if gg >= 3:
                    wts.append((sA, f"exp{gg-3}"))
                if gg == 0:
                    wts.append((sA, f"evk{NBLK-1}_1_7"))
                    wts.append((sA, f"evf{NBLK-1}"))
                if gg == 1:
                    wts.append((sA, f"evv{NBLK-1}_3_1"))
                if gg == 2:
                    wts.append((sA, f"evx{NBLK-1}_31"))
                if t > 0 and (gg - tile_gg0[0]) < 3 and True:
                    wts.append((sV, f"comb{t-1}_3_1"))
                P.step("pe", mms, waits=wts, incs=[(sP, 1)], label=f"sg{gg}")
                flush_av(keep=2)
                if g == 0:
                    flush_av(keep=0)
                    if pend_nrm[0] is not None:
                        _enrm, _avlab = pend_nrm[0]
                        pend_nrm[0] = None
                        _enrm(_avlab=_avlab)
                if g == ngr - 1:
                    sg_last[(t, hh)] = f"sg{gg}"

                def expg(sc, _p=pgrp, _pt=ptl, _n=512 * (k1 - k0)):
                    return sc.activation(_pt[:, 0:_n], _p[:, 0:_n], AF.Exp)
                we = [(sP, f"sg{gg}")]
                if gg >= 3:
                    we.append((sP, f"avg{gg-3}"))
                P.step("act", expg, waits=we, incs=[(sA, 1)], label=f"exp{gg}")

                mlabs = []
                for kk in range(k0, k1):
                    m = kk - 4 * t
                    if m >= 0:
                        msk_ctr[0] += 1
                        def msk(v, _pt=ptl, _off=512 * (kk - k0), _m=m):
                            return v.tensor_tensor(
                                _pt[:, _off:_off + 512], _pt[:, _off:_off + 512],
                                master[:, 384 - 128 * _m:896 - 128 * _m], ALU.min)
                        P.step("dve", msk, waits=[(sA, f"exp{gg}")],
                               incs=[(sV, 1)], label=f"msk{msk_ctr[0]}")
                        mlabs.append(f"msk{msk_ctr[0]}")

                def av(ten, _k0=k0, _k1=k1, _pt=ptl, _hh=hh, _nch=nch):
                    pa = pacc if _hh % 2 == 0 else pex[:, 0:264]
                    ins = None
                    for kk in range(_k0, _k1):
                        for s4 in range(4):
                            ins = ten.matmul(
                                pa[:, 66 * s4:66 * s4 + 65],
                                _pt[:, 512 * (kk - _k0) + 128 * s4:
                                    512 * (kk - _k0) + 128 * s4 + 128],
                                vst[:, kk, 65 * _hh:65 * _hh + 65],
                                start=(kk == 0 and s4 == 0),
                                stop=(kk == _nch - 1 and s4 == 3),
                                skip_group_check=True)
                    return ins
                wav = [(sA, f"exp{gg}")] + [(sV, ml) for ml in mlabs]
                if g == 0:
                    if nrm_ctr[0] >= 1 + (1 if hh >= 2 else 0):
                        pass
                    if hh >= 2:
                        wav.append((sV, f"nrm{(t * H + hh) - 2 + 1}"))
                    elif t > 0:
                        if hh == 0:
                            wav.append((sV, f"nrm{(t - 1) * H + H - 2 + 1}"))
                        else:
                            wav.append((sV, f"comb{t-1}_3_1"))
                    elif t == 0 and hh == 1:
                        wav.append((sV, "c_offc"))
                pend_av.append((av, wav, f"avg{gg}"))
                gg_ctr[0] += 1

            def emit_nrm(_t=t, _hh=hh, _avlab=None):
                pa = pacc if _hh % 2 == 0 else pex[:, 0:264]
                P.step("dve", (lambda v, _pa=pa: v.tensor_copy(
                           lsb[:], _pa.rearrange("p (s c) -> p s c", c=66)[:, :, 64])),
                       waits=[(sP, _avlab)], incs=[(sV, 1)],
                       label=f"nrmA{_t}_{_hh}")
                def nrm_b(v):
                    ins = None
                    for s4 in range(4):
                        ins = v.reciprocal(recs[s4][:], lsb[:, s4:s4 + 1])
                    return ins
                P.step("dve", nrm_b, waits=[(sV, f"nrmA{_t}_{_hh}")], incs=[(sV, 1)],
                       label=f"nrmB{_t}_{_hh}")
                def nrm_c(v):
                    ins = None
                    for s4 in range(4):
                        ins = v.tensor_scalar_mul(
                            otok[:, 1024 * s4 + 64 * _hh:1024 * s4 + 64 * _hh + 64],
                            pa[:, 66 * s4:66 * s4 + 64], recs[s4][:, 0:1])
                    return ins
                wn = [(sV, f"nrmB{_t}_{_hh}")]
                if _t == 0 and _hh == 0:
                    wn.append((sP, f"trxend{NBLK-1}"))
                if _t > 0 and _hh == 0:
                    wn.append((sP, f"odone{_t-1}"))
                nrm_ctr[0] += 1
                P.step("dve", nrm_c, waits=wn, incs=[(sV, 1)], label=f"nrm{nrm_ctr[0]}")
            pend_nrm[0] = (emit_nrm, f"avg{gg_ctr[0]-1}")

        if _DBG_DUMP and t == 0:
            P.step("gp", dma(dbgo_d, otok[:]), waits=[(sV, f"nrm{nrm_ctr[0]}")],
                   incs=[(sSt, 16)])
        flush_av(keep=0)
        if pend_nrm[0] is not None:
            _enrm, _avlab = pend_nrm[0]
            pend_nrm[0] = None
            _enrm(_avlab=_avlab)
        nrm_all = f"nrm{nrm_ctr[0]}"
        if t + 1 < NT:
            emit_qld(t + 1)
        pst4 = [psG[:, 2048 + 256 * j:2048 + 256 * j + 256].bitcast(BF16)
                for j in range(4)]
        pstall = psG[:, 2048:3072].bitcast(BF16).rearrange("p (b k) -> p b k",
                                                           k=512)
        for s4 in range(4):
            for h in range(2):
                rid = s4 * 2 + h
                for j in range(4):
                    c = 4 * h + j
                    wtt = [(sV, nrm_all)]
                    if rid >= 1:
                        wtt.append((sV, f"oev{t}_{s4}_{h-1}" if h == 1
                                    else f"oev{t}_{s4-1}_1"))
                    P.step("pe", (lambda ten, _s=s4, _c=c, _p=pst4[j]:
                                  ten.transpose(_p[:, 0:128],
                                                otok[:, 1024 * _s + 128 * _c:
                                                     1024 * _s + 128 * _c + 128],
                                                identb[:])),
                           waits=wtt, incs=[(sP, 1)], label=f"otr{t}_{s4}_{h}_{j}")
                if pend_wo:
                    _f, _w, _l = pend_wo.pop(0)
                    P.step("pe", _f, waits=_w, incs=[(sP, 1)], label=_l)
                woev = [(sP, f"otr{t}_{s4}_{h}_3")]
                if rid >= 1:
                    woev.append((sP, f"wo{t}_{s4}_{h-1}" if h == 1
                                 else f"wo{t}_{s4-1}_1"))
                P.step("dve", (lambda v:
                               v.tensor_copy(otst.rearrange("p (s k) -> p s k",
                                                            k=128),
                                             pstall[:, :, 0:128])),
                       waits=woev, incs=[(sV, 1)], label=f"oev{t}_{s4}_{h}")
                def mwo(ten, _h=h):
                    ins = None
                    for j2 in range(4):
                        cc = 4 * _h + j2
                        for n in range(2):
                            ins = ten.matmul(psY[n],
                                             otst[:, 128 * j2:128 * j2 + 128],
                                             w2[:, 1024 * cc + 512 * n:
                                                1024 * cc + 512 * n + 512],
                                             start=(_h == 0 and j2 == 0),
                                             stop=(_h == 1 and j2 == 3),
                                             skip_group_check=True)
                    return ins
                wwo = [(sV, f"oev{t}_{s4}_{h}"), (sSt, "ld_wo")]
                if h == 0:
                    if s4 > 0:
                        wwo.append((sV, f"comb{t}_{s4-1}_1"))
                    elif t > 0:
                        wwo.append((sV, f"comb{t-1}_3_1"))
                pend_wo.append((mwo, wwo, f"wo{t}_{s4}_{h}"))
            while pend_wo:
                _f, _w, _l = pend_wo.pop(0)
                P.step("pe", _f, waits=_w, incs=[(sP, 1)], label=_l)
            if s4 == 3:
                P.step("pe", lambda ten: ten.nop(), waits=[(sP, f"otr{t}_3_1_3")],
                       incs=[(sP, 1)], label=f"odone{t}")

            def ffa(ten, _s=s4):
                ins = None
                for c in range(4):
                    ten.matmul(psF[0],
                               xnt2[:, c * 512 + _s * 128:c * 512 + _s * 128 + 128],
                               wtr[:, c * 512:c * 512 + 512],
                               start=(c == 0), stop=False, skip_group_check=True)
                ten.matmul(psF[0], ones2[:], bias2r[:], start=False, stop=True,
                           skip_group_check=True)
                for c in range(4):
                    ins = ten.matmul(
                        pex[:, 0:512],
                        xnt2[:, (c + 4) * 512 + _s * 128:(c + 4) * 512 + _s * 128 + 128],
                        wti[:, c * 512:c * 512 + 512],
                        start=(c == 0), stop=(c == 3), skip_group_check=True)
                return ins
            def ffb(ten, _s=s4):
                for c in range(8):
                    wt_t = wti if c < 4 else wtr
                    ten.matmul(psF[1],
                               xnt2[:, c * 512 + _s * 128:c * 512 + _s * 128 + 128],
                               wt_t[:, (c % 4) * 512:(c % 4) * 512 + 512],
                               start=(c == 0), stop=False, skip_group_check=True)
                return ten.matmul(psF[1], ones2[:], bias2i[:], start=False,
                                  stop=True, skip_group_check=True)
            wff = [(sV, "wtr"), (sV, "wti"), (sXN, f"xnld{t}"), (sSt, "bsp_i"),
                   (sV, nrm_all)]
            if s4 > 0:
                wff.append((sV, f"comb{t}_{s4-1}_1"))
            elif t > 0:
                wff.append((sV, f"comb{t-1}_3_1"))
            else:
                wff.append((sV, "c_offc"))
            P.step("pe", ffa, waits=wff, incs=[(sP, 1)], label=f"ff{t}_{s4}A")
            P.step("pe", ffb, waits=[], incs=[(sP, 1)], label=f"ff{t}_{s4}B")

            for n in range(2):
                def comb(v, _s=s4, _n=n):
                    yb = ybuf[_s % 2]
                    v.scalar_tensor_tensor(yb[:, 512 * _n:512 * _n + 512],
                                           psY[_n], 1.0,
                                           xres[:, 1024 * _s + 512 * _n:
                                                1024 * _s + 512 * _n + 512],
                                           ALU.mult, ALU.add)
                    if _n == 0:
                        v.scalar_tensor_tensor(yb[:, 0:512], pex[:, 0:512], -1.0,
                                               yb[:, 0:512], ALU.mult, ALU.add)
                    return v.scalar_tensor_tensor(yb[:, 512 * _n:512 * _n + 512],
                                                  psF[_n], 1.0,
                                                  yb[:, 512 * _n:512 * _n + 512],
                                                  ALU.mult, ALU.add)
                wc = [(sP, f"wo{t}_{s4}_1"), (sXR, f"xres{t}"),
                      (sP, f"ff{t}_{s4}A" if n == 0 else f"ff{t}_{s4}B")]
                idx = 4 * t + s4
                if idx >= 2:
                    wc.append((sY, 16 * (idx - 1)))
                P.step("dve", comb, waits=wc, incs=[(sV, 1)], label=f"comb{t}_{s4}_{n}")
            P.step("gp", dma(yr[t, s4], ybuf[s4 % 2][:]),
                   waits=[(sV, f"comb{t}_{s4}_1")], incs=[(sY, 16)],
                   label=f"yst{t}_{s4}")

    P.step("gp", lambda g_: g_.nop(), waits=[(sY, 16 * 4 * NT)])

    if _DBG_PHASE < 4:
        cut = [P._ph0, P._ph1, P._ph2, P._ph3][_DBG_PHASE]
        P.steps = P.steps[:cut]
    if _DBG_NSTEPS is not None:
        P.steps = P.steps[:_DBG_NSTEPS]
    P.resolve()

    with nc.Block() as block:
        @block.sync
        def _(sync):
            P.emit("sync", sync)

        @block.vector
        def _(vector):
            P.emit("dve", vector)

        @block.scalar
        def _(scalar):
            P.emit("act", scalar)

        @block.tensor
        def _(tensor):
            P.emit("pe", tensor)

        @block.gpsimd
        def _(gpsimd):
            P.emit("gp", gpsimd)

    es.close()
    return nc


def kernel(L0_attn_x, L0_attn_Wq, L0_attn_Wk, L0_attn_Wv, L0_attn_Wo, L0_attn_Wf,
           L0_ffn_G1_R, L0_ffn_G2_R, L0_ffn_G1_I, L0_ffn_G2_I,
           L0_ffn_bias_real, L0_ffn_bias_imag):
    x = np.ascontiguousarray(np.asarray(L0_attn_x, dtype=np.float32))
    com = {
        "wq": np.ascontiguousarray(np.asarray(L0_attn_Wq, dtype=np.float32)),
        "wk": np.ascontiguousarray(np.asarray(L0_attn_Wk, dtype=np.float32)),
        "wv": np.ascontiguousarray(np.asarray(L0_attn_Wv, dtype=np.float32)),
        "wo": np.ascontiguousarray(np.asarray(L0_attn_Wo, dtype=np.float32)),
        "wf": np.ascontiguousarray(np.asarray(L0_attn_Wf, dtype=np.float32)),
        "g1r": np.ascontiguousarray(np.asarray(L0_ffn_G1_R, dtype=np.float32)),
        "g2r": np.ascontiguousarray(np.asarray(L0_ffn_G2_R, dtype=np.float32)),
        "g1i": np.ascontiguousarray(np.asarray(L0_ffn_G1_I, dtype=np.float32)),
        "g2i": np.ascontiguousarray(np.asarray(L0_ffn_G2_I, dtype=np.float32)),
        "br": np.ascontiguousarray(np.asarray(L0_ffn_bias_real, dtype=np.float32)),
        "bi": np.ascontiguousarray(np.asarray(L0_ffn_bias_imag, dtype=np.float32)),
    }
    nc = build_nc()
    in_maps = [dict(com, x=np.ascontiguousarray(x[b])) for b in range(8)]
    res = run_bass_kernel_spmd(nc, in_maps, core_ids=list(range(8)))
    return np.stack([res.results[b]["y"] for b in range(8)], axis=0)


# version 28
# speedup vs baseline: 1.2612x; 1.0097x over previous
import numpy as np
import ml_dtypes
from contextlib import ExitStack

import concourse.bass as bass
import concourse.mybir as mybir
from concourse.bass_utils import run_bass_kernel_spmd

F32 = mybir.dt.float32
BF16 = mybir.dt.bfloat16
AF = mybir.ActivationFunctionType
ALU = mybir.AluOpType
AX = mybir.AxisListType
BF = ml_dtypes.bfloat16

_DBG_PHASE = 4
_DBG_DUMP = False
_DBG_NSTEPS = None

N = 4096
D = 1024
H = 16
HD = 64
NBLK = 8
NT = 8
G = 2


class Plan:

    def __init__(self):
        self.steps = []
        self.labels = {}

    def step(self, eng, fn, waits=(), incs=(), label=None, serialize=False):
        self.steps.append([eng, fn, list(waits), list(incs), label, serialize])

    def resolve(self):
        cnt = {}
        for st in self.steps:
            _eng, _fn, waits, incs, label, serialize = st
            rw = []
            if serialize and incs:
                sem0 = incs[0][0]
                rw.append((sem0, cnt.get(id(sem0), 0)))
            for sem, lab in waits:
                if isinstance(lab, int):
                    rw.append((sem, lab))
                else:
                    s2, v = self.labels[lab]
                    assert s2 is sem, f"label {lab} bound to a different sem"
                    rw.append((sem, v))
            st[2] = rw
            for sem, n in incs:
                cnt[id(sem)] = cnt.get(id(sem), 0) + n
            if label is not None:
                assert len(incs) == 1, f"labeled step {label} must have exactly one inc"
                self.labels[label] = (incs[0][0], cnt[id(incs[0][0])])

    def emit(self, eng_name, eng):
        hi = {}
        for eng2, fn, waits, incs, _label, _ser in self.steps:
            if eng2 != eng_name:
                continue
            merged = {}
            for sem, val in waits:
                k = id(sem)
                if val > merged.get(k, (None, -1))[1]:
                    merged[k] = (sem, val)
            for k, (sem, val) in merged.items():
                if val > hi.get(k, -1):
                    eng.wait_ge(sem, val)
                    hi[k] = val
            ins = fn(eng)
            for sem, n in incs:
                assert ins is not None, "step with incs must return an instruction"
                ins.then_inc(sem, n)


def _consts():
    ident = np.eye(128, dtype=BF)
    identf = np.eye(128, dtype=np.float32)
    uu = np.arange(896)[None, :] - 384
    jj = np.arange(128)[:, None]
    master = np.where(jj > uu, 0.0, 3.0e38).astype(BF)
    ones3 = np.ones((3, H * 512), dtype=BF)
    ones2 = np.ones((2, 128), dtype=BF)
    p = np.arange(512)
    comb_d = (p[None, :] // 16 == np.arange(32)[:, None]).astype(np.float32)
    p2 = np.arange(128)
    comb_g = (p2[None, :] % 16 == np.arange(16)[:, None]).astype(np.float32)
    k = np.arange(128)
    Ltri = ((k[:, None] // 8 == k[None, :] // 8) &
            (k[:, None] % 8 < k[None, :] % 8)).astype(np.float32)
    return ident, identf, master, ones3, ones2, comb_d, comb_g, Ltri


def build_nc():
    nc = bass.Bass("TRN2", target_bir_lowering=False)

    x_d = nc.dram_tensor("x", [N, D], F32, kind="ExternalInput").ap()
    wq_d = nc.dram_tensor("wq", [D, D], F32, kind="ExternalInput").ap()
    wk_d = nc.dram_tensor("wk", [D, D], F32, kind="ExternalInput").ap()
    wv_d = nc.dram_tensor("wv", [D, D], F32, kind="ExternalInput").ap()
    wo_d = nc.dram_tensor("wo", [D, D], F32, kind="ExternalInput").ap()
    wf_d = nc.dram_tensor("wf", [D, H], F32, kind="ExternalInput").ap()
    g1r_d = nc.dram_tensor("g1r", [1, 32, 4, 32], F32, kind="ExternalInput").ap()
    g2r_d = nc.dram_tensor("g2r", [4, 16, 1, 16], F32, kind="ExternalInput").ap()
    g1i_d = nc.dram_tensor("g1i", [1, 32, 4, 32], F32, kind="ExternalInput").ap()
    g2i_d = nc.dram_tensor("g2i", [4, 16, 1, 16], F32, kind="ExternalInput").ap()
    br_d = nc.dram_tensor("br", [512], F32, kind="ExternalInput").ap()
    bi_d = nc.dram_tensor("bi", [512], F32, kind="ExternalInput").ap()
    y_d = nc.dram_tensor("y", [N, D], F32, kind="ExternalOutput").ap()

    knd = "ExternalOutput" if _DBG_DUMP else "Internal"
    if _DBG_DUMP:
        dbgp_d = nc.dram_tensor("dbgp", [2, 128, G * 512], BF16, kind="ExternalOutput").ap()
        dbgo_d = nc.dram_tensor("dbgo", [128, 4096], BF16, kind="ExternalOutput").ap()
        dbgw_d = nc.dram_tensor("dbgw", [2, 128, 2048], BF16, kind="ExternalOutput").ap()
    kbuf = nc.dram_tensor("kbuf", [H, HD, N], BF16, kind=knd).ap()
    qbuf = nc.dram_tensor("qbuf", [NT, HD, H, 512], BF16, kind=knd).ap()
    cqbuf = nc.dram_tensor("cqbuf", [H, 3, N], BF16, kind=knd).ap()
    ckbuf = nc.dram_tensor("ckbuf", [H, 3, N], BF16, kind=knd).ap()
    xbuf = nc.dram_tensor("xbuf", [NBLK, 128, 8 * 512], BF16, kind=knd).ap()
    fbuf = nc.dram_tensor("fbuf", [128, 512], F32, kind=knd).ap()

    ident_np, identf_np, master_np, ones3_np, ones2_np, comb_d_np, comb_g_np, Ltri_np = _consts()
    identb_d = nc.inline_tensor(ident_np, "identb").ap()
    identf_d = nc.inline_tensor(identf_np, "identf").ap()
    master_d = nc.inline_tensor(master_np, "masterc").ap()
    ones3_d = nc.inline_tensor(ones3_np, "ones3c").ap()
    ones2_d = nc.inline_tensor(ones2_np, "ones2c").ap()
    comb_d_d = nc.inline_tensor(comb_d_np, "combdc").ap()
    comb_g_d = nc.inline_tensor(comb_g_np, "combgc").ap()
    Ltri_d = nc.inline_tensor(Ltri_np, "ltric").ap()

    es = ExitStack()
    sb = lambda name, shape, dt: es.enter_context(nc.sbuf_tensor(name, shape, dt))
    psalloc = lambda name, shape, dt: es.enter_context(nc.psum_tensor(name, shape, dt))

    vst = sb("vst", [128, 32, H * 65], BF16)
    w1 = sb("w1", [128, 8 * 1024], BF16)
    w2 = sb("w2", [128, 8 * 1024], BF16)
    wfb = sb("wfb", [128, 8 * 16], BF16)
    kwin = [sb(f"kwin{i}", [128, N], BF16) for i in range(2)]
    qwin = sb("qwin", [128, H * 512], BF16)
    xres = sb("xres", [128, 4 * 1024], F32)
    xnt2 = sb("xnt2", [128, 8 * 512], BF16)
    otok = sb("otok", [128, 4 * 1024], BF16)
    ptile = [sb(f"ptile{i}", [128, G * 512], BF16) for i in range(3)]
    ybuf = [sb(f"ybuf{i}", [128, 1024], F32) for i in range(2)]
    xblk0 = sb("xblk0", [128, 4 * 1024], F32)
    xblk = [xblk0, xblk0]
    wtr = sb("wtr", [128, 4 * 512], BF16)
    wti = sb("wti", [128, 4 * 512], BF16)
    bias2r = sb("bias2r", [2, 512], BF16)
    bias2i = sb("bias2i", [2, 512], BF16)
    identb = sb("identb_sb", [128, 128], BF16)
    master = sb("master_sb", [128, 896], BF16)
    ones2 = sb("ones2_sb", [2, 128], BF16)
    epst = sb("epst", [128, 1], F32)
    onecol = sb("onecol", [128, 1], F32)
    sum_s = [sb(f"sum_s{i}", [128, 1], F32) for i in range(4)]
    ssq_s = [sb(f"ssq_s{i}", [128, 1], F32) for i in range(4)]
    m_s = [sb(f"m_s{i}", [128, 1], F32) for i in range(4)]
    vr_s = [sb(f"vr_s{i}", [128, 1], F32) for i in range(4)]
    t_s = [sb(f"t_s{i}", [128, 1], F32) for i in range(4)]
    var_s = [sb(f"var_s{i}", [128, 1], F32) for i in range(4)]
    rstd_s = [sb(f"rstd_s{i}", [128, 1], F32) for i in range(4)]
    nb_s = [sb(f"nb_s{i}", [128, 1], F32) for i in range(4)]
    lsb = sb("lsb", [128, 4], F32)
    recs = [sb(f"recs{i}", [128, 1], F32) for i in range(4)]
    offs = sb("offs", [128, 1], F32)
    totals = sb("totals", [128, 1], F32)
    _kw0_addr = nc.lookup_mloc(kwin[0]).addr
    _kw1_addr = nc.lookup_mloc(kwin[1]).addr
    xnt1s = [nc.alloc_sbuf_tensor_at("xnt1a", [128, 8 * 512], BF16, offset=_kw0_addr).ap(),
             nc.alloc_sbuf_tensor_at("xnt1b", [128, 8 * 512], BF16, offset=_kw1_addr).ap()]
    otst = ptile[2][:, 0:512]
    xnst = otok
    uae = xres[:, 0:2048]
    ube = xres[:, 2048:4096]
    prod = xnst[:].bitcast(F32)
    xnt2f = xnt2[:].bitcast(F32)
    ua_sb = xnt2f[:, 0:512]
    wtf32 = xnt2f[:, 512:1024]
    ub_sb = xnt2f[:, 1024:1088]
    at_sb = xnt2f[0:32, 1088:1216]
    bt_sb = xnt2f[0:16, 1216:1280]
    gsta = xnt2f[:, 1280:1408]
    combd = xnt2f[0:32, 1408:1920]
    combg = xnt2f[0:16, 1920:2048]
    ybf1 = ybuf[1].bitcast(F32) if False else ybuf[1]
    ltri = ybuf[1][:, 512:640]
    identf = ybuf[1][:, 640:768]
    kst4 = [ptile[i // 2][:, (i % 2) * 512:(i % 2) * 512 + 512]
            for i in range(6)]
    qwf = qwin[:].bitcast(F32)
    bspl = qwin[0:1, 0:1024]
    bf32 = qwf[0:1, 512:1024]
    btmp = qwf[0:1, 1024:1536]
    btmp2 = qwf[0:1, 1536:2048]
    xrf = xres
    f128 = xrf[:, 0:512]
    cst = [xrf[:, 512:1024], xrf[:, 1024:1536]]
    casc = xrf[:, 1536:2048]
    ctmp = xrf[:, 2048:2560]
    csp = xres[:].bitcast(BF16)[:, 5120:6656]
    csn = xres[:].bitcast(BF16)[:, 6656:8192]
    sq_s = ybuf[0][:, 0:1024]
    fst = ybuf[1][0:16, 0:512]

    psG = psalloc("psG", [128, 3072], F32)
    pacc = psalloc("pacc", [128, 264], F32)
    pex = psalloc("pex", [128, 512], F32)
    psA = psG[:, 0:1536]
    psB = psG[:, 1536:3072]
    pstA = psG[:, 2048:2304].bitcast(BF16)
    pstB = psG[:, 2560:2816].bitcast(BF16)
    pstC = psG[:, 1024:1280].bitcast(BF16)
    pstD = psG[:, 1536:1792].bitcast(BF16)
    psY = [psG[:, 0:512], psG[:, 512:1024]]
    psF = [psG[:, 1024:1536], psG[:, 1536:2048]]

    sW = es.enter_context(nc.semaphore("sW"))
    sX = es.enter_context(nc.semaphore("sX"))
    sK0 = es.enter_context(nc.semaphore("sK0"))
    sK1 = es.enter_context(nc.semaphore("sK1"))
    sQ = es.enter_context(nc.semaphore("sQ"))
    sXR = es.enter_context(nc.semaphore("sXR"))
    sXN = es.enter_context(nc.semaphore("sXN"))
    sSt = es.enter_context(nc.semaphore("sSt"))
    sY = es.enter_context(nc.semaphore("sYm"))
    sP = es.enter_context(nc.semaphore("sP"))
    sA = es.enter_context(nc.semaphore("sA"))
    sV = es.enter_context(nc.semaphore("sV"))

    P = Plan()

    def dma(out, in_, **kw):
        return lambda eng: eng.dma_start(out=out, in_=in_, **kw)

    P.step("dve", lambda v: v.memset(qwin[64:128, :], 0.0), incs=[(sV, 1)],
           label="qw_zero")
    P.step("sync", dma(identb[:], identb_d), incs=[(sW, 16)], serialize=True)
    P.step("sync", dma(identf[:], identf_d), incs=[(sW, 16)], serialize=True)
    P.step("sync", dma(master[:], master_d), incs=[(sW, 16)], serialize=True)
    P.step("sync", dma(qwin[67:70, :], ones3_d), waits=[(sV, "qw_zero")],
           incs=[(sW, 16)], serialize=True)
    P.step("sync", dma(ones2[:], ones2_d), incs=[(sW, 16)], serialize=True)
    P.step("sync", dma(combd[:], comb_d_d), incs=[(sW, 16)], serialize=True)
    P.step("sync", dma(combg[:], comb_g_d), incs=[(sW, 16)], serialize=True)
    P.step("sync", dma(ltri[:], Ltri_d), incs=[(sW, 16)], label="ld_small", serialize=True)
    wk_r = wk_d.rearrange("(c p) n -> p c n", p=128)
    wq_r = wq_d.rearrange("(c p) n -> p c n", p=128)
    wf_r = wf_d.rearrange("(c p) n -> p c n", p=128)
    P.step("gp", dma(w1.rearrange("p (c n) -> p c n", c=8), wk_r),
           incs=[(sSt, 16)], label="ld_wk", serialize=True)
    P.step("gp", dma(w2.rearrange("p (c n) -> p c n", c=8), wq_r),
           incs=[(sSt, 16)], label="ld_wq", serialize=True)
    P.step("gp", dma(wfb.rearrange("p (c n) -> p c n", c=8), wf_r),
           incs=[(sSt, 16)], label="ld_wf", serialize=True)
    P.step("sync", dma(gsta[:, 0:32], g1r_d.rearrange("a b c d -> (a b c) d")),
           incs=[(sW, 16)], label="ld_g1r", serialize=True)
    P.step("sync", dma(gsta[:, 32:64], g1i_d.rearrange("a b c d -> (a b c) d")),
           incs=[(sW, 16)], label="ld_g1i", serialize=True)
    P.step("sync", dma(gsta[0:64, 64:80], g2r_d.rearrange("c e a g -> (c e a) g")),
           incs=[(sW, 16)], label="ld_g2r", serialize=True)
    P.step("sync", dma(gsta[0:64, 80:96], g2i_d.rearrange("c e a g -> (c e a) g")),
           incs=[(sW, 16)], label="ld_g2i", serialize=True)
    P.step("sync", dma(bf32[:], br_d.rearrange("(a n) -> a n", a=1)),
           incs=[(sW, 16)], label="ld_br", serialize=True)

    P.step("dve", lambda v: v.memset(epst[:], 1e-5), waits=[(sW, "ld_small")],
           incs=[(sV, 1)], label="eps_set")
    P.step("dve", lambda v: v.memset(onecol[:], 1.0))
    P.step("dve", lambda v: v.memset(vst[:], 1.0), incs=[(sV, 1)], label="vst_ones")
    P.step("dve", lambda v: v.tensor_scalar_mul(w2[:], w2[:], 0.125),
           waits=[(sSt, "ld_wq")], incs=[(sV, 1)], label="wq_scaled")

    def build_wt(tag, g1_cols, g2_cols, wt_tile, neg_tile):
        wtr1 = [(sW, "ld_g1" + tag), (sW, "ld_small")]
        if tag == "i":
            wtr1.append((sV, "ubcr"))
        P.step("pe", (lambda t, _a=g1_cols:
                      t.transpose(pex[0:32, 0:128], gsta[:, _a[0]:_a[1]], identf[:])),
               waits=wtr1, incs=[(sP, 1)], label=f"tr_g1{tag}")
        P.step("dve", lambda v: v.tensor_copy(at_sb[:], pex[0:32, 0:128]),
               waits=[(sP, f"tr_g1{tag}")], incs=[(sV, 1)], label=f"at{tag}")
        P.step("pe", (lambda t, _a=g2_cols:
                      t.transpose(pex[0:16, 128:192], gsta[0:64, _a[0]:_a[1]], identf[0:64, 0:64])),
               waits=[(sW, "ld_g2" + tag), (sV, f"at{tag}")],
               incs=[(sP, 1)], label=f"tr_g2{tag}")
        P.step("dve", lambda v: v.tensor_copy(bt_sb[:], pex[0:16, 128:192]),
               waits=[(sP, f"tr_g2{tag}")], incs=[(sV, 1)], label=f"bt{tag}")

        def mk_ua(t):
            ins = None
            for p4 in range(4):
                ins = t.matmul(pex[:, p4 * 128:p4 * 128 + 128],
                               combd[:, p4 * 128:p4 * 128 + 128], at_sb[:],
                               start=True, stop=True)
            return ins
        P.step("pe", mk_ua, waits=[(sV, f"at{tag}"), (sV, f"bt{tag}")],
               incs=[(sP, 1)], label=f"ua{tag}")
        P.step("dve", lambda v: v.tensor_copy(ua_sb[:], pex[:, 0:512]),
               waits=[(sP, f"ua{tag}")], incs=[(sV, 1)], label=f"uac{tag}")
        P.step("pe", lambda t: t.matmul(pex[:, 0:64], combg[:], bt_sb[:],
                                        start=True, stop=True),
               waits=[(sV, f"uac{tag}")], incs=[(sP, 1)], label=f"ub{tag}")
        P.step("dve", lambda v: v.tensor_copy(ub_sb[:], pex[:, 0:64]),
               waits=[(sP, f"ub{tag}")], incs=[(sV, 1)], label=f"ubc{tag}")

        def mk_wt_chunk(v, p4, _wt=wt_tile, _neg=neg_tile):
            ua_r = ua_sb[:, p4 * 128:p4 * 128 + 128].rearrange("p (b c) -> p b c", c=4)
            uae_r = uae.rearrange("p (b c e) -> p e b c", c=4, e=16)
            for e in range(16):
                v.tensor_copy(uae_r[:, e], ua_r)
            ub_r = ub_sb[:, 0:64].rearrange("p (c e) -> p c e", c=4)
            ube_r = ube.rearrange("p (b c e) -> p b c e", c=4, e=16)
            for b8 in range(32):
                v.tensor_copy(ube_r[:, b8], ub_r)
            ins = v.tensor_mul(prod, uae, ube)
            return ins
        for p4 in range(4):
            P.step("dve", (lambda v, _p4=p4: mk_wt_chunk(v, _p4)),
                   waits=[(sV, f"ubc{tag}")] if p4 == 0 else [(sV, f"wtred{tag}{p4-1}")],
                   incs=[(sV, 1)], label=f"wtp{tag}{p4}")
            def red(v, _p4=p4, _wt=wt_tile, _neg=neg_tile):
                pr2 = prod.rearrange("p (b c e) -> p b c e", c=4, e=16)
                wv_ = wtf32.rearrange("p (b e) -> p b e", e=16)
                v.tensor_tensor(wv_, pr2[:, :, 0], pr2[:, :, 1], ALU.add)
                v.tensor_tensor(wv_, wv_, pr2[:, :, 2], ALU.add)
                v.tensor_tensor(wv_, wv_, pr2[:, :, 3], ALU.add)
                return v.tensor_copy(_wt[:, _p4 * 512:_p4 * 512 + 512], wtf32[:])
            P.step("dve", red, waits=[(sV, f"wtp{tag}{p4}")],
                   incs=[(sV, 1)], label=f"wtred{tag}{p4}")
        P.step("dve", lambda v: v.tensor_copy(btmp[:], btmp[:]),
               waits=[(sV, f"wtred{tag}3")], incs=[(sV, 1)], label=f"wt{tag}")

    build_wt("r", (0, 32), (64, 80), wtr, None)
    build_wt("i", (32, 64), (80, 96), wti, None)
    if _DBG_DUMP:
        P.step("gp", dma(dbgw_d[0], wtr[:]), waits=[(sV, "wtr")], incs=[(sSt, 16)])
        P.step("gp", dma(dbgw_d[1], wti[:]), waits=[(sV, "wti")], incs=[(sSt, 16)])

    def bias_split(v):
        v.tensor_copy(bspl[0:1, 0:512], bf32[:])
        v.tensor_copy(btmp[:], bspl[0:1, 0:512])
        v.tensor_sub(btmp2[:], bf32[:], btmp[:])
        return v.tensor_copy(bspl[0:1, 512:1024], btmp2[:])
    P.step("dve", bias_split, waits=[(sW, "ld_br")], incs=[(sV, 1)], label="bsp_r0")
    P.step("gp", dma(bias2r[0:1, :], bspl[0:1, 0:512]), waits=[(sV, "bsp_r0")],
           incs=[(sSt, 16)], serialize=True)
    P.step("gp", dma(bias2r[1:2, :], bspl[0:1, 512:1024]), incs=[(sSt, 16)], label="bsp_r", serialize=True)
    P.step("dve", lambda v: v.tensor_copy(btmp2[:], btmp2[:]),
           waits=[(sSt, "bsp_r")], incs=[(sV, 1)], label="bsplit_r")
    P.step("sync", dma(bf32[:], bi_d.rearrange("(a n) -> a n", a=1)),
           waits=[(sV, "bsplit_r")], incs=[(sW, 16)], label="ld_bi", serialize=True)
    P.step("dve", bias_split, waits=[(sW, "ld_bi")], incs=[(sV, 1)], label="bsp_i0")
    P.step("gp", dma(bias2i[0:1, :], bspl[0:1, 0:512]), waits=[(sV, "bsp_i0")],
           incs=[(sSt, 16)], serialize=True)
    P.step("gp", dma(bias2i[1:2, :], bspl[0:1, 512:1024]), incs=[(sSt, 16)], label="bsp_i", serialize=True)

    P._ph0 = len(P.steps)
    xr = x_d.rearrange("(b s p) d -> b p s d", p=128, s=4)

    def emit_front(b):
        bb = b % 2
        w = [(sV, f"aff{b-1}")] if b >= 1 else []
        P.step("sync", dma(xblk[bb].rearrange("p (s d) -> p s d", s=4), xr[b]),
               waits=w, incs=[(sX, 16)], label=f"xld{b}")

        def ln_stats(v, _bb=bb):
            ins = None
            for s in range(4):
                xsub = xblk[_bb][:, s * 1024:(s + 1) * 1024]
                v.reduce_sum(sum_s[s][:], xsub, axis=AX.X)
                v.tensor_mul(sq_s[:], xsub, xsub)
                ins = v.reduce_sum(ssq_s[s][:], sq_s[:], axis=AX.X)
            return ins
        P.step("dve", ln_stats, waits=[(sX, 16 * (b + 1))], incs=[(sV, 1)],
               label=f"lnst{b}")

        def ln_ab(v):
            ins = None
            for s in range(4):
                v.tensor_scalar_mul(m_s[s][:], sum_s[s][:], 1.0 / 1024)
                ins = v.tensor_scalar_mul(vr_s[s][:], ssq_s[s][:], 1.0 / 1024)
            return ins
        P.step("dve", ln_ab, waits=[(sV, f"lnst{b}")], incs=[(sV, 1)], label=f"lnab{b}")

        def ln_c(v):
            ins = None
            for s in range(4):
                ins = v.tensor_mul(t_s[s][:], m_s[s][:], m_s[s][:])
            return ins
        P.step("dve", ln_c, waits=[(sV, f"lnab{b}")], incs=[(sV, 1)], label=f"lnc{b}")

        def ln_d(v):
            ins = None
            for s in range(4):
                ins = v.tensor_sub(var_s[s][:], vr_s[s][:], t_s[s][:])
            return ins
        P.step("dve", ln_d, waits=[(sV, f"lnc{b}")], incs=[(sV, 1)], label=f"lnd{b}")

        def ln_ln(sc):
            ins = None
            for s in range(4):
                ins = sc.activation(rstd_s[s][:], var_s[s][:], AF.Ln, bias=epst[:, 0:1])
            return ins
        P.step("act", ln_ln, waits=[(sV, f"lnd{b}"), (sV, "eps_set")],
               incs=[(sA, 1)], label=f"lnln{b}")

        def ln_exp(sc):
            ins = None
            for s in range(4):
                ins = sc.activation(rstd_s[s][:], rstd_s[s][:], AF.Exp, scale=-0.5)
            return ins
        P.step("act", ln_exp, waits=[(sA, f"lnln{b}")], incs=[(sA, 1)], label=f"rstd{b}")

        def ln_nb(v):
            ins = None
            for s in range(4):
                ins = v.scalar_tensor_tensor(nb_s[s][:], m_s[s][:], -1.0, rstd_s[s][:],
                                             ALU.mult, ALU.mult)
            return ins
        P.step("dve", ln_nb, waits=[(sA, f"rstd{b}")], incs=[(sV, 1)], label=f"lnnb{b}")

        def ln_aff(v, _bb=bb):
            ins = None
            for s in range(4):
                ins = v.tensor_scalar(xnst[:, s * 1024:(s + 1) * 1024],
                                      xblk[_bb][:, s * 1024:(s + 1) * 1024],
                                      rstd_s[s][:, 0:1], nb_s[s][:, 0:1], ALU.mult, ALU.add)
            return ins
        waff = [(sV, f"lnnb{b}")]
        if b >= 1:
            waff.append((sP, f"trxend{b-1}"))
        P.step("dve", ln_aff, waits=waff, incs=[(sV, 1)], label=f"aff{b}")

    def emit_front_trx(b):
        bb = b % 2
        for s in range(4):
            for c in range(8):
                i = s * 8 + c
                pst = pstA if i % 2 == 0 else pstB
                wtt = [(sV, f"aff{b}")]
                if b > 0 or i >= 2:
                    wtt.append((sA, f"evx{b}_{i-2}" if i >= 2 else f"evx{b-1}_{30+i}"))
                P.step("pe", (lambda t, _s=s, _c=c, _p=pst:
                              t.transpose(_p[:, 0:128],
                                          xnst[:, _s * 1024 + _c * 128:_s * 1024 + _c * 128 + 128],
                                          identb[:])),
                       waits=wtt, incs=[(sP, 1)], label=f"trx{b}_{i}")
                P.step("act", (lambda sc, _s=s, _c=c, _p=pst, _x=xnt1s[b % 2]:
                               sc.copy(_x[:, _c * 512 + _s * 128:_c * 512 + _s * 128 + 128],
                                       _p[:, 0:128])),
                       waits=[(sP, f"trx{b}_{i}")], incs=[(sA, 1)], label=f"evx{b}_{i}")
        P.step("pe", lambda t: t.nop(), waits=[(sP, f"trx{b}_31")],
               incs=[(sP, 1)], label=f"trxend{b}")
        P.step("gp", dma(xbuf[b], xnt1s[b % 2][:]), waits=[(sA, f"evx{b}_31")],
               incs=[(sSt, 16)], label=f"xbst{b}", serialize=True)

        continue_marker = None
    def emit_mmblock(b, mid=None):
        xnt1 = xnt1s[b % 2]
        for wi in range(2):
            if wi == 1 and mid is not None:
                mid()
            wt_tile = w1 if wi == 0 else w2
            buf = kbuf if wi == 0 else qbuf
            pko = 0 if wi == 0 else 512
            for m in range(8):
                def mmk(t, _wt=wt_tile, _m=m, _o=pko):
                    ins = None
                    for c in range(8):
                        ins = t.matmul(psA[:, _o:_o + 512],
                                       _wt[:, c * 1024 + _m * 128:c * 1024 + _m * 128 + 128],
                                       xnt1[:, c * 512:c * 512 + 512],
                                       start=(c == 0), stop=(c == 7))
                    return ins
                wts = [(sA, f"evx{b}_31"), (sSt, "ld_wk" if wi == 0 else "ld_wq")]
                if wi == 1:
                    wts.append((sV, "wq_scaled"))
                if b == 0 and wi == 0 and m == 0:
                    wts.append((sV, "wtr"))
                    wts.append((sV, "wti"))
                if b > 0 or m > 0:
                    prev = f"evk{b}_{wi}_{m-1}" if m > 0 else f"evk{b-1}_{wi}_7"
                    wts.append((sA, prev))
                P.step("pe", mmk, waits=wts, incs=[(sP, 1)], label=f"mmk{b}_{wi}_{m}")
                jj = (b * 16 + wi * 8 + m)
                wev = [(sP, f"mmk{b}_{wi}_{m}")]
                if jj >= 6:
                    pj = jj - 6
                    wev.append((sSt, f"kqst{pj // 16}_{(pj % 16) // 8}_{pj % 8}"))
                P.step("act", (lambda sc, _j=jj, _o=pko:
                               sc.copy(kst4[_j % 6][:], psA[:, _o:_o + 512])),
                       waits=wev, incs=[(sA, 1)],
                       label=f"evk{b}_{wi}_{m}")
                if wi == 0:
                    P.step("gp", dma(buf[2 * m:2 * m + 2, :, b * 512:(b + 1) * 512]
                                     .rearrange("h r t -> (h r) t"), kst4[jj % 6][:]),
                           waits=[(sA, f"evk{b}_{wi}_{m}")], incs=[(sSt, 16)],
                           label=f"kqst{b}_{wi}_{m}", serialize=(m == 0))
                else:
                    P.step("gp", dma(qbuf[b, :, 2 * m, :], kst4[jj % 6][0:64, :]),
                           waits=[(sA, f"evk{b}_{wi}_{m}")], incs=[(sSt, 16)])
                    P.step("gp", dma(qbuf[b, :, 2 * m + 1, :], kst4[jj % 6][64:128, :]),
                           incs=[(sSt, 16)], label=f"kqst{b}_{wi}_{m}")

        def mmf(t):
            ins = None
            for c in range(8):
                ins = t.matmul(pex[0:16, 0:512], wfb[:, c * 16:c * 16 + 16],
                               xnt1[:, c * 512:c * 512 + 512], start=(c == 0), stop=(c == 7))
            return ins
        wtsf = [(sA, f"evx{b}_31"), (sSt, "ld_wf")]
        if b == 0:
            wtsf += [(sV, "wtr"), (sV, "wti")]
        else:
            wtsf.append((sA, f"evf{b-1}"))
        P.step("pe", mmf, waits=wtsf, incs=[(sP, 1)], label=f"mmf{b}")
        wevf = [(sP, f"mmf{b}")]
        if b > 0:
            wevf.append((sSt, f"fst{b-1}"))
        P.step("act", lambda sc: sc.copy(fst[:], pex[0:16, 0:512]),
               waits=wevf, incs=[(sA, 1)], label=f"evf{b}")
        P.step("gp", dma(fbuf.rearrange("(h s) t -> h s t", s=8)[:, b, :], fst[:]),
               waits=[(sA, f"evf{b}")], incs=[(sSt, 16)], label=f"fst{b}")

    emit_front(0)
    emit_front_trx(0)
    for b in range(1, NBLK):
        emit_mmblock(b - 1, mid=(lambda _b=b: emit_front(_b)))
        emit_front_trx(b)
    emit_mmblock(NBLK - 1)

    def kw_init(v, kb):
        v.memset(kwin[kb][64:128, :], 0.0)
        return v.memset(kwin[kb][64:67, :], 1.0)
    P.step("dve", lambda v: kw_init(v, 0),
           waits=[(sP, f"mmf{NBLK-2}"), (sSt, f"xbst{NBLK-2}")],
           incs=[(sV, 1)], label="kw0_ones")
    P.step("dve", lambda v: kw_init(v, 1),
           waits=[(sP, f"mmf{NBLK-1}"), (sSt, f"xbst{NBLK-1}")],
           incs=[(sV, 1)], label="kw1_ones")

    P._ph1 = len(P.steps)
    wv_r = wv_d.rearrange("(c p) n -> p c n", p=128)
    wo_r = wo_d.rearrange("(c p) n -> p c n", p=128)
    P.step("gp", dma(w1.rearrange("p (c n) -> p c n", c=8), wv_r),
           waits=[(sP, f"mmk{NBLK-1}_0_7")], incs=[(sSt, 16)], label="ld_wv", serialize=True)
    P.step("gp", dma(w2.rearrange("p (c n) -> p c n", c=8), wo_r),
           waits=[(sP, f"mmk{NBLK-1}_1_7")], incs=[(sSt, 16)], label="ld_wo", serialize=True)

    for b in range(NBLK):
        wts = [(sSt, f"xbst{b}")]
        if b > 0:
            wts.append((sP, f"mmv{b-1}_3_1"))
        P.step("sync", dma(xnt2[:], xbuf[b]), waits=wts, incs=[(sX, 16)], label=f"xrld{b}")
        for s in range(4):
            for n in range(2):
                def mmv(t, _s=s, _n=n):
                    ins = None
                    for c in range(8):
                        ins = t.matmul(psG[:, 1024 + _n * 512:1024 + _n * 512 + 512],
                                       xnt2[:, c * 512 + _s * 128:c * 512 + _s * 128 + 128],
                                       w1[:, c * 1024 + _n * 512:c * 1024 + _n * 512 + 512],
                                       start=(c == 0), stop=(c == 7))
                    return ins
                wts = [(sX, 16 * (NBLK + b + 1)), (sSt, "ld_wv")]
                if b == 0 and s == 0 and n == 0:
                    wts.append((sA, f"evx{NBLK-1}_31"))
                if b > 0 or s > 0 or n > 0:
                    pv = (f"evv{b}_{s}_{n-1}" if n > 0 else
                          (f"evv{b}_{s-1}_1" if s > 0 else f"evv{b-1}_3_1"))
                    wts.append((sA, pv))
                P.step("pe", mmv, waits=wts, incs=[(sP, 1)], label=f"mmv{b}_{s}_{n}")
                kap = 4 * b + s
                P.step("act", (lambda sc, _k=kap, _n=n:
                               sc.copy(vst.rearrange("p k (h c) -> p k h c", c=65)
                                       [:, _k, 8 * _n:8 * _n + 8, 0:64],
                                       (psG[:, 1024 + _n * 512:1024 + _n * 512 + 512]
                                        ).rearrange("p (h c) -> p h c", c=64))),
                       waits=[(sP, f"mmv{b}_{s}_{n}"), (sV, "vst_ones")],
                       incs=[(sA, 1)], label=f"evv{b}_{s}_{n}")

    P._ph2 = len(P.steps)
    P.step("sync", dma(f128[:], fbuf), waits=[(sSt, f"fst{NBLK-1}")],
           incs=[(sX, 16)], label="fld")
    P.step("act", lambda sc: sc.activation(casc[:], f128[:], AF.Abs),
           waits=[(sX, 16 * (2 * NBLK + 1))], incs=[(sA, 1)], label="c_abs")
    P.step("act", lambda sc: sc.activation(casc[:], casc[:], AF.Exp, scale=-1.0),
           waits=[(sA, "c_abs")], incs=[(sA, 1)], label="c_exp")
    P.step("act", lambda sc: sc.activation(casc[:], casc[:], AF.Ln, bias=onecol[:, 0:1]),
           waits=[(sA, "c_exp")], incs=[(sA, 1)], label="c_ln")
    P.step("dve", lambda v: v.tensor_scalar_min(ctmp[:], f128[:], 0.0),
           waits=[(sA, "c_ln")], incs=[(sV, 1)], label="c_min")
    P.step("dve", lambda v: v.tensor_sub(cst[0][:], ctmp[:], casc[:]),
           waits=[(sV, "c_min")], incs=[(sV, 1)], label="c_lf")
    src_i = 0
    for st_i, sh in enumerate([1, 2, 4, 8, 16, 32, 64, 128, 256]):
        dst = 1 - src_i
        def csum(v, _s=src_i, _d=dst, _sh=sh):
            v.tensor_copy(cst[_d][:, 0:_sh], cst[_s][:, 0:_sh])
            return v.tensor_add(cst[_d][:, _sh:512], cst[_s][:, _sh:512],
                                cst[_s][:, 0:512 - _sh])
        P.step("dve", csum, waits=[(sV, "c_lf" if st_i == 0 else f"cs{st_i-1}")],
               incs=[(sV, 1)], label=f"cs{st_i}")
        src_i = dst
    cfin = cst[src_i]
    P.step("dve", lambda v: v.tensor_copy(totals[:], cfin[:, 511:512]),
           waits=[(sV, "cs8")], incs=[(sV, 1)], label="c_tot")
    P.step("pe", lambda t: t.matmul(pex[:, 0:1], ltri[:], totals[:], start=True, stop=True),
           waits=[(sV, "c_tot"), (sA, f"evf{NBLK-1}")], incs=[(sP, 1)], label="c_off")
    P.step("dve", lambda v: v.tensor_copy(offs[:], pex[:, 0:1]),
           waits=[(sP, "c_off")], incs=[(sV, 1)], label="c_offc")
    P.step("dve", lambda v: v.tensor_scalar_add(ctmp[:], cfin[:], offs[:, 0:1]),
           waits=[(sV, "c_offc")], incs=[(sV, 1)], label="c_full")
    def csplit1(v):
        v.tensor_copy(csp[:, 0:512], ctmp[:])
        return v.tensor_copy(casc[:], csp[:, 0:512])
    P.step("dve", csplit1, waits=[(sV, "c_full")], incs=[(sV, 1)], label="csp1")
    def csplit2(v):
        v.tensor_sub(f128[:], ctmp[:], casc[:])
        return v.tensor_copy(csp[:, 512:1024], f128[:])
    P.step("dve", csplit2, waits=[(sV, "csp1")], incs=[(sV, 1)], label="csp2")
    def csplit3(v):
        v.tensor_copy(casc[:], csp[:, 512:1024])
        return v.tensor_sub(f128[:], f128[:], casc[:])
    P.step("dve", csplit3, waits=[(sV, "csp2")], incs=[(sV, 1)], label="csp3")
    P.step("dve", lambda v: v.tensor_copy(csp[:, 1024:1536], f128[:]),
           waits=[(sV, "csp3")], incs=[(sV, 1)], label="csp4")
    P.step("dve", lambda v: v.tensor_scalar_mul(csn[:], csp[:], -1.0),
           waits=[(sV, "csp4")], incs=[(sV, 1)], label="csp5")
    for hh in range(H):
        for sp in range(3):
            P.step("gp", dma(cqbuf[hh, sp, :].rearrange("(s t) -> s t", s=8),
                             csp[8 * hh:8 * hh + 8, sp * 512:sp * 512 + 512]),
                   waits=[(sV, "csp5")] if (hh == 0 and sp == 0) else [],
                   incs=[(sSt, 16)], label=f"cq{hh}_{sp}")
            P.step("gp", dma(ckbuf[hh, sp, :].rearrange("(s t) -> s t", s=8),
                             csn[8 * hh:8 * hh + 8, sp * 512:sp * 512 + 512]),
                   incs=[(sSt, 16)], label=f"ck{hh}_{sp}")

    P._ph3 = len(P.steps)
    xr2 = x_d.rearrange("(t s p) d -> t p s d", p=128, s=4)
    yr = y_d.rearrange("(t s p) d -> t s p d", p=128, s=4)

    gg_ctr = [0]
    nrm_ctr = [0]
    msk_ctr = [0]
    sg_last = {}
    pend_nrm = [None]
    pend_av = []
    pend_wo = []
    def flush_av(keep=0):
        while len(pend_av) > keep:
            _fn, _w, _lab = pend_av.pop(0)
            P.step("pe", _fn, waits=_w, incs=[(sP, 1)], label=_lab)

    set_off = [0]
    for t in range(NT):
        w = 512 * (t + 1)
        nch = 4 * (t + 1)
        tile_gg0 = [gg_ctr[0]]
        if t > 0:
            set_off[0] = (2 - gg_ctr[0]) % 3
        def emit_qld(_t):
            wq_ = ([(sSt, f"ck{H-1}_2")] if _t == 0
                   else [(sP, sg_last[(_t - 1, H - 1)])])
            P.step("gp", dma(qwin[0:64, :], qbuf[_t].rearrange("r h t -> r (h t)")),
                   waits=wq_, incs=[(sQ, 16)], label=f"qld{_t}")
            P.step("gp", dma(qwin[64:67, :].rearrange("i (h t) -> i h t", h=H),
                             cqbuf[:, :, 512 * _t:512 * _t + 512]
                             .rearrange("h i t -> i h t")),
                   incs=[(sQ, 16)], label=f"qldc{_t}")
        if t == 0:
            emit_qld(0)
        wx = [(sSt, f"ck{H-1}_2")] if t == 0 else [(sV, f"comb{t-1}_3_1")]
        P.step("gp", dma(xres.rearrange("p (s d) -> p s d", s=4), xr2[t]),
               waits=wx, incs=[(sXR, 16)], label=f"xres{t}")
        wxn = [(sP, f"mmv{NBLK-1}_3_1")] if t == 0 else [(sP, f"ff{t-1}_3B")]
        P.step("gp", dma(xnt2[:], xbuf[t]), waits=wxn, incs=[(sXN, 16)], label=f"xnld{t}")

        for hh in range(H):
            kb = hh % 2
            wk_ = [(sV, "kw0_ones" if kb == 0 else "kw1_ones"), (sSt, f"ck{H-1}_2")]
            if t * H + hh >= 2:
                if hh >= 2:
                    wk_.append((sP, sg_last[(t, hh - 2)]))
                else:
                    wk_.append((sP, sg_last[(t - 1, H - 2 + hh)]))
            sKp = sK0 if kb == 0 else sK1
            P.step("sync", dma(kwin[kb][0:64, 0:w], kbuf[hh, :, 0:w]),
                   waits=wk_, incs=[(sKp, 16)])
            P.step("sync", dma(kwin[kb][67:70, 0:w], ckbuf[hh, :, 0:w]),
                   incs=[(sKp, 16)], label=f"kld{t}_{hh}")

            ngr = (nch + G - 1) // G
            for g in range(ngr):
                k0, k1 = G * g, min(G * g + G, nch)
                gg = gg_ctr[0]
                gset = (gg + set_off[0]) % 3
                pgrp = psG[:, 1024 * gset:1024 * gset + 1024]
                ptl = ptile[gset]

                def mms(ten, _k0=k0, _k1=k1, _p=pgrp, _hh=hh):
                    ins = None
                    for kk in range(_k0, _k1):
                        ins = ten.matmul(_p[:, 512 * (kk - _k0):512 * (kk - _k0) + 512],
                                         kwin[_hh % 2][:, 128 * kk:128 * kk + 128],
                                         qwin[:, 512 * _hh:512 * _hh + 512],
                                         start=True, stop=True)
                    return ins
                wts = []
                if g == 0:
                    wts += [(sK0 if hh % 2 == 0 else sK1, f"kld{t}_{hh}"),
                            (sQ, f"qldc{t}")]
                if gg >= 3:
                    wts.append((sA, f"exp{gg-3}"))
                if gg == 0:
                    wts.append((sA, f"evk{NBLK-1}_1_7"))
                    wts.append((sA, f"evf{NBLK-1}"))
                if gg == 1:
                    wts.append((sA, f"evv{NBLK-1}_3_1"))
                if gg == 2:
                    wts.append((sA, f"evx{NBLK-1}_31"))
                if t > 0 and (gg - tile_gg0[0]) < 3 and True:
                    wts.append((sV, f"comb{t-1}_3_1"))
                P.step("pe", mms, waits=wts, incs=[(sP, 1)], label=f"sg{gg}")
                flush_av(keep=2)
                if g == 0:
                    flush_av(keep=0)
                    if pend_nrm[0] is not None:
                        _enrm, _avlab = pend_nrm[0]
                        pend_nrm[0] = None
                        _enrm(_avlab=_avlab)
                if g == ngr - 1:
                    sg_last[(t, hh)] = f"sg{gg}"

                def expg(sc, _p=pgrp, _pt=ptl, _n=512 * (k1 - k0)):
                    return sc.activation(_pt[:, 0:_n], _p[:, 0:_n], AF.Exp)
                we = [(sP, f"sg{gg}")]
                if gg >= 3:
                    we.append((sP, f"avg{gg-3}"))
                P.step("act", expg, waits=we, incs=[(sA, 1)], label=f"exp{gg}")

                mlabs = []
                for kk in range(k0, k1):
                    m = kk - 4 * t
                    if m >= 0:
                        msk_ctr[0] += 1
                        def msk(v, _pt=ptl, _off=512 * (kk - k0), _m=m):
                            return v.tensor_tensor(
                                _pt[:, _off:_off + 512], _pt[:, _off:_off + 512],
                                master[:, 384 - 128 * _m:896 - 128 * _m], ALU.min)
                        P.step("dve", msk, waits=[(sA, f"exp{gg}")],
                               incs=[(sV, 1)], label=f"msk{msk_ctr[0]}")
                        mlabs.append(f"msk{msk_ctr[0]}")

                def av(ten, _k0=k0, _k1=k1, _pt=ptl, _hh=hh, _nch=nch):
                    pa = pacc if _hh % 2 == 0 else pex[:, 0:264]
                    ins = None
                    for kk in range(_k0, _k1):
                        for s4 in range(4):
                            ins = ten.matmul(
                                pa[:, 66 * s4:66 * s4 + 65],
                                _pt[:, 512 * (kk - _k0) + 128 * s4:
                                    512 * (kk - _k0) + 128 * s4 + 128],
                                vst[:, kk, 65 * _hh:65 * _hh + 65],
                                start=(kk == 0 and s4 == 0),
                                stop=(kk == _nch - 1 and s4 == 3),
                                skip_group_check=True)
                    return ins
                wav = [(sA, f"exp{gg}")] + [(sV, ml) for ml in mlabs]
                if g == 0:
                    if nrm_ctr[0] >= 1 + (1 if hh >= 2 else 0):
                        pass
                    if hh >= 2:
                        wav.append((sV, f"nrm{(t * H + hh) - 2 + 1}"))
                    elif t > 0:
                        if hh == 0:
                            wav.append((sV, f"nrm{(t - 1) * H + H - 2 + 1}"))
                        else:
                            wav.append((sV, f"comb{t-1}_3_1"))
                    elif t == 0 and hh == 1:
                        wav.append((sV, "c_offc"))
                pend_av.append((av, wav, f"avg{gg}"))
                gg_ctr[0] += 1

            def emit_nrm(_t=t, _hh=hh, _avlab=None):
                pa = pacc if _hh % 2 == 0 else pex[:, 0:264]
                P.step("dve", (lambda v, _pa=pa: v.tensor_copy(
                           lsb[:], _pa.rearrange("p (s c) -> p s c", c=66)[:, :, 64])),
                       waits=[(sP, _avlab)], incs=[(sV, 1)],
                       label=f"nrmA{_t}_{_hh}")
                def nrm_b(v):
                    ins = None
                    for s4 in range(4):
                        ins = v.reciprocal(recs[s4][:], lsb[:, s4:s4 + 1])
                    return ins
                P.step("dve", nrm_b, waits=[(sV, f"nrmA{_t}_{_hh}")], incs=[(sV, 1)],
                       label=f"nrmB{_t}_{_hh}")
                def nrm_c(v):
                    ins = None
                    for s4 in range(4):
                        ins = v.tensor_scalar_mul(
                            otok[:, 1024 * s4 + 64 * _hh:1024 * s4 + 64 * _hh + 64],
                            pa[:, 66 * s4:66 * s4 + 64], recs[s4][:, 0:1])
                    return ins
                wn = [(sV, f"nrmB{_t}_{_hh}")]
                if _t == 0 and _hh == 0:
                    wn.append((sP, f"trxend{NBLK-1}"))
                if _t > 0 and _hh == 0:
                    wn.append((sP, f"odone{_t-1}"))
                nrm_ctr[0] += 1
                P.step("dve", nrm_c, waits=wn, incs=[(sV, 1)], label=f"nrm{nrm_ctr[0]}")
            pend_nrm[0] = (emit_nrm, f"avg{gg_ctr[0]-1}")

        if _DBG_DUMP and t == 0:
            P.step("gp", dma(dbgo_d, otok[:]), waits=[(sV, f"nrm{nrm_ctr[0]}")],
                   incs=[(sSt, 16)])
        flush_av(keep=0)
        if pend_nrm[0] is not None:
            _enrm, _avlab = pend_nrm[0]
            pend_nrm[0] = None
            _enrm(_avlab=_avlab)
        nrm_all = f"nrm{nrm_ctr[0]}"
        if t + 1 < NT:
            emit_qld(t + 1)
        pst4 = [psG[:, 2048 + 256 * j:2048 + 256 * j + 256].bitcast(BF16)
                for j in range(4)]
        pstall = psG[:, 2048:3072].bitcast(BF16).rearrange("p (b k) -> p b k",
                                                           k=512)
        for s4 in range(4):
            for h in range(2):
                rid = s4 * 2 + h
                for j in range(4):
                    c = 4 * h + j
                    wtt = [(sV, nrm_all)]
                    if rid >= 1:
                        wtt.append((sV, f"oev{t}_{s4}_{h-1}" if h == 1
                                    else f"oev{t}_{s4-1}_1"))
                    P.step("pe", (lambda ten, _s=s4, _c=c, _p=pst4[j]:
                                  ten.transpose(_p[:, 0:128],
                                                otok[:, 1024 * _s + 128 * _c:
                                                     1024 * _s + 128 * _c + 128],
                                                identb[:])),
                           waits=wtt, incs=[(sP, 1)], label=f"otr{t}_{s4}_{h}_{j}")
                if pend_wo:
                    _f, _w, _l = pend_wo.pop(0)
                    P.step("pe", _f, waits=_w, incs=[(sP, 1)], label=_l)
                woev = [(sP, f"otr{t}_{s4}_{h}_3")]
                if rid >= 1:
                    woev.append((sP, f"wo{t}_{s4}_{h-1}" if h == 1
                                 else f"wo{t}_{s4-1}_1"))
                P.step("dve", (lambda v:
                               v.tensor_copy(otst.rearrange("p (s k) -> p s k",
                                                            k=128),
                                             pstall[:, :, 0:128])),
                       waits=woev, incs=[(sV, 1)], label=f"oev{t}_{s4}_{h}")
                def mwo(ten, _h=h):
                    ins = None
                    for j2 in range(4):
                        cc = 4 * _h + j2
                        for n in range(2):
                            ins = ten.matmul(psY[n],
                                             otst[:, 128 * j2:128 * j2 + 128],
                                             w2[:, 1024 * cc + 512 * n:
                                                1024 * cc + 512 * n + 512],
                                             start=(_h == 0 and j2 == 0),
                                             stop=(_h == 1 and j2 == 3),
                                             skip_group_check=True)
                    return ins
                wwo = [(sV, f"oev{t}_{s4}_{h}"), (sSt, "ld_wo")]
                if h == 0:
                    if s4 > 0:
                        wwo.append((sV, f"comb{t}_{s4-1}_1"))
                    elif t > 0:
                        wwo.append((sV, f"comb{t-1}_3_1"))
                pend_wo.append((mwo, wwo, f"wo{t}_{s4}_{h}"))
            while pend_wo:
                _f, _w, _l = pend_wo.pop(0)
                P.step("pe", _f, waits=_w, incs=[(sP, 1)], label=_l)
            if s4 == 3:
                P.step("pe", lambda ten: ten.nop(), waits=[(sP, f"otr{t}_3_1_3")],
                       incs=[(sP, 1)], label=f"odone{t}")

            def ffa(ten, _s=s4):
                ins = None
                for c in range(4):
                    ten.matmul(psF[0],
                               xnt2[:, c * 512 + _s * 128:c * 512 + _s * 128 + 128],
                               wtr[:, c * 512:c * 512 + 512],
                               start=(c == 0), stop=False, skip_group_check=True)
                ten.matmul(psF[0], ones2[:], bias2r[:], start=False, stop=True,
                           skip_group_check=True)
                for c in range(4):
                    ins = ten.matmul(
                        pex[:, 0:512],
                        xnt2[:, (c + 4) * 512 + _s * 128:(c + 4) * 512 + _s * 128 + 128],
                        wti[:, c * 512:c * 512 + 512],
                        start=(c == 0), stop=(c == 3), skip_group_check=True)
                return ins
            def ffb(ten, _s=s4):
                for c in range(8):
                    wt_t = wti if c < 4 else wtr
                    ten.matmul(psF[1],
                               xnt2[:, c * 512 + _s * 128:c * 512 + _s * 128 + 128],
                               wt_t[:, (c % 4) * 512:(c % 4) * 512 + 512],
                               start=(c == 0), stop=False, skip_group_check=True)
                return ten.matmul(psF[1], ones2[:], bias2i[:], start=False,
                                  stop=True, skip_group_check=True)
            wff = [(sV, "wtr"), (sV, "wti"), (sXN, f"xnld{t}"), (sSt, "bsp_i"),
                   (sV, nrm_all)]
            if s4 > 0:
                wff.append((sV, f"comb{t}_{s4-1}_1"))
            elif t > 0:
                wff.append((sV, f"comb{t-1}_3_1"))
            else:
                wff.append((sV, "c_offc"))
            P.step("pe", ffa, waits=wff, incs=[(sP, 1)], label=f"ff{t}_{s4}A")
            P.step("pe", ffb, waits=[], incs=[(sP, 1)], label=f"ff{t}_{s4}B")

            for n in range(2):
                def comb(v, _s=s4, _n=n):
                    yb = ybuf[_s % 2]
                    v.scalar_tensor_tensor(yb[:, 512 * _n:512 * _n + 512],
                                           psY[_n], 1.0,
                                           xres[:, 1024 * _s + 512 * _n:
                                                1024 * _s + 512 * _n + 512],
                                           ALU.mult, ALU.add)
                    if _n == 0:
                        v.scalar_tensor_tensor(yb[:, 0:512], pex[:, 0:512], -1.0,
                                               yb[:, 0:512], ALU.mult, ALU.add)
                    return v.scalar_tensor_tensor(yb[:, 512 * _n:512 * _n + 512],
                                                  psF[_n], 1.0,
                                                  yb[:, 512 * _n:512 * _n + 512],
                                                  ALU.mult, ALU.add)
                wc = [(sP, f"wo{t}_{s4}_1"), (sXR, f"xres{t}"),
                      (sP, f"ff{t}_{s4}A" if n == 0 else f"ff{t}_{s4}B")]
                idx = 4 * t + s4
                if idx >= 2:
                    wc.append((sY, 16 * (idx - 1)))
                P.step("dve", comb, waits=wc, incs=[(sV, 1)], label=f"comb{t}_{s4}_{n}")
            P.step("gp", dma(yr[t, s4], ybuf[s4 % 2][:]),
                   waits=[(sV, f"comb{t}_{s4}_1")], incs=[(sY, 16)],
                   label=f"yst{t}_{s4}")

    P.step("gp", lambda g_: g_.nop(), waits=[(sY, 16 * 4 * NT)])

    if _DBG_PHASE < 4:
        cut = [P._ph0, P._ph1, P._ph2, P._ph3][_DBG_PHASE]
        P.steps = P.steps[:cut]
    if _DBG_NSTEPS is not None:
        P.steps = P.steps[:_DBG_NSTEPS]
    P.resolve()

    with nc.Block() as block:
        @block.sync
        def _(sync):
            P.emit("sync", sync)

        @block.vector
        def _(vector):
            P.emit("dve", vector)

        @block.scalar
        def _(scalar):
            P.emit("act", scalar)

        @block.tensor
        def _(tensor):
            P.emit("pe", tensor)

        @block.gpsimd
        def _(gpsimd):
            P.emit("gp", gpsimd)

    es.close()
    return nc


def kernel(L0_attn_x, L0_attn_Wq, L0_attn_Wk, L0_attn_Wv, L0_attn_Wo, L0_attn_Wf,
           L0_ffn_G1_R, L0_ffn_G2_R, L0_ffn_G1_I, L0_ffn_G2_I,
           L0_ffn_bias_real, L0_ffn_bias_imag):
    x = np.ascontiguousarray(np.asarray(L0_attn_x, dtype=np.float32))
    com = {
        "wq": np.ascontiguousarray(np.asarray(L0_attn_Wq, dtype=np.float32)),
        "wk": np.ascontiguousarray(np.asarray(L0_attn_Wk, dtype=np.float32)),
        "wv": np.ascontiguousarray(np.asarray(L0_attn_Wv, dtype=np.float32)),
        "wo": np.ascontiguousarray(np.asarray(L0_attn_Wo, dtype=np.float32)),
        "wf": np.ascontiguousarray(np.asarray(L0_attn_Wf, dtype=np.float32)),
        "g1r": np.ascontiguousarray(np.asarray(L0_ffn_G1_R, dtype=np.float32)),
        "g2r": np.ascontiguousarray(np.asarray(L0_ffn_G2_R, dtype=np.float32)),
        "g1i": np.ascontiguousarray(np.asarray(L0_ffn_G1_I, dtype=np.float32)),
        "g2i": np.ascontiguousarray(np.asarray(L0_ffn_G2_I, dtype=np.float32)),
        "br": np.ascontiguousarray(np.asarray(L0_ffn_bias_real, dtype=np.float32)),
        "bi": np.ascontiguousarray(np.asarray(L0_ffn_bias_imag, dtype=np.float32)),
    }
    nc = build_nc()
    in_maps = [dict(com, x=np.ascontiguousarray(x[b])) for b in range(8)]
    res = run_bass_kernel_spmd(nc, in_maps, core_ids=list(range(8)))
    return np.stack([res.results[b]["y"] for b in range(8)], axis=0)


# version 29 (final)
# speedup vs baseline: 1.2630x; 1.0015x over previous
import numpy as np
import ml_dtypes
from contextlib import ExitStack

import concourse.bass as bass
import concourse.mybir as mybir
from concourse.bass_utils import run_bass_kernel_spmd

F32 = mybir.dt.float32
BF16 = mybir.dt.bfloat16
AF = mybir.ActivationFunctionType
ALU = mybir.AluOpType
AX = mybir.AxisListType
BF = ml_dtypes.bfloat16

_DBG_PHASE = 4
_DBG_DUMP = False
_DBG_NSTEPS = None

N = 4096
D = 1024
H = 16
HD = 64
NBLK = 8
NT = 8
G = 2


class Plan:

    def __init__(self):
        self.steps = []
        self.labels = {}

    def step(self, eng, fn, waits=(), incs=(), label=None, serialize=False):
        self.steps.append([eng, fn, list(waits), list(incs), label, serialize])

    def resolve(self):
        cnt = {}
        for st in self.steps:
            _eng, _fn, waits, incs, label, serialize = st
            rw = []
            if serialize and incs:
                sem0 = incs[0][0]
                rw.append((sem0, cnt.get(id(sem0), 0)))
            for sem, lab in waits:
                if isinstance(lab, int):
                    rw.append((sem, lab))
                else:
                    s2, v = self.labels[lab]
                    assert s2 is sem, f"label {lab} bound to a different sem"
                    rw.append((sem, v))
            st[2] = rw
            for sem, n in incs:
                cnt[id(sem)] = cnt.get(id(sem), 0) + n
            if label is not None:
                assert len(incs) == 1, f"labeled step {label} must have exactly one inc"
                self.labels[label] = (incs[0][0], cnt[id(incs[0][0])])

    def emit(self, eng_name, eng):
        hi = {}
        for eng2, fn, waits, incs, _label, _ser in self.steps:
            if eng2 != eng_name:
                continue
            merged = {}
            for sem, val in waits:
                k = id(sem)
                if val > merged.get(k, (None, -1))[1]:
                    merged[k] = (sem, val)
            for k, (sem, val) in merged.items():
                if val > hi.get(k, -1):
                    eng.wait_ge(sem, val)
                    hi[k] = val
            ins = fn(eng)
            for sem, n in incs:
                assert ins is not None, "step with incs must return an instruction"
                ins.then_inc(sem, n)


def _consts():
    ident = np.eye(128, dtype=BF)
    identf = np.eye(128, dtype=np.float32)
    uu = np.arange(896)[None, :] - 384
    jj = np.arange(128)[:, None]
    master = np.where(jj > uu, 0.0, 3.0e38).astype(BF)
    ones3 = np.ones((3, H * 512), dtype=BF)
    ones2 = np.ones((2, 128), dtype=BF)
    p = np.arange(512)
    comb_d = (p[None, :] // 16 == np.arange(32)[:, None]).astype(np.float32)
    p2 = np.arange(128)
    comb_g = (p2[None, :] % 16 == np.arange(16)[:, None]).astype(np.float32)
    k = np.arange(128)
    Ltri = ((k[:, None] // 8 == k[None, :] // 8) &
            (k[:, None] % 8 < k[None, :] % 8)).astype(np.float32)
    return ident, identf, master, ones3, ones2, comb_d, comb_g, Ltri


def build_nc():
    nc = bass.Bass("TRN2", target_bir_lowering=False)

    x_d = nc.dram_tensor("x", [N, D], F32, kind="ExternalInput").ap()
    wq_d = nc.dram_tensor("wq", [D, D], F32, kind="ExternalInput").ap()
    wk_d = nc.dram_tensor("wk", [D, D], F32, kind="ExternalInput").ap()
    wv_d = nc.dram_tensor("wv", [D, D], F32, kind="ExternalInput").ap()
    wo_d = nc.dram_tensor("wo", [D, D], F32, kind="ExternalInput").ap()
    wf_d = nc.dram_tensor("wf", [D, H], F32, kind="ExternalInput").ap()
    g1r_d = nc.dram_tensor("g1r", [1, 32, 4, 32], F32, kind="ExternalInput").ap()
    g2r_d = nc.dram_tensor("g2r", [4, 16, 1, 16], F32, kind="ExternalInput").ap()
    g1i_d = nc.dram_tensor("g1i", [1, 32, 4, 32], F32, kind="ExternalInput").ap()
    g2i_d = nc.dram_tensor("g2i", [4, 16, 1, 16], F32, kind="ExternalInput").ap()
    br_d = nc.dram_tensor("br", [512], F32, kind="ExternalInput").ap()
    bi_d = nc.dram_tensor("bi", [512], F32, kind="ExternalInput").ap()
    y_d = nc.dram_tensor("y", [N, D], F32, kind="ExternalOutput").ap()

    knd = "ExternalOutput" if _DBG_DUMP else "Internal"
    if _DBG_DUMP:
        dbgp_d = nc.dram_tensor("dbgp", [2, 128, G * 512], BF16, kind="ExternalOutput").ap()
        dbgo_d = nc.dram_tensor("dbgo", [128, 4096], BF16, kind="ExternalOutput").ap()
        dbgw_d = nc.dram_tensor("dbgw", [2, 128, 2048], BF16, kind="ExternalOutput").ap()
    kbuf = nc.dram_tensor("kbuf", [H, HD, N], BF16, kind=knd).ap()
    qbuf = nc.dram_tensor("qbuf", [NT, HD, H, 512], BF16, kind=knd).ap()
    cqbuf = nc.dram_tensor("cqbuf", [H, 3, N], BF16, kind=knd).ap()
    ckbuf = nc.dram_tensor("ckbuf", [H, 3, N], BF16, kind=knd).ap()
    xbuf = nc.dram_tensor("xbuf", [NBLK, 128, 8 * 512], BF16, kind=knd).ap()
    fbuf = nc.dram_tensor("fbuf", [128, 512], F32, kind=knd).ap()

    ident_np, identf_np, master_np, ones3_np, ones2_np, comb_d_np, comb_g_np, Ltri_np = _consts()
    identb_d = nc.inline_tensor(ident_np, "identb").ap()
    identf_d = nc.inline_tensor(identf_np, "identf").ap()
    master_d = nc.inline_tensor(master_np, "masterc").ap()
    ones3_d = nc.inline_tensor(ones3_np, "ones3c").ap()
    ones2_d = nc.inline_tensor(ones2_np, "ones2c").ap()
    comb_d_d = nc.inline_tensor(comb_d_np, "combdc").ap()
    comb_g_d = nc.inline_tensor(comb_g_np, "combgc").ap()
    Ltri_d = nc.inline_tensor(Ltri_np, "ltric").ap()

    es = ExitStack()
    sb = lambda name, shape, dt: es.enter_context(nc.sbuf_tensor(name, shape, dt))
    psalloc = lambda name, shape, dt: es.enter_context(nc.psum_tensor(name, shape, dt))

    vst = sb("vst", [128, 32, H * 65], BF16)
    w1 = sb("w1", [128, 8 * 1024], BF16)
    w2 = sb("w2", [128, 8 * 1024], BF16)
    wfb = sb("wfb", [128, 8 * 16], BF16)
    kwin = [sb(f"kwin{i}", [128, N], BF16) for i in range(2)]
    qwin = sb("qwin", [128, H * 512], BF16)
    xres = sb("xres", [128, 4 * 1024], F32)
    xnt2 = sb("xnt2", [128, 8 * 512], BF16)
    otok = sb("otok", [128, 4 * 1024], BF16)
    ptile = [sb(f"ptile{i}", [128, G * 512], BF16) for i in range(3)]
    ybuf = [sb(f"ybuf{i}", [128, 1024], F32) for i in range(2)]
    xblk0 = sb("xblk0", [128, 4 * 1024], F32)
    xblk = [xblk0, xblk0]
    wtr = sb("wtr", [128, 4 * 512], BF16)
    wti = sb("wti", [128, 4 * 512], BF16)
    bias2r = sb("bias2r", [2, 512], BF16)
    bias2i = sb("bias2i", [2, 512], BF16)
    identb = sb("identb_sb", [128, 128], BF16)
    master = sb("master_sb", [128, 896], BF16)
    ones2 = sb("ones2_sb", [2, 128], BF16)
    epst = sb("epst", [128, 1], F32)
    onecol = sb("onecol", [128, 1], F32)
    sum_s = [sb(f"sum_s{i}", [128, 1], F32) for i in range(4)]
    ssq_s = [sb(f"ssq_s{i}", [128, 1], F32) for i in range(4)]
    m_s = [sb(f"m_s{i}", [128, 1], F32) for i in range(4)]
    vr_s = [sb(f"vr_s{i}", [128, 1], F32) for i in range(4)]
    t_s = [sb(f"t_s{i}", [128, 1], F32) for i in range(4)]
    var_s = [sb(f"var_s{i}", [128, 1], F32) for i in range(4)]
    rstd_s = [sb(f"rstd_s{i}", [128, 1], F32) for i in range(4)]
    nb_s = [sb(f"nb_s{i}", [128, 1], F32) for i in range(4)]
    lsb = sb("lsb", [128, 4], F32)
    recs = [sb(f"recs{i}", [128, 1], F32) for i in range(4)]
    offs = sb("offs", [128, 1], F32)
    totals = sb("totals", [128, 1], F32)
    _kw0_addr = nc.lookup_mloc(kwin[0]).addr
    _kw1_addr = nc.lookup_mloc(kwin[1]).addr
    xnt1s = [nc.alloc_sbuf_tensor_at("xnt1a", [128, 8 * 512], BF16, offset=_kw0_addr).ap(),
             nc.alloc_sbuf_tensor_at("xnt1b", [128, 8 * 512], BF16, offset=_kw1_addr).ap()]
    otst = ptile[2][:, 0:512]
    xnst = otok
    uae = xres[:, 0:2048]
    ube = xres[:, 2048:4096]
    prod = xnst[:].bitcast(F32)
    xnt2f = xnt2[:].bitcast(F32)
    ua_sb = xnt2f[:, 0:512]
    wtf32 = xnt2f[:, 512:1024]
    ub_sb = xnt2f[:, 1024:1088]
    at_sb = xnt2f[0:32, 1088:1216]
    bt_sb = xnt2f[0:16, 1216:1280]
    gsta = xnt2f[:, 1280:1408]
    combd = xnt2f[0:32, 1408:1920]
    combg = xnt2f[0:16, 1920:2048]
    ybf1 = ybuf[1].bitcast(F32) if False else ybuf[1]
    ltri = ybuf[1][:, 512:640]
    identf = ybuf[1][:, 640:768]
    kst4 = [ptile[i // 2][:, (i % 2) * 512:(i % 2) * 512 + 512]
            for i in range(6)]
    qwf = qwin[:].bitcast(F32)
    bspl = qwin[0:1, 0:1024]
    bf32 = qwf[0:1, 512:1024]
    btmp = qwf[0:1, 1024:1536]
    btmp2 = qwf[0:1, 1536:2048]
    xrf = xres
    f128 = xrf[:, 0:512]
    cst = [xrf[:, 512:1024], xrf[:, 1024:1536]]
    casc = xrf[:, 1536:2048]
    ctmp = xrf[:, 2048:2560]
    csp = xres[:].bitcast(BF16)[:, 5120:6656]
    csn = xres[:].bitcast(BF16)[:, 6656:8192]
    sq_s = ybuf[0][:, 0:1024]
    fst = ybuf[1][0:16, 0:512]

    psG = psalloc("psG", [128, 3072], F32)
    pacc = psalloc("pacc", [128, 264], F32)
    pex = psalloc("pex", [128, 512], F32)
    psA = psG[:, 0:1536]
    psB = psG[:, 1536:3072]
    pstA = psG[:, 2048:2304].bitcast(BF16)
    pstB = psG[:, 2560:2816].bitcast(BF16)
    pstC = psG[:, 1024:1280].bitcast(BF16)
    pstD = psG[:, 1536:1792].bitcast(BF16)
    psY = [psG[:, 0:512], psG[:, 512:1024]]
    psF = [psG[:, 1024:1536], psG[:, 1536:2048]]

    sW = es.enter_context(nc.semaphore("sW"))
    sX = es.enter_context(nc.semaphore("sX"))
    sK0 = es.enter_context(nc.semaphore("sK0"))
    sK1 = es.enter_context(nc.semaphore("sK1"))
    sQ = es.enter_context(nc.semaphore("sQ"))
    sXR = es.enter_context(nc.semaphore("sXR"))
    sXN = es.enter_context(nc.semaphore("sXN"))
    sXB = es.enter_context(nc.semaphore("sXB"))
    sSt = es.enter_context(nc.semaphore("sSt"))
    sY = es.enter_context(nc.semaphore("sYm"))
    sP = es.enter_context(nc.semaphore("sP"))
    sA = es.enter_context(nc.semaphore("sA"))
    sV = es.enter_context(nc.semaphore("sV"))

    P = Plan()

    def dma(out, in_, **kw):
        return lambda eng: eng.dma_start(out=out, in_=in_, **kw)

    P.step("dve", lambda v: v.memset(qwin[64:128, :], 0.0), incs=[(sV, 1)],
           label="qw_zero")
    P.step("sync", dma(identb[:], identb_d), incs=[(sW, 16)], serialize=True)
    P.step("sync", dma(identf[:], identf_d), incs=[(sW, 16)], serialize=True)
    P.step("sync", dma(master[:], master_d), incs=[(sW, 16)], serialize=True)
    P.step("sync", dma(qwin[67:70, :], ones3_d), waits=[(sV, "qw_zero")],
           incs=[(sW, 16)], serialize=True)
    P.step("sync", dma(ones2[:], ones2_d), incs=[(sW, 16)], serialize=True)
    P.step("sync", dma(combd[:], comb_d_d), incs=[(sW, 16)], serialize=True)
    P.step("sync", dma(combg[:], comb_g_d), incs=[(sW, 16)], serialize=True)
    P.step("sync", dma(ltri[:], Ltri_d), incs=[(sW, 16)], label="ld_small", serialize=True)
    wk_r = wk_d.rearrange("(c p) n -> p c n", p=128)
    wq_r = wq_d.rearrange("(c p) n -> p c n", p=128)
    wf_r = wf_d.rearrange("(c p) n -> p c n", p=128)
    P.step("gp", dma(w1.rearrange("p (c n) -> p c n", c=8), wk_r),
           incs=[(sSt, 16)], label="ld_wk", serialize=True)
    P.step("gp", dma(w2.rearrange("p (c n) -> p c n", c=8), wq_r),
           incs=[(sSt, 16)], label="ld_wq", serialize=True)
    P.step("gp", dma(wfb.rearrange("p (c n) -> p c n", c=8), wf_r),
           incs=[(sSt, 16)], label="ld_wf", serialize=True)
    P.step("sync", dma(gsta[:, 0:32], g1r_d.rearrange("a b c d -> (a b c) d")),
           incs=[(sW, 16)], label="ld_g1r", serialize=True)
    P.step("sync", dma(gsta[:, 32:64], g1i_d.rearrange("a b c d -> (a b c) d")),
           incs=[(sW, 16)], label="ld_g1i", serialize=True)
    P.step("sync", dma(gsta[0:64, 64:80], g2r_d.rearrange("c e a g -> (c e a) g")),
           incs=[(sW, 16)], label="ld_g2r", serialize=True)
    P.step("sync", dma(gsta[0:64, 80:96], g2i_d.rearrange("c e a g -> (c e a) g")),
           incs=[(sW, 16)], label="ld_g2i", serialize=True)
    P.step("sync", dma(bf32[:], br_d.rearrange("(a n) -> a n", a=1)),
           incs=[(sW, 16)], label="ld_br", serialize=True)

    P.step("dve", lambda v: v.memset(epst[:], 1e-5), waits=[(sW, "ld_small")],
           incs=[(sV, 1)], label="eps_set")
    P.step("dve", lambda v: v.memset(onecol[:], 1.0))
    P.step("dve", lambda v: v.memset(vst[:], 1.0), incs=[(sV, 1)], label="vst_ones")
    P.step("dve", lambda v: v.tensor_scalar_mul(w2[:], w2[:], 0.125),
           waits=[(sSt, "ld_wq")], incs=[(sV, 1)], label="wq_scaled")

    def build_wt(tag, g1_cols, g2_cols, wt_tile, neg_tile):
        wtr1 = [(sW, "ld_g1" + tag), (sW, "ld_small")]
        if tag == "i":
            wtr1.append((sV, "ubcr"))
        P.step("pe", (lambda t, _a=g1_cols:
                      t.transpose(pex[0:32, 0:128], gsta[:, _a[0]:_a[1]], identf[:])),
               waits=wtr1, incs=[(sP, 1)], label=f"tr_g1{tag}")
        P.step("dve", lambda v: v.tensor_copy(at_sb[:], pex[0:32, 0:128]),
               waits=[(sP, f"tr_g1{tag}")], incs=[(sV, 1)], label=f"at{tag}")
        P.step("pe", (lambda t, _a=g2_cols:
                      t.transpose(pex[0:16, 128:192], gsta[0:64, _a[0]:_a[1]], identf[0:64, 0:64])),
               waits=[(sW, "ld_g2" + tag), (sV, f"at{tag}")],
               incs=[(sP, 1)], label=f"tr_g2{tag}")
        P.step("dve", lambda v: v.tensor_copy(bt_sb[:], pex[0:16, 128:192]),
               waits=[(sP, f"tr_g2{tag}")], incs=[(sV, 1)], label=f"bt{tag}")

        def mk_ua(t):
            ins = None
            for p4 in range(4):
                ins = t.matmul(pex[:, p4 * 128:p4 * 128 + 128],
                               combd[:, p4 * 128:p4 * 128 + 128], at_sb[:],
                               start=True, stop=True)
            return ins
        P.step("pe", mk_ua, waits=[(sV, f"at{tag}"), (sV, f"bt{tag}")],
               incs=[(sP, 1)], label=f"ua{tag}")
        P.step("dve", lambda v: v.tensor_copy(ua_sb[:], pex[:, 0:512]),
               waits=[(sP, f"ua{tag}")], incs=[(sV, 1)], label=f"uac{tag}")
        P.step("pe", lambda t: t.matmul(pex[:, 0:64], combg[:], bt_sb[:],
                                        start=True, stop=True),
               waits=[(sV, f"uac{tag}")], incs=[(sP, 1)], label=f"ub{tag}")
        P.step("dve", lambda v: v.tensor_copy(ub_sb[:], pex[:, 0:64]),
               waits=[(sP, f"ub{tag}")], incs=[(sV, 1)], label=f"ubc{tag}")

        def mk_wt_chunk(v, p4, _wt=wt_tile, _neg=neg_tile):
            ua_r = ua_sb[:, p4 * 128:p4 * 128 + 128].rearrange("p (b c) -> p b c", c=4)
            uae_r = uae.rearrange("p (b c e) -> p e b c", c=4, e=16)
            for e in range(16):
                v.tensor_copy(uae_r[:, e], ua_r)
            ub_r = ub_sb[:, 0:64].rearrange("p (c e) -> p c e", c=4)
            ube_r = ube.rearrange("p (b c e) -> p b c e", c=4, e=16)
            for b8 in range(32):
                v.tensor_copy(ube_r[:, b8], ub_r)
            ins = v.tensor_mul(prod, uae, ube)
            return ins
        for p4 in range(4):
            P.step("dve", (lambda v, _p4=p4: mk_wt_chunk(v, _p4)),
                   waits=[(sV, f"ubc{tag}")] if p4 == 0 else [(sV, f"wtred{tag}{p4-1}")],
                   incs=[(sV, 1)], label=f"wtp{tag}{p4}")
            def red(v, _p4=p4, _wt=wt_tile, _neg=neg_tile):
                pr2 = prod.rearrange("p (b c e) -> p b c e", c=4, e=16)
                wv_ = wtf32.rearrange("p (b e) -> p b e", e=16)
                v.tensor_tensor(wv_, pr2[:, :, 0], pr2[:, :, 1], ALU.add)
                v.tensor_tensor(wv_, wv_, pr2[:, :, 2], ALU.add)
                v.tensor_tensor(wv_, wv_, pr2[:, :, 3], ALU.add)
                return v.tensor_copy(_wt[:, _p4 * 512:_p4 * 512 + 512], wtf32[:])
            P.step("dve", red, waits=[(sV, f"wtp{tag}{p4}")],
                   incs=[(sV, 1)], label=f"wtred{tag}{p4}")
        P.step("dve", lambda v: v.tensor_copy(btmp[:], btmp[:]),
               waits=[(sV, f"wtred{tag}3")], incs=[(sV, 1)], label=f"wt{tag}")

    build_wt("r", (0, 32), (64, 80), wtr, None)
    build_wt("i", (32, 64), (80, 96), wti, None)
    if _DBG_DUMP:
        P.step("gp", dma(dbgw_d[0], wtr[:]), waits=[(sV, "wtr")], incs=[(sSt, 16)])
        P.step("gp", dma(dbgw_d[1], wti[:]), waits=[(sV, "wti")], incs=[(sSt, 16)])

    def bias_split(v):
        v.tensor_copy(bspl[0:1, 0:512], bf32[:])
        v.tensor_copy(btmp[:], bspl[0:1, 0:512])
        v.tensor_sub(btmp2[:], bf32[:], btmp[:])
        return v.tensor_copy(bspl[0:1, 512:1024], btmp2[:])
    P.step("dve", bias_split, waits=[(sW, "ld_br")], incs=[(sV, 1)], label="bsp_r0")
    P.step("gp", dma(bias2r[0:1, :], bspl[0:1, 0:512]), waits=[(sV, "bsp_r0")],
           incs=[(sSt, 16)], serialize=True)
    P.step("gp", dma(bias2r[1:2, :], bspl[0:1, 512:1024]), incs=[(sSt, 16)], label="bsp_r", serialize=True)
    P.step("dve", lambda v: v.tensor_copy(btmp2[:], btmp2[:]),
           waits=[(sSt, "bsp_r")], incs=[(sV, 1)], label="bsplit_r")
    P.step("sync", dma(bf32[:], bi_d.rearrange("(a n) -> a n", a=1)),
           waits=[(sV, "bsplit_r")], incs=[(sW, 16)], label="ld_bi", serialize=True)
    P.step("dve", bias_split, waits=[(sW, "ld_bi")], incs=[(sV, 1)], label="bsp_i0")
    P.step("gp", dma(bias2i[0:1, :], bspl[0:1, 0:512]), waits=[(sV, "bsp_i0")],
           incs=[(sSt, 16)], serialize=True)
    P.step("gp", dma(bias2i[1:2, :], bspl[0:1, 512:1024]), incs=[(sSt, 16)], label="bsp_i", serialize=True)

    P._ph0 = len(P.steps)
    xr = x_d.rearrange("(b s p) d -> b p s d", p=128, s=4)

    def emit_front(b):
        bb = b % 2
        w = [(sV, f"aff{b-1}")] if b >= 1 else []
        P.step("sync", dma(xblk[bb].rearrange("p (s d) -> p s d", s=4), xr[b]),
               waits=w, incs=[(sX, 16)], label=f"xld{b}")

        def ln_stats(v, _bb=bb):
            ins = None
            for s in range(4):
                xsub = xblk[_bb][:, s * 1024:(s + 1) * 1024]
                v.reduce_sum(sum_s[s][:], xsub, axis=AX.X)
                v.tensor_mul(sq_s[:], xsub, xsub)
                ins = v.reduce_sum(ssq_s[s][:], sq_s[:], axis=AX.X)
            return ins
        P.step("dve", ln_stats, waits=[(sX, 16 * (b + 1))], incs=[(sV, 1)],
               label=f"lnst{b}")

        def ln_ab(v):
            ins = None
            for s in range(4):
                v.tensor_scalar_mul(m_s[s][:], sum_s[s][:], 1.0 / 1024)
                ins = v.tensor_scalar_mul(vr_s[s][:], ssq_s[s][:], 1.0 / 1024)
            return ins
        P.step("dve", ln_ab, waits=[(sV, f"lnst{b}")], incs=[(sV, 1)], label=f"lnab{b}")

        def ln_c(v):
            ins = None
            for s in range(4):
                ins = v.tensor_mul(t_s[s][:], m_s[s][:], m_s[s][:])
            return ins
        P.step("dve", ln_c, waits=[(sV, f"lnab{b}")], incs=[(sV, 1)], label=f"lnc{b}")

        def ln_d(v):
            ins = None
            for s in range(4):
                ins = v.tensor_sub(var_s[s][:], vr_s[s][:], t_s[s][:])
            return ins
        P.step("dve", ln_d, waits=[(sV, f"lnc{b}")], incs=[(sV, 1)], label=f"lnd{b}")

        def ln_ln(sc):
            ins = None
            for s in range(4):
                ins = sc.activation(rstd_s[s][:], var_s[s][:], AF.Ln, bias=epst[:, 0:1])
            return ins
        P.step("act", ln_ln, waits=[(sV, f"lnd{b}"), (sV, "eps_set")],
               incs=[(sA, 1)], label=f"lnln{b}")

        def ln_exp(sc):
            ins = None
            for s in range(4):
                ins = sc.activation(rstd_s[s][:], rstd_s[s][:], AF.Exp, scale=-0.5)
            return ins
        P.step("act", ln_exp, waits=[(sA, f"lnln{b}")], incs=[(sA, 1)], label=f"rstd{b}")

        def ln_nb(v):
            ins = None
            for s in range(4):
                ins = v.scalar_tensor_tensor(nb_s[s][:], m_s[s][:], -1.0, rstd_s[s][:],
                                             ALU.mult, ALU.mult)
            return ins
        P.step("dve", ln_nb, waits=[(sA, f"rstd{b}")], incs=[(sV, 1)], label=f"lnnb{b}")

        def ln_aff(v, _bb=bb):
            ins = None
            for s in range(4):
                ins = v.tensor_scalar(xnst[:, s * 1024:(s + 1) * 1024],
                                      xblk[_bb][:, s * 1024:(s + 1) * 1024],
                                      rstd_s[s][:, 0:1], nb_s[s][:, 0:1], ALU.mult, ALU.add)
            return ins
        waff = [(sV, f"lnnb{b}")]
        if b >= 1:
            waff.append((sP, f"trxend{b-1}"))
        P.step("dve", ln_aff, waits=waff, incs=[(sV, 1)], label=f"aff{b}")

    def emit_front_trx(b):
        bb = b % 2
        for s in range(4):
            for c in range(8):
                i = s * 8 + c
                pst = pstA if i % 2 == 0 else pstB
                wtt = [(sV, f"aff{b}")]
                if b > 0 or i >= 2:
                    wtt.append((sA, f"evx{b}_{i-2}" if i >= 2 else f"evx{b-1}_{30+i}"))
                P.step("pe", (lambda t, _s=s, _c=c, _p=pst:
                              t.transpose(_p[:, 0:128],
                                          xnst[:, _s * 1024 + _c * 128:_s * 1024 + _c * 128 + 128],
                                          identb[:])),
                       waits=wtt, incs=[(sP, 1)], label=f"trx{b}_{i}")
                wev_ = [(sP, f"trx{b}_{i}")]
                if i == 0 and b >= 2:
                    wev_.append((sXB, f"xbst{b-2}"))
                P.step("act", (lambda sc, _s=s, _c=c, _p=pst, _x=xnt1s[b % 2]:
                               sc.copy(_x[:, _c * 512 + _s * 128:_c * 512 + _s * 128 + 128],
                                       _p[:, 0:128])),
                       waits=wev_, incs=[(sA, 1)], label=f"evx{b}_{i}")
        P.step("pe", lambda t: t.nop(), waits=[(sP, f"trx{b}_31")],
               incs=[(sP, 1)], label=f"trxend{b}")

        continue_marker = None
    def emit_mmblock(b, mid=None):
        xnt1 = xnt1s[b % 2]
        for wi in range(2):
            if wi == 1 and mid is not None:
                mid()
            wt_tile = w1 if wi == 0 else w2
            buf = kbuf if wi == 0 else qbuf
            pko = 0 if wi == 0 else 512
            for m in range(8):
                def mmk(t, _wt=wt_tile, _m=m, _o=pko):
                    ins = None
                    for c in range(8):
                        ins = t.matmul(psA[:, _o:_o + 512],
                                       _wt[:, c * 1024 + _m * 128:c * 1024 + _m * 128 + 128],
                                       xnt1[:, c * 512:c * 512 + 512],
                                       start=(c == 0), stop=(c == 7))
                    return ins
                wts = [(sA, f"evx{b}_31"), (sSt, "ld_wk" if wi == 0 else "ld_wq")]
                if wi == 1:
                    wts.append((sV, "wq_scaled"))
                if b == 0 and wi == 0 and m == 0:
                    wts.append((sV, "wtr"))
                    wts.append((sV, "wti"))
                if b > 0 or m > 0:
                    prev = f"evk{b}_{wi}_{m-1}" if m > 0 else f"evk{b-1}_{wi}_7"
                    wts.append((sA, prev))
                P.step("pe", mmk, waits=wts, incs=[(sP, 1)], label=f"mmk{b}_{wi}_{m}")
                jj = (b * 16 + wi * 8 + m)
                wev = [(sP, f"mmk{b}_{wi}_{m}")]
                if jj >= 6:
                    pj = jj - 6
                    wev.append((sSt, f"kqst{pj // 16}_{(pj % 16) // 8}_{pj % 8}"))
                P.step("act", (lambda sc, _j=jj, _o=pko:
                               sc.copy(kst4[_j % 6][:], psA[:, _o:_o + 512])),
                       waits=wev, incs=[(sA, 1)],
                       label=f"evk{b}_{wi}_{m}")
                if wi == 0:
                    P.step("gp", dma(buf[2 * m:2 * m + 2, :, b * 512:(b + 1) * 512]
                                     .rearrange("h r t -> (h r) t"), kst4[jj % 6][:]),
                           waits=[(sA, f"evk{b}_{wi}_{m}")], incs=[(sSt, 16)],
                           label=f"kqst{b}_{wi}_{m}", serialize=(m == 0))
                else:
                    P.step("gp", dma(qbuf[b, :, 2 * m, :], kst4[jj % 6][0:64, :]),
                           waits=[(sA, f"evk{b}_{wi}_{m}")], incs=[(sSt, 16)])
                    P.step("gp", dma(qbuf[b, :, 2 * m + 1, :], kst4[jj % 6][64:128, :]),
                           incs=[(sSt, 16)], label=f"kqst{b}_{wi}_{m}")

        def mmf(t):
            ins = None
            for c in range(8):
                ins = t.matmul(pex[0:16, 0:512], wfb[:, c * 16:c * 16 + 16],
                               xnt1[:, c * 512:c * 512 + 512], start=(c == 0), stop=(c == 7))
            return ins
        wtsf = [(sA, f"evx{b}_31"), (sSt, "ld_wf")]
        if b == 0:
            wtsf += [(sV, "wtr"), (sV, "wti")]
        else:
            wtsf.append((sA, f"evf{b-1}"))
        P.step("pe", mmf, waits=wtsf, incs=[(sP, 1)], label=f"mmf{b}")
        wevf = [(sP, f"mmf{b}")]
        if b > 0:
            wevf.append((sSt, f"fst{b-1}"))
        P.step("act", lambda sc: sc.copy(fst[:], pex[0:16, 0:512]),
               waits=wevf, incs=[(sA, 1)], label=f"evf{b}")
        P.step("gp", dma(fbuf.rearrange("(h s) t -> h s t", s=8)[:, b, :], fst[:]),
               waits=[(sA, f"evf{b}")], incs=[(sSt, 16)], label=f"fst{b}")
        P.step("gp", dma(xbuf[b], xnt1s[b % 2][:]), waits=[(sA, f"evx{b}_31")],
               incs=[(sXB, 16)], label=f"xbst{b}")

    emit_front(0)
    emit_front_trx(0)
    for b in range(1, NBLK):
        emit_mmblock(b - 1, mid=(lambda _b=b: emit_front(_b)))
        emit_front_trx(b)
    emit_mmblock(NBLK - 1)

    def kw_init(v, kb):
        v.memset(kwin[kb][64:128, :], 0.0)
        return v.memset(kwin[kb][64:67, :], 1.0)
    P.step("dve", lambda v: kw_init(v, 0),
           waits=[(sP, f"mmf{NBLK-2}"), (sXB, f"xbst{NBLK-2}")],
           incs=[(sV, 1)], label="kw0_ones")
    P.step("dve", lambda v: kw_init(v, 1),
           waits=[(sP, f"mmf{NBLK-1}"), (sXB, f"xbst{NBLK-1}")],
           incs=[(sV, 1)], label="kw1_ones")

    P._ph1 = len(P.steps)
    wv_r = wv_d.rearrange("(c p) n -> p c n", p=128)
    wo_r = wo_d.rearrange("(c p) n -> p c n", p=128)
    P.step("gp", dma(w1.rearrange("p (c n) -> p c n", c=8), wv_r),
           waits=[(sP, f"mmk{NBLK-1}_0_7")], incs=[(sSt, 16)], label="ld_wv", serialize=True)
    P.step("gp", dma(w2.rearrange("p (c n) -> p c n", c=8), wo_r),
           waits=[(sP, f"mmk{NBLK-1}_1_7")], incs=[(sSt, 16)], label="ld_wo", serialize=True)

    for b in range(NBLK):
        wts = [(sXB, f"xbst{b}")]
        if b > 0:
            wts.append((sP, f"mmv{b-1}_3_1"))
        P.step("sync", dma(xnt2[:], xbuf[b]), waits=wts, incs=[(sX, 16)], label=f"xrld{b}")
        for s in range(4):
            for n in range(2):
                def mmv(t, _s=s, _n=n):
                    ins = None
                    for c in range(8):
                        ins = t.matmul(psG[:, 1024 + _n * 512:1024 + _n * 512 + 512],
                                       xnt2[:, c * 512 + _s * 128:c * 512 + _s * 128 + 128],
                                       w1[:, c * 1024 + _n * 512:c * 1024 + _n * 512 + 512],
                                       start=(c == 0), stop=(c == 7))
                    return ins
                wts = [(sX, 16 * (NBLK + b + 1)), (sSt, "ld_wv")]
                if b == 0 and s == 0 and n == 0:
                    wts.append((sA, f"evx{NBLK-1}_31"))
                if b > 0 or s > 0 or n > 0:
                    pv = (f"evv{b}_{s}_{n-1}" if n > 0 else
                          (f"evv{b}_{s-1}_1" if s > 0 else f"evv{b-1}_3_1"))
                    wts.append((sA, pv))
                P.step("pe", mmv, waits=wts, incs=[(sP, 1)], label=f"mmv{b}_{s}_{n}")
                kap = 4 * b + s
                P.step("act", (lambda sc, _k=kap, _n=n:
                               sc.copy(vst.rearrange("p k (h c) -> p k h c", c=65)
                                       [:, _k, 8 * _n:8 * _n + 8, 0:64],
                                       (psG[:, 1024 + _n * 512:1024 + _n * 512 + 512]
                                        ).rearrange("p (h c) -> p h c", c=64))),
                       waits=[(sP, f"mmv{b}_{s}_{n}"), (sV, "vst_ones")],
                       incs=[(sA, 1)], label=f"evv{b}_{s}_{n}")

    P._ph2 = len(P.steps)
    P.step("sync", dma(f128[:], fbuf), waits=[(sSt, f"fst{NBLK-1}")],
           incs=[(sX, 16)], label="fld")
    P.step("act", lambda sc: sc.activation(casc[:], f128[:], AF.Abs),
           waits=[(sX, 16 * (2 * NBLK + 1))], incs=[(sA, 1)], label="c_abs")
    P.step("act", lambda sc: sc.activation(casc[:], casc[:], AF.Exp, scale=-1.0),
           waits=[(sA, "c_abs")], incs=[(sA, 1)], label="c_exp")
    P.step("act", lambda sc: sc.activation(casc[:], casc[:], AF.Ln, bias=onecol[:, 0:1]),
           waits=[(sA, "c_exp")], incs=[(sA, 1)], label="c_ln")
    P.step("dve", lambda v: v.tensor_scalar_min(ctmp[:], f128[:], 0.0),
           waits=[(sA, "c_ln")], incs=[(sV, 1)], label="c_min")
    P.step("dve", lambda v: v.tensor_sub(cst[0][:], ctmp[:], casc[:]),
           waits=[(sV, "c_min")], incs=[(sV, 1)], label="c_lf")
    src_i = 0
    for st_i, sh in enumerate([1, 2, 4, 8, 16, 32, 64, 128, 256]):
        dst = 1 - src_i
        def csum(v, _s=src_i, _d=dst, _sh=sh):
            v.tensor_copy(cst[_d][:, 0:_sh], cst[_s][:, 0:_sh])
            return v.tensor_add(cst[_d][:, _sh:512], cst[_s][:, _sh:512],
                                cst[_s][:, 0:512 - _sh])
        P.step("dve", csum, waits=[(sV, "c_lf" if st_i == 0 else f"cs{st_i-1}")],
               incs=[(sV, 1)], label=f"cs{st_i}")
        src_i = dst
    cfin = cst[src_i]
    P.step("dve", lambda v: v.tensor_copy(totals[:], cfin[:, 511:512]),
           waits=[(sV, "cs8")], incs=[(sV, 1)], label="c_tot")
    P.step("pe", lambda t: t.matmul(pex[:, 0:1], ltri[:], totals[:], start=True, stop=True),
           waits=[(sV, "c_tot"), (sA, f"evf{NBLK-1}")], incs=[(sP, 1)], label="c_off")
    P.step("dve", lambda v: v.tensor_copy(offs[:], pex[:, 0:1]),
           waits=[(sP, "c_off")], incs=[(sV, 1)], label="c_offc")
    P.step("dve", lambda v: v.tensor_scalar_add(ctmp[:], cfin[:], offs[:, 0:1]),
           waits=[(sV, "c_offc")], incs=[(sV, 1)], label="c_full")
    def csplit1(v):
        v.tensor_copy(csp[:, 0:512], ctmp[:])
        return v.tensor_copy(casc[:], csp[:, 0:512])
    P.step("dve", csplit1, waits=[(sV, "c_full")], incs=[(sV, 1)], label="csp1")
    def csplit2(v):
        v.tensor_sub(f128[:], ctmp[:], casc[:])
        return v.tensor_copy(csp[:, 512:1024], f128[:])
    P.step("dve", csplit2, waits=[(sV, "csp1")], incs=[(sV, 1)], label="csp2")
    def csplit3(v):
        v.tensor_copy(casc[:], csp[:, 512:1024])
        return v.tensor_sub(f128[:], f128[:], casc[:])
    P.step("dve", csplit3, waits=[(sV, "csp2")], incs=[(sV, 1)], label="csp3")
    P.step("dve", lambda v: v.tensor_copy(csp[:, 1024:1536], f128[:]),
           waits=[(sV, "csp3")], incs=[(sV, 1)], label="csp4")
    P.step("dve", lambda v: v.tensor_scalar_mul(csn[:], csp[:], -1.0),
           waits=[(sV, "csp4")], incs=[(sV, 1)], label="csp5")
    for hh in range(H):
        for sp in range(3):
            P.step("gp", dma(cqbuf[hh, sp, :].rearrange("(s t) -> s t", s=8),
                             csp[8 * hh:8 * hh + 8, sp * 512:sp * 512 + 512]),
                   waits=[(sV, "csp5")] if (hh == 0 and sp == 0) else [],
                   incs=[(sSt, 16)], label=f"cq{hh}_{sp}")
            P.step("gp", dma(ckbuf[hh, sp, :].rearrange("(s t) -> s t", s=8),
                             csn[8 * hh:8 * hh + 8, sp * 512:sp * 512 + 512]),
                   incs=[(sSt, 16)], label=f"ck{hh}_{sp}")

    P._ph3 = len(P.steps)
    xr2 = x_d.rearrange("(t s p) d -> t p s d", p=128, s=4)
    yr = y_d.rearrange("(t s p) d -> t s p d", p=128, s=4)

    gg_ctr = [0]
    nrm_ctr = [0]
    msk_ctr = [0]
    sg_last = {}
    pend_nrm = [None]
    pend_av = []
    pend_wo = []
    def flush_av(keep=0):
        while len(pend_av) > keep:
            _fn, _w, _lab = pend_av.pop(0)
            P.step("pe", _fn, waits=_w, incs=[(sP, 1)], label=_lab)

    set_off = [0]
    for t in range(NT):
        w = 512 * (t + 1)
        nch = 4 * (t + 1)
        tile_gg0 = [gg_ctr[0]]
        if t > 0:
            set_off[0] = (2 - gg_ctr[0]) % 3
        def emit_qld(_t):
            wq_ = ([(sSt, f"ck{H-1}_2")] if _t == 0
                   else [(sP, sg_last[(_t - 1, H - 1)])])
            P.step("gp", dma(qwin[0:64, :], qbuf[_t].rearrange("r h t -> r (h t)")),
                   waits=wq_, incs=[(sQ, 16)], label=f"qld{_t}")
            P.step("gp", dma(qwin[64:67, :].rearrange("i (h t) -> i h t", h=H),
                             cqbuf[:, :, 512 * _t:512 * _t + 512]
                             .rearrange("h i t -> i h t")),
                   incs=[(sQ, 16)], label=f"qldc{_t}")
        if t == 0:
            emit_qld(0)
        wx = [(sSt, f"ck{H-1}_2")] if t == 0 else [(sV, f"comb{t-1}_3_1")]
        P.step("gp", dma(xres.rearrange("p (s d) -> p s d", s=4), xr2[t]),
               waits=wx, incs=[(sXR, 16)], label=f"xres{t}")
        wxn = [(sP, f"mmv{NBLK-1}_3_1")] if t == 0 else [(sP, f"ff{t-1}_3B")]
        P.step("gp", dma(xnt2[:], xbuf[t]), waits=wxn, incs=[(sXN, 16)], label=f"xnld{t}")

        for hh in range(H):
            kb = hh % 2
            wk_ = [(sV, "kw0_ones" if kb == 0 else "kw1_ones"), (sSt, f"ck{H-1}_2")]
            if t * H + hh >= 2:
                if hh >= 2:
                    wk_.append((sP, sg_last[(t, hh - 2)]))
                else:
                    wk_.append((sP, sg_last[(t - 1, H - 2 + hh)]))
            sKp = sK0 if kb == 0 else sK1
            P.step("sync", dma(kwin[kb][0:64, 0:w], kbuf[hh, :, 0:w]),
                   waits=wk_, incs=[(sKp, 16)])
            P.step("sync", dma(kwin[kb][67:70, 0:w], ckbuf[hh, :, 0:w]),
                   incs=[(sKp, 16)], label=f"kld{t}_{hh}")

            ngr = (nch + G - 1) // G
            for g in range(ngr):
                k0, k1 = G * g, min(G * g + G, nch)
                gg = gg_ctr[0]
                gset = (gg + set_off[0]) % 3
                pgrp = psG[:, 1024 * gset:1024 * gset + 1024]
                ptl = ptile[gset]

                def mms(ten, _k0=k0, _k1=k1, _p=pgrp, _hh=hh):
                    ins = None
                    for kk in range(_k0, _k1):
                        ins = ten.matmul(_p[:, 512 * (kk - _k0):512 * (kk - _k0) + 512],
                                         kwin[_hh % 2][:, 128 * kk:128 * kk + 128],
                                         qwin[:, 512 * _hh:512 * _hh + 512],
                                         start=True, stop=True)
                    return ins
                wts = []
                if g == 0:
                    wts += [(sK0 if hh % 2 == 0 else sK1, f"kld{t}_{hh}"),
                            (sQ, f"qldc{t}")]
                if gg >= 3:
                    wts.append((sA, f"exp{gg-3}"))
                if gg == 0:
                    wts.append((sA, f"evk{NBLK-1}_1_7"))
                    wts.append((sA, f"evf{NBLK-1}"))
                if gg == 1:
                    wts.append((sA, f"evv{NBLK-1}_3_1"))
                if gg == 2:
                    wts.append((sA, f"evx{NBLK-1}_31"))
                if t > 0 and (gg - tile_gg0[0]) < 3 and True:
                    wts.append((sV, f"comb{t-1}_3_1"))
                P.step("pe", mms, waits=wts, incs=[(sP, 1)], label=f"sg{gg}")
                flush_av(keep=2)
                if g == 0:
                    flush_av(keep=0)
                    if pend_nrm[0] is not None:
                        _enrm, _avlab = pend_nrm[0]
                        pend_nrm[0] = None
                        _enrm(_avlab=_avlab)
                if g == ngr - 1:
                    sg_last[(t, hh)] = f"sg{gg}"

                def expg(sc, _p=pgrp, _pt=ptl, _n=512 * (k1 - k0)):
                    return sc.activation(_pt[:, 0:_n], _p[:, 0:_n], AF.Exp)
                we = [(sP, f"sg{gg}")]
                if gg >= 3:
                    we.append((sP, f"avg{gg-3}"))
                P.step("act", expg, waits=we, incs=[(sA, 1)], label=f"exp{gg}")

                mlabs = []
                for kk in range(k0, k1):
                    m = kk - 4 * t
                    if m >= 0:
                        msk_ctr[0] += 1
                        def msk(v, _pt=ptl, _off=512 * (kk - k0), _m=m):
                            return v.tensor_tensor(
                                _pt[:, _off:_off + 512], _pt[:, _off:_off + 512],
                                master[:, 384 - 128 * _m:896 - 128 * _m], ALU.min)
                        P.step("dve", msk, waits=[(sA, f"exp{gg}")],
                               incs=[(sV, 1)], label=f"msk{msk_ctr[0]}")
                        mlabs.append(f"msk{msk_ctr[0]}")

                def av(ten, _k0=k0, _k1=k1, _pt=ptl, _hh=hh, _nch=nch):
                    pa = pacc if _hh % 2 == 0 else pex[:, 0:264]
                    ins = None
                    for kk in range(_k0, _k1):
                        for s4 in range(4):
                            ins = ten.matmul(
                                pa[:, 66 * s4:66 * s4 + 65],
                                _pt[:, 512 * (kk - _k0) + 128 * s4:
                                    512 * (kk - _k0) + 128 * s4 + 128],
                                vst[:, kk, 65 * _hh:65 * _hh + 65],
                                start=(kk == 0 and s4 == 0),
                                stop=(kk == _nch - 1 and s4 == 3),
                                skip_group_check=True)
                    return ins
                wav = [(sA, f"exp{gg}")] + [(sV, ml) for ml in mlabs]
                if g == 0:
                    if nrm_ctr[0] >= 1 + (1 if hh >= 2 else 0):
                        pass
                    if hh >= 2:
                        wav.append((sV, f"nrm{(t * H + hh) - 2 + 1}"))
                    elif t > 0:
                        if hh == 0:
                            wav.append((sV, f"nrm{(t - 1) * H + H - 2 + 1}"))
                        else:
                            wav.append((sV, f"comb{t-1}_3_1"))
                    elif t == 0 and hh == 1:
                        wav.append((sV, "c_offc"))
                pend_av.append((av, wav, f"avg{gg}"))
                gg_ctr[0] += 1

            def emit_nrm(_t=t, _hh=hh, _avlab=None):
                pa = pacc if _hh % 2 == 0 else pex[:, 0:264]
                P.step("dve", (lambda v, _pa=pa: v.tensor_copy(
                           lsb[:], _pa.rearrange("p (s c) -> p s c", c=66)[:, :, 64])),
                       waits=[(sP, _avlab)], incs=[(sV, 1)],
                       label=f"nrmA{_t}_{_hh}")
                def nrm_b(v):
                    ins = None
                    for s4 in range(4):
                        ins = v.reciprocal(recs[s4][:], lsb[:, s4:s4 + 1])
                    return ins
                P.step("dve", nrm_b, waits=[(sV, f"nrmA{_t}_{_hh}")], incs=[(sV, 1)],
                       label=f"nrmB{_t}_{_hh}")
                def nrm_c(v):
                    ins = None
                    for s4 in range(4):
                        ins = v.tensor_scalar_mul(
                            otok[:, 1024 * s4 + 64 * _hh:1024 * s4 + 64 * _hh + 64],
                            pa[:, 66 * s4:66 * s4 + 64], recs[s4][:, 0:1])
                    return ins
                wn = [(sV, f"nrmB{_t}_{_hh}")]
                if _t == 0 and _hh == 0:
                    wn.append((sP, f"trxend{NBLK-1}"))
                if _t > 0 and _hh == 0:
                    wn.append((sP, f"odone{_t-1}"))
                nrm_ctr[0] += 1
                P.step("dve", nrm_c, waits=wn, incs=[(sV, 1)], label=f"nrm{nrm_ctr[0]}")
            pend_nrm[0] = (emit_nrm, f"avg{gg_ctr[0]-1}")

        if _DBG_DUMP and t == 0:
            P.step("gp", dma(dbgo_d, otok[:]), waits=[(sV, f"nrm{nrm_ctr[0]}")],
                   incs=[(sSt, 16)])
        flush_av(keep=0)
        if pend_nrm[0] is not None:
            _enrm, _avlab = pend_nrm[0]
            pend_nrm[0] = None
            _enrm(_avlab=_avlab)
        nrm_all = f"nrm{nrm_ctr[0]}"
        if t + 1 < NT:
            emit_qld(t + 1)
        pst4 = [psG[:, 2048 + 256 * j:2048 + 256 * j + 256].bitcast(BF16)
                for j in range(4)]
        pstall = psG[:, 2048:3072].bitcast(BF16).rearrange("p (b k) -> p b k",
                                                           k=512)
        for s4 in range(4):
            for h in range(2):
                rid = s4 * 2 + h
                for j in range(4):
                    c = 4 * h + j
                    wtt = [(sV, nrm_all)]
                    if rid >= 1:
                        wtt.append((sV, f"oev{t}_{s4}_{h-1}" if h == 1
                                    else f"oev{t}_{s4-1}_1"))
                    P.step("pe", (lambda ten, _s=s4, _c=c, _p=pst4[j]:
                                  ten.transpose(_p[:, 0:128],
                                                otok[:, 1024 * _s + 128 * _c:
                                                     1024 * _s + 128 * _c + 128],
                                                identb[:])),
                           waits=wtt, incs=[(sP, 1)], label=f"otr{t}_{s4}_{h}_{j}")
                if pend_wo:
                    _f, _w, _l = pend_wo.pop(0)
                    P.step("pe", _f, waits=_w, incs=[(sP, 1)], label=_l)
                woev = [(sP, f"otr{t}_{s4}_{h}_3")]
                if rid >= 1:
                    woev.append((sP, f"wo{t}_{s4}_{h-1}" if h == 1
                                 else f"wo{t}_{s4-1}_1"))
                P.step("dve", (lambda v:
                               v.tensor_copy(otst.rearrange("p (s k) -> p s k",
                                                            k=128),
                                             pstall[:, :, 0:128])),
                       waits=woev, incs=[(sV, 1)], label=f"oev{t}_{s4}_{h}")
                def mwo(ten, _h=h):
                    ins = None
                    for j2 in range(4):
                        cc = 4 * _h + j2
                        for n in range(2):
                            ins = ten.matmul(psY[n],
                                             otst[:, 128 * j2:128 * j2 + 128],
                                             w2[:, 1024 * cc + 512 * n:
                                                1024 * cc + 512 * n + 512],
                                             start=(_h == 0 and j2 == 0),
                                             stop=(_h == 1 and j2 == 3),
                                             skip_group_check=True)
                    return ins
                wwo = [(sV, f"oev{t}_{s4}_{h}"), (sSt, "ld_wo")]
                if h == 0:
                    if s4 > 0:
                        wwo.append((sV, f"comb{t}_{s4-1}_1"))
                    elif t > 0:
                        wwo.append((sV, f"comb{t-1}_3_1"))
                pend_wo.append((mwo, wwo, f"wo{t}_{s4}_{h}"))
            while pend_wo:
                _f, _w, _l = pend_wo.pop(0)
                P.step("pe", _f, waits=_w, incs=[(sP, 1)], label=_l)
            if s4 == 3:
                P.step("pe", lambda ten: ten.nop(), waits=[(sP, f"otr{t}_3_1_3")],
                       incs=[(sP, 1)], label=f"odone{t}")

            def ffa(ten, _s=s4):
                ins = None
                for c in range(4):
                    ten.matmul(psF[0],
                               xnt2[:, c * 512 + _s * 128:c * 512 + _s * 128 + 128],
                               wtr[:, c * 512:c * 512 + 512],
                               start=(c == 0), stop=False, skip_group_check=True)
                ten.matmul(psF[0], ones2[:], bias2r[:], start=False, stop=True,
                           skip_group_check=True)
                for c in range(4):
                    ins = ten.matmul(
                        pex[:, 0:512],
                        xnt2[:, (c + 4) * 512 + _s * 128:(c + 4) * 512 + _s * 128 + 128],
                        wti[:, c * 512:c * 512 + 512],
                        start=(c == 0), stop=(c == 3), skip_group_check=True)
                return ins
            def ffb(ten, _s=s4):
                for c in range(8):
                    wt_t = wti if c < 4 else wtr
                    ten.matmul(psF[1],
                               xnt2[:, c * 512 + _s * 128:c * 512 + _s * 128 + 128],
                               wt_t[:, (c % 4) * 512:(c % 4) * 512 + 512],
                               start=(c == 0), stop=False, skip_group_check=True)
                return ten.matmul(psF[1], ones2[:], bias2i[:], start=False,
                                  stop=True, skip_group_check=True)
            wff = [(sV, "wtr"), (sV, "wti"), (sXN, f"xnld{t}"), (sSt, "bsp_i"),
                   (sV, nrm_all)]
            if s4 > 0:
                wff.append((sV, f"comb{t}_{s4-1}_1"))
            elif t > 0:
                wff.append((sV, f"comb{t-1}_3_1"))
            else:
                wff.append((sV, "c_offc"))
            P.step("pe", ffa, waits=wff, incs=[(sP, 1)], label=f"ff{t}_{s4}A")
            P.step("pe", ffb, waits=[], incs=[(sP, 1)], label=f"ff{t}_{s4}B")

            for n in range(2):
                def comb(v, _s=s4, _n=n):
                    yb = ybuf[_s % 2]
                    v.scalar_tensor_tensor(yb[:, 512 * _n:512 * _n + 512],
                                           psY[_n], 1.0,
                                           xres[:, 1024 * _s + 512 * _n:
                                                1024 * _s + 512 * _n + 512],
                                           ALU.mult, ALU.add)
                    if _n == 0:
                        v.scalar_tensor_tensor(yb[:, 0:512], pex[:, 0:512], -1.0,
                                               yb[:, 0:512], ALU.mult, ALU.add)
                    return v.scalar_tensor_tensor(yb[:, 512 * _n:512 * _n + 512],
                                                  psF[_n], 1.0,
                                                  yb[:, 512 * _n:512 * _n + 512],
                                                  ALU.mult, ALU.add)
                wc = [(sP, f"wo{t}_{s4}_1"), (sXR, f"xres{t}"),
                      (sP, f"ff{t}_{s4}A" if n == 0 else f"ff{t}_{s4}B")]
                idx = 4 * t + s4
                if idx >= 2:
                    wc.append((sY, 16 * (idx - 1)))
                P.step("dve", comb, waits=wc, incs=[(sV, 1)], label=f"comb{t}_{s4}_{n}")
            P.step("gp", dma(yr[t, s4], ybuf[s4 % 2][:]),
                   waits=[(sV, f"comb{t}_{s4}_1")], incs=[(sY, 16)],
                   label=f"yst{t}_{s4}")

    P.step("gp", lambda g_: g_.nop(), waits=[(sY, 16 * 4 * NT)])

    if _DBG_PHASE < 4:
        cut = [P._ph0, P._ph1, P._ph2, P._ph3][_DBG_PHASE]
        P.steps = P.steps[:cut]
    if _DBG_NSTEPS is not None:
        P.steps = P.steps[:_DBG_NSTEPS]
    P.resolve()

    with nc.Block() as block:
        @block.sync
        def _(sync):
            P.emit("sync", sync)

        @block.vector
        def _(vector):
            P.emit("dve", vector)

        @block.scalar
        def _(scalar):
            P.emit("act", scalar)

        @block.tensor
        def _(tensor):
            P.emit("pe", tensor)

        @block.gpsimd
        def _(gpsimd):
            P.emit("gp", gpsimd)

    es.close()
    return nc


def kernel(L0_attn_x, L0_attn_Wq, L0_attn_Wk, L0_attn_Wv, L0_attn_Wo, L0_attn_Wf,
           L0_ffn_G1_R, L0_ffn_G2_R, L0_ffn_G1_I, L0_ffn_G2_I,
           L0_ffn_bias_real, L0_ffn_bias_imag):
    x = np.ascontiguousarray(np.asarray(L0_attn_x, dtype=np.float32))
    com = {
        "wq": np.ascontiguousarray(np.asarray(L0_attn_Wq, dtype=np.float32)),
        "wk": np.ascontiguousarray(np.asarray(L0_attn_Wk, dtype=np.float32)),
        "wv": np.ascontiguousarray(np.asarray(L0_attn_Wv, dtype=np.float32)),
        "wo": np.ascontiguousarray(np.asarray(L0_attn_Wo, dtype=np.float32)),
        "wf": np.ascontiguousarray(np.asarray(L0_attn_Wf, dtype=np.float32)),
        "g1r": np.ascontiguousarray(np.asarray(L0_ffn_G1_R, dtype=np.float32)),
        "g2r": np.ascontiguousarray(np.asarray(L0_ffn_G2_R, dtype=np.float32)),
        "g1i": np.ascontiguousarray(np.asarray(L0_ffn_G1_I, dtype=np.float32)),
        "g2i": np.ascontiguousarray(np.asarray(L0_ffn_G2_I, dtype=np.float32)),
        "br": np.ascontiguousarray(np.asarray(L0_ffn_bias_real, dtype=np.float32)),
        "bi": np.ascontiguousarray(np.asarray(L0_ffn_bias_imag, dtype=np.float32)),
    }
    nc = build_nc()
    in_maps = [dict(com, x=np.ascontiguousarray(x[b])) for b in range(8)]
    res = run_bass_kernel_spmd(nc, in_maps, core_ids=list(range(8)))
    return np.stack([res.results[b]["y"] for b in range(8)], axis=0)
